# version 04
# baseline (speedup 1.0000x reference)
import math
import numpy as np
import concourse.bass as bass
import concourse.mybir as mybir
from concourse.bass_utils import run_bass_kernel_spmd

F32 = mybir.dt.float32
F32R = mybir.dt.float32r
I32 = mybir.dt.int32
AF = mybir.ActivationFunctionType
OP = mybir.AluOpType
AX = mybir.AxisListType

D = 1024
T = 8192
WT = 512
NW = T // WT
KC = D // 128
LN_EPS = 1e-5


class Sem:
    __slots__ = ("h", "v")
    def __init__(self, h):
        self.h = h
        self.v = 0


class Ring:
    def __init__(self, P, name, n, step=16):
        self.ss = [P.sem(f"{name}{i}") for i in range(n)]
        self.n = n
        self.step = step
        self.base = [s.v for s in self.ss]
        self.last = -1

    def inc(self, inst, i, amount=16):
        s = self.ss[i % self.n]
        inst.then_inc(s.h, amount)
        s.v += amount
        self.last = max(self.last, i)

    def wait(self, eng, i):
        j = i % self.n
        eng.wait_ge(self.ss[j].h, self.base[j] + self.step * (i // self.n + 1))

    def wait_all(self, eng):
        for j in range(self.n):
            if self.last < j:
                continue
            i = self.last - ((self.last - j) % self.n)
            eng.wait_ge(self.ss[j].h, self.base[j] + self.step * (i // self.n + 1))


class Prog:
    def __init__(self, name="prog"):
        self.nc = bass.Bass("TRN2", target_bir_lowering=False, name=name)
        self.nc._allow_low_precision_reason = "f32r matmul operands by design"
        self._sems = {}
        self._consts = {}
        self._guards = []
        self._tname = "p"
        self._tn = 0

    def sem(self, name):
        if name not in self._sems:
            self._sems[name] = Sem(self.nc.alloc_semaphore(name))
        return self._sems[name]

    def pass_begin(self, name):
        self.pass_end()
        self._tname = name
        self._tn = 0

    def pass_end(self):
        for g in reversed(self._guards):
            g.__exit__(None, None, None)
        self._guards = []

    def alloc(self, cols, dt=F32, parts=128):
        g = self.nc.sbuf_tensor(f"{self._tname}_s{self._tn}", [parts, cols], dt)
        self._tn += 1
        h = g.__enter__()
        self._guards.append(g)
        return h.ap()

    def alloc3(self, c0, c1, dt=F32):
        return self.alloc(c0 * c1, dt).rearrange("p (a b) -> p a b", a=c0)

    def const(self, key, data):
        if key not in self._consts:
            self._consts[key] = self.nc.inline_tensor(
                np.ascontiguousarray(data), name=f"c_{key}").ap()
        return self._consts[key]


def fm_tiles(dram_ap, d_feat):
    return dram_ap.rearrange("(c p) t -> p c t", p=128)


def w_tiles(dram_ap, d_in, d_out):
    return dram_ap.rearrange("(c p) o -> p c o", p=128)


def vec2(v):
    v = np.asarray(v)
    return np.ascontiguousarray(v.reshape(-1, 128).T)


def _t(a):
    return np.ascontiguousarray(np.asarray(a).T)


def wtile(w):
    w = np.asarray(w)
    kci = w.shape[0] // 128
    ocn = w.shape[1] // 128
    return np.ascontiguousarray(
        w.reshape(kci, 128, ocn, 128).transpose(2, 1, 0, 3
                                                ).reshape(ocn, 128, kci * 128))



def emit_ln(P, src_view, xn_dram, t_total=T, gamma=None, beta=None, name="ln"):
    nc = P.nc
    nw = t_total // WT
    xn_v = fm_tiles(xn_dram, D)
    s_pre = P.sem("pre"); s_mm = P.sem("mm"); s_ev = P.sem("ev")
    s_ac = P.sem("ac"); s_sf = P.sem("sf")
    b_pre, b_mm, b_ev, b_ac = s_pre.v, s_mm.v, s_ev.v, s_ac.v
    r_ld = Ring(P, "ldr", 2)
    r_st = Ring(P, "str", 2)

    P.pass_begin(name)
    xw = [P.alloc3(KC, WT) for _ in range(2)]
    sq = P.alloc3(KC, WT)
    xn = [P.alloc3(KC, WT) for _ in range(2)]
    tmp = P.alloc3(KC, WT)
    strip_r = P.alloc(WT * 2)
    strip_f = P.alloc(WT * 3)
    mu_sb = strip_r[0:1, 0:WT]
    rs_sb = strip_r[0:1, WT:2 * WT]
    musq_sb = strip_f[0:1, 0:WT]
    var_sb = strip_f[0:1, WT:2 * WT]
    sd_sb = strip_f[0:1, 2 * WT:3 * WT]
    gb = P.alloc(2 * KC) if gamma is not None else None
    ones_col = P.alloc(1, F32R)
    ones_row_t = P.alloc(128, F32R)
    ones_row = ones_row_t[0:1, :]
    eps_t = P.alloc(1)

    c_ones_col = P.const("ones128x1", np.ones((128, 1), np.float32))
    c_ones_row = P.const("ones1x128", np.ones((1, 128), np.float32))
    c_eps = P.const("eps_ln", np.full((128, 1), LN_EPS, np.float32))
    nld = 3 + (2 if gamma is not None else 0)

    with (
        nc.psum_tensor(f"{name}_pmu", [1, WT], F32) as pmu,
        nc.psum_tensor(f"{name}_pvar", [1, WT], F32) as pvar,
        nc.psum_tensor(f"{name}_pmub", [128, WT], F32) as pmub,
        nc.psum_tensor(f"{name}_prsb", [128, WT], F32) as prsb,
        nc.Block() as blk,
    ):
        @blk.sync
        def _(sy):
            sy.dma_start(ones_col, c_ones_col.bitcast(F32R)).then_inc(s_pre.h, 16)
            sy.dma_start(ones_row, c_ones_row.bitcast(F32R)).then_inc(s_pre.h, 16)
            sy.dma_start(eps_t, c_eps).then_inc(s_pre.h, 16)
            if gamma is not None:
                sy.dma_start(gb[:, 0:KC], gamma).then_inc(s_pre.h, 16)
                sy.dma_start(gb[:, KC:2 * KC], beta).then_inc(s_pre.h, 16)
            for w in range(nw):
                if w >= 2:
                    sy.wait_ge(s_ev.h, b_ev + (w - 2) * 4 + 4)
                d = sy.dma_start(xw[w % 2][:, :, :].bitcast(F32R),
                                 src_view[:, :, w * WT:(w + 1) * WT].bitcast(F32R))
                r_ld.inc(d, w)
        s_pre.v += 16 * nld

        @blk.scalar
        def _(sc):
            sc.wait_ge(s_pre.h, b_pre + 16 * nld)
            for w in range(nw):
                r_ld.wait(sc, w)
                if w >= 1:
                    sc.wait_ge(s_mm.h, b_mm + (w - 1) * 3 + 2)
                sc.activation(sq.rearrange("p a b -> p (a b)").bitcast(F32R),
                              xw[w % 2].rearrange("p a b -> p (a b)"),
                              AF.Square).then_inc(s_ac.h, 1)
                sc.wait_ge(s_ev.h, b_ev + w * 4 + 2)
                sc.activation(sd_sb, var_sb, AF.Sqrt,
                              bias=eps_t[0:1, :]).then_inc(s_ac.h, 1)
        s_ac.v += 2 * nw

        @blk.tensor
        def _(te):
            te.wait_ge(s_pre.h, b_pre + 16 * nld)
            for w in range(nw):
                r_ld.wait(te, w)
                if w >= 1:
                    te.wait_ge(s_ev.h, b_ev + (w - 1) * 4 + 1)
                for c in range(KC):
                    mm = te.matmul(pmu[:], ones_col,
                                   xw[w % 2][:, c, :].bitcast(F32R),
                                   start=(c == 0), stop=(c == KC - 1))
                mm.then_inc(s_mm.h, 1)
                te.wait_ge(s_ac.h, b_ac + 2 * w + 1)
                if w >= 1:
                    te.wait_ge(s_ev.h, b_ev + (w - 1) * 4 + 2)
                for c in range(KC):
                    mm = te.matmul(pvar[:], ones_col, sq[:, c, :].bitcast(F32R),
                                   start=(c == 0), stop=(c == KC - 1))
                mm.then_inc(s_mm.h, 1)
                if w >= 1:
                    te.wait_ge(s_ev.h, b_ev + (w - 1) * 4 + 4)
                te.wait_ge(s_ev.h, b_ev + w * 4 + 1)
                te.matmul(pmub[:], ones_row, mu_sb.bitcast(F32R),
                          start=True, stop=True)
                te.wait_ge(s_ev.h, b_ev + w * 4 + 3)
                mm = te.matmul(prsb[:], ones_row, rs_sb.bitcast(F32R),
                               start=True, stop=True)
                mm.then_inc(s_mm.h, 1)
        s_mm.v += 3 * nw

        @blk.vector
        def _(ve):
            for w in range(nw):
                ve.wait_ge(s_mm.h, b_mm + w * 3 + 1)
                ve.tensor_scalar_mul(mu_sb.bitcast(F32R), pmu[:],
                                     1.0 / D).then_inc(s_ev.h, 1)
                ve.wait_ge(s_ev.h, b_ev + w * 4 + 1)
                ve.tensor_mul(musq_sb, mu_sb, mu_sb).then_inc(s_sf.h, 1)
                s_sf.v += 1
                ve.wait_ge(s_mm.h, b_mm + w * 3 + 2)
                ve.wait_ge(s_sf.h, s_sf.v)
                ve.scalar_tensor_tensor(var_sb, pvar[:], 1.0 / D, musq_sb,
                                        OP.mult, OP.subtract).then_inc(s_ev.h, 1)
                ve.wait_ge(s_ac.h, b_ac + 2 * w + 2)
                ve.reciprocal(rs_sb.bitcast(F32R), sd_sb).then_inc(s_ev.h, 1)
                ve.wait_ge(s_mm.h, b_mm + w * 3 + 3)
                if w >= 2:
                    r_st.wait(ve, w - 2)
                for c in range(KC):
                    ve.tensor_sub(tmp[:, c, :], xw[w % 2][:, c, :],
                                  pmub[:]).then_inc(s_sf.h, 1)
                    s_sf.v += 1
                    ve.wait_ge(s_sf.h, s_sf.v)
                    if gamma is None:
                        mmv = ve.tensor_mul(xn[w % 2][:, c, :].bitcast(F32R),
                                            tmp[:, c, :], prsb[:])
                    else:
                        ve.tensor_mul(tmp[:, c, :], tmp[:, c, :],
                                      prsb[:]).then_inc(s_sf.h, 1)
                        s_sf.v += 1
                        ve.wait_ge(s_sf.h, s_sf.v)
                        mmv = ve.tensor_scalar(xn[w % 2][:, c, :].bitcast(F32R),
                                               tmp[:, c, :], gb[:, c:c + 1],
                                               gb[:, KC + c:KC + c + 1],
                                               OP.mult, OP.add)
                    if c == KC - 1:
                        mmv.then_inc(s_ev.h, 1)
        s_ev.v += 4 * nw

        @blk.gpsimd
        def _(gp):
            for w in range(nw):
                gp.wait_ge(s_ev.h, b_ev + w * 4 + 4)
                d = gp.dma_start(xn_v[:, :, w * WT:(w + 1) * WT],
                                 xn[w % 2][:, :, :])
                r_st.inc(d, w)
            r_st.wait_all(gp)



def emit_proj(P, src_dram, w_dram, d_in, d_out, name,
              out_dram=None, t_total=T,
              epi="f32r", bias=None, scale=None, act=None, scale_const=None,
              old_dram=None, w2_dram=None, scale2=None):
    nc = P.nc
    nw = t_total // WT
    kci = d_in // 128
    oc_n = d_out // 128
    src_v = fm_tiles(src_dram, d_in)
    out_v = fm_tiles(out_dram, d_out)
    old_v = fm_tiles(old_dram, d_out) if old_dram is not None else None

    s_wl = P.sem("wl"); s_mm = P.sem("mm"); s_ev = P.sem("ev"); s_ac = P.sem("ac")
    b_wl, b_mm, b_ev, b_ac = s_wl.v, s_mm.v, s_ev.v, s_ac.v
    r_ld = Ring(P, "ldr", 2, step=16 * (2 if old_dram is not None else 1))
    NOB = 4
    r_st = Ring(P, "str", NOB)

    P.pass_begin(name)
    wsl = P.alloc3(kci, d_out, F32R)
    wsl2 = P.alloc3(kci, d_out, F32R) if w2_dram is not None else None
    xw = [P.alloc3(kci, WT, F32R) for _ in range(2)]
    oldw = [P.alloc3(oc_n, WT) for _ in range(2)] if old_dram is not None else None
    ob = [P.alloc(WT) for _ in range(NOB)]
    sg = [P.alloc(WT) for _ in range(2)] if epi == "gateup" else None
    vecs = P.alloc(3 * oc_n)
    bias_t = vecs[:, 0:oc_n]
    scale_t = vecs[:, oc_n:2 * oc_n]
    scale2_t = vecs[:, 2 * oc_n:3 * oc_n]
    NPB = 4

    nwl = (1 + (1 if w2_dram is not None else 0) + (1 if bias is not None else 0)
           + (1 if scale is not None else 0) + (1 if scale2 is not None else 0))
    mm_per_g = 2 if w2_dram is not None else 1

    with (
        nc.psum_tensor(f"{name}_pb", [128, NPB, WT], F32) as pb,
        nc.Block() as blk,
    ):
        @blk.scalar
        def _(sc):
            sc.dma_start(wsl[:, :, :],
                         w_tiles(w_dram, d_in, d_out).bitcast(F32R)
                         ).then_inc(s_wl.h, 16)
            if w2_dram is not None:
                sc.dma_start(wsl2[:, :, :],
                             w_tiles(w2_dram, d_in, d_out).bitcast(F32R)
                             ).then_inc(s_wl.h, 16)
            if bias is not None:
                sc.dma_start(bias_t, bias).then_inc(s_wl.h, 16)
            if scale is not None:
                sc.dma_start(scale_t, scale).then_inc(s_wl.h, 16)
            if scale2 is not None:
                sc.dma_start(scale2_t, scale2).then_inc(s_wl.h, 16)
        s_wl.v += 16 * nwl

        @blk.sync
        def _(sy):
            for w in range(nw):
                if w >= 2:
                    sy.wait_ge(s_mm.h, b_mm + (w - 1) * oc_n * mm_per_g)
                    if old_dram is not None:
                        sy.wait_ge(s_ev.h, b_ev + (w - 1) * oc_n)
                d = sy.dma_start(xw[w % 2][:, :, :],
                                 src_v[:, :, w * WT:(w + 1) * WT].bitcast(F32R))
                r_ld.inc(d, w)
                if old_dram is not None:
                    d = sy.dma_start(oldw[w % 2][:, :, :],
                                     old_v[:, :, w * WT:(w + 1) * WT])
                    r_ld.inc(d, w)

        @blk.tensor
        def _(te):
            te.wait_ge(s_wl.h, b_wl + 16 * nwl)
            g = 0
            for w in range(nw):
                r_ld.wait(te, w)
                for oc in range(oc_n):
                    if w2_dram is None:
                        bank = g % NPB
                        if g >= NPB:
                            te.wait_ge(s_ev.h, b_ev + g - NPB + 1)
                        for kc in range(kci):
                            mm = te.matmul(pb[:, bank, :],
                                           wsl[:, kc, oc * 128:(oc + 1) * 128],
                                           xw[w % 2][:, kc, :],
                                           start=(kc == 0), stop=(kc == kci - 1))
                        mm.then_inc(s_mm.h, 1)
                    else:
                        bank = (g % 2) * 2
                        if g >= 2:
                            te.wait_ge(s_ev.h, b_ev + g - 1)
                        for kc in range(kci):
                            mm = te.matmul(pb[:, bank, :],
                                           wsl[:, kc, oc * 128:(oc + 1) * 128],
                                           xw[w % 2][:, kc, :],
                                           start=(kc == 0), stop=(kc == kci - 1))
                        mm.then_inc(s_mm.h, 1)
                        for kc in range(kci):
                            mm = te.matmul(pb[:, bank + 1, :],
                                           wsl2[:, kc, oc * 128:(oc + 1) * 128],
                                           xw[w % 2][:, kc, :],
                                           start=(kc == 0), stop=(kc == kci - 1))
                        mm.then_inc(s_mm.h, 1)
                    g += 1
        s_mm.v += nw * oc_n * mm_per_g

        if epi == "gateup":
            @blk.scalar
            def _(sc):
                g = 0
                for w in range(nw):
                    for oc in range(oc_n):
                        bank = (g % 2) * 2
                        sc.wait_ge(s_mm.h, b_mm + g * 2 + 1)
                        if g >= 2:
                            sc.wait_ge(s_ev.h, b_ev + g - 1)
                        kw = {}
                        if scale is not None:
                            kw["scale"] = scale_t[:, oc:oc + 1]
                        sc.activation(sg[g % 2], pb[:, bank, :], AF.Silu,
                                      **kw).then_inc(s_ac.h, 1)
                        g += 1
            s_ac.v += nw * oc_n

            @blk.vector
            def _(ve):
                g = 0
                for w in range(nw):
                    for oc in range(oc_n):
                        bank = (g % 2) * 2
                        ve.wait_ge(s_mm.h, b_mm + g * 2 + 2)
                        ve.wait_ge(s_ac.h, b_ac + g + 1)
                        if g >= NOB:
                            r_st.wait(ve, g - NOB)
                        if scale2 is not None:
                            ve.scalar_tensor_tensor(
                                ob[g % NOB], pb[:, bank + 1, :],
                                scale2_t[:, oc:oc + 1], sg[g % 2],
                                OP.mult, OP.mult).then_inc(s_ev.h, 1)
                        else:
                            ve.tensor_mul(ob[g % NOB], pb[:, bank + 1, :],
                                          sg[g % 2]).then_inc(s_ev.h, 1)
                        g += 1
            s_ev.v += nw * oc_n
        elif epi == "residual":
            @blk.vector
            def _(ve):
                ve.wait_ge(s_wl.h, b_wl + 16 * nwl)
                g = 0
                for w in range(nw):
                    for oc in range(oc_n):
                        bank = g % NPB
                        ve.wait_ge(s_mm.h, b_mm + g + 1)
                        r_ld.wait(ve, w)
                        if g >= NOB:
                            r_st.wait(ve, g - NOB)
                        if scale is not None:
                            ve.scalar_tensor_tensor(
                                ob[g % NOB], pb[:, bank, :],
                                scale_t[:, oc:oc + 1], oldw[w % 2][:, oc, :],
                                OP.mult, OP.add).then_inc(s_ev.h, 1)
                        elif bias is not None:
                            ve.scalar_tensor_tensor(
                                ob[g % NOB], pb[:, bank, :],
                                bias_t[:, oc:oc + 1], oldw[w % 2][:, oc, :],
                                OP.add, OP.add).then_inc(s_ev.h, 1)
                        else:
                            ve.tensor_add(ob[g % NOB], pb[:, bank, :],
                                          oldw[w % 2][:, oc, :]).then_inc(s_ev.h, 1)
                        g += 1
            s_ev.v += nw * oc_n
        elif epi == "f32r" and act is None and (bias is not None or scale is not None):
            @blk.vector
            def _(ve):
                ve.wait_ge(s_wl.h, b_wl + 16 * nwl)
                g = 0
                for w in range(nw):
                    for oc in range(oc_n):
                        bank = g % NPB
                        ve.wait_ge(s_mm.h, b_mm + g + 1)
                        if g >= NOB:
                            r_st.wait(ve, g - NOB)
                        s1 = scale_t[:, oc:oc + 1] if scale is not None else 1.0
                        s2 = bias_t[:, oc:oc + 1] if bias is not None else 0.0
                        ve.tensor_scalar(ob[g % NOB], pb[:, bank, :], s1, s2,
                                         OP.mult, OP.add).then_inc(s_ev.h, 1)
                        g += 1
            s_ev.v += nw * oc_n
        else:
            @blk.scalar
            def _(sc):
                g = 0
                for w in range(nw):
                    for oc in range(oc_n):
                        bank = g % NPB
                        sc.wait_ge(s_mm.h, b_mm + g + 1)
                        if g >= NOB:
                            r_st.wait(sc, g - NOB)
                        kw = {}
                        if bias is not None:
                            kw["bias"] = bias_t[:, oc:oc + 1]
                        if scale is not None:
                            kw["scale"] = scale_t[:, oc:oc + 1]
                        elif scale_const is not None:
                            kw["scale"] = float(scale_const)
                        fn = act if act is not None else AF.Copy
                        sc.activation(ob[g % NOB], pb[:, bank, :],
                                      fn, **kw).then_inc(s_ev.h, 1)
                        g += 1
            s_ev.v += nw * oc_n

        @blk.gpsimd
        def _(gp):
            g = 0
            for w in range(nw):
                for oc in range(oc_n):
                    gp.wait_ge(s_ev.h, b_ev + g + 1)
                    d = gp.dma_start(out_v[:, oc, w * WT:(w + 1) * WT],
                                     ob[g % NOB][:, :])
                    r_st.inc(d, g)
                    g += 1
            r_st.wait_all(gp)



def emit_dequant(P, wint_dram, wout_dram, rows, cols, name="dq",
                 out_tiled=False):
    nc = P.nc
    rc = rows // 128
    s_ac = P.sem("ac")
    b_ac = s_ac.v
    r_ld = Ring(P, "ldr", 2)
    r_st = Ring(P, "str", 2, step=16 * (cols // 128 if out_tiled else 1))
    P.pass_begin(name)
    nseg = 1
    while (rc // nseg) * cols > 4096 or rc % nseg:
        nseg += 1
        if nseg > rc:
            nseg = rc
            break
    seg = rc // nseg
    ti = [P.alloc3(seg, cols, I32) for _ in range(2)]
    to = [P.alloc3(seg, cols) for _ in range(2)]
    wi_v = w_tiles(wint_dram, rows, cols)
    wo_v = None if out_tiled else w_tiles(wout_dram, rows, cols)
    ocn = cols // 128
    with nc.Block() as blk:
        @blk.sync
        def _(sy):
            for s in range(nseg):
                if s >= 2:
                    sy.wait_ge(s_ac.h, b_ac + s - 1)
                d = sy.dma_start(ti[s % 2][:, :, :],
                                 wi_v[:, s * seg:(s + 1) * seg, :])
                r_ld.inc(d, s)

        @blk.scalar
        def _(sc):
            for s in range(nseg):
                r_ld.wait(sc, s)
                if s >= 2:
                    r_st.wait(sc, s - 2)
                sc.activation(to[s % 2].rearrange("p a b -> p (a b)"),
                              ti[s % 2].rearrange("p a b -> p (a b)"),
                              AF.Copy, scale=1.0 / 127.0).then_inc(s_ac.h, 1)
        s_ac.v += nseg

        @blk.gpsimd
        def _(gp):
            for s in range(nseg):
                gp.wait_ge(s_ac.h, b_ac + s + 1)
                if out_tiled:
                    for oc in range(ocn):
                        dst = bass.AP(
                            wout_dram.tensor,
                            oc * 128 * rc * 128 + s * seg * 128,
                            [[rc * 128, 128], [128, seg], [1, 128]])
                        d = gp.dma_start(
                            dst, to[s % 2].rearrange(
                                "p a (o m) -> p a o m", m=128)[:, :, oc, :])
                        r_st.inc(d, s, amount=0 if False else 16)
                else:
                    d = gp.dma_start(wo_v[:, s * seg:(s + 1) * seg, :],
                                     to[s % 2][:, :, :])
                    r_st.inc(d, s)
            r_st.wait_all(gp)



def emit_bigmm(P, src_dram, w_dram, d_in, d_out, name, out_dram,
               epi="f32r", act=None, scale=None, bias=None, old_dram=None,
               wp_windows=2, t_total=T, w_tiled=True):
    nc = P.nc
    WP = wp_windows * WT
    np_ = t_total // WP
    kci = d_in // 128
    oc_n = d_out // 128
    src_v = fm_tiles(src_dram, d_in)
    out_v = fm_tiles(out_dram, d_out)
    old_v = fm_tiles(old_dram, d_out) if old_dram is not None else None
    w_v = None if w_tiled else w_tiles(w_dram, d_in, d_out)

    s_pre = P.sem("pre"); s_mm = P.sem("mm"); s_ev = P.sem("ev")
    b_pre, b_mm, b_ev = s_pre.v, s_mm.v, s_ev.v
    nold = 1 if old_dram is not None else 0
    r_sl = Ring(P, "slr", 1, step=16 * (kci + nold))
    r_wt = Ring(P, "wtr", 2)
    NOB = 4
    r_st = Ring(P, "str", NOB)

    P.pass_begin(name)
    hsl = P.alloc3(kci, WP, F32R)
    wt = [P.alloc3(kci, 128, F32R) for _ in range(2)]
    oldw = P.alloc3(oc_n, WP) if old_dram is not None else None
    ob = [P.alloc(WT) for _ in range(NOB)]
    vecs = P.alloc(2 * oc_n)
    scale_t = vecs[:, 0:oc_n]
    bias_t = vecs[:, oc_n:2 * oc_n]
    nsc = (1 if scale is not None else 0) + (1 if bias is not None else 0)

    with (
        nc.psum_tensor(f"{name}_pb", [128, 4, WT], F32) as pb,
        nc.Block() as blk,
    ):
        @blk.sync
        def _(sy):
            if scale is not None:
                sy.dma_start(scale_t, scale).then_inc(s_pre.h, 16)
            if bias is not None:
                sy.dma_start(bias_t, bias).then_inc(s_pre.h, 16)
            gw = 0
            for p in range(np_):
                if p >= 1:
                    sy.wait_ge(s_mm.h, b_mm + p * oc_n * wp_windows)
                    if old_dram is not None:
                        sy.wait_ge(s_ev.h, b_ev + p * oc_n * wp_windows)
                for kc in range(kci):
                    d = sy.dma_start(hsl[:, kc, :],
                                     src_v[:, kc, p * WP:(p + 1) * WP].bitcast(F32R))
                    r_sl.inc(d, p)
                if old_dram is not None:
                    d = sy.dma_start(oldw[:, :, :],
                                     old_v[:, :, p * WP:(p + 1) * WP])
                    r_sl.inc(d, p)
                for oc in range(oc_n):
                    if gw >= 2:
                        sy.wait_ge(s_mm.h, b_mm + (gw - 1) * wp_windows)
                    wsrc = (w_dram[oc, :, :].bitcast(F32R)
                            if w_tiled else
                            w_v[:, :, oc * 128:(oc + 1) * 128].bitcast(F32R))
                    d = sy.dma_start(
                        wt[gw % 2].rearrange("p a b -> p (a b)") if w_tiled
                        else wt[gw % 2][:, :, :], wsrc)
                    r_wt.inc(d, gw)
                    gw += 1
        s_pre.v += 16 * nsc

        @blk.tensor
        def _(te):
            gw = 0
            for p in range(np_):
                r_sl.wait(te, p)
                for oc in range(oc_n):
                    r_wt.wait(te, gw)
                    for h in range(wp_windows):
                        bank = (gw % 2) * 2 + h
                        if gw >= 2:
                            te.wait_ge(s_ev.h,
                                       b_ev + (gw - 2) * wp_windows + h + 1)
                        for kc in range(kci):
                            mm = te.matmul(pb[:, bank, :],
                                           wt[gw % 2][:, kc, :],
                                           hsl[:, kc, h * WT:(h + 1) * WT],
                                           start=(kc == 0), stop=(kc == kci - 1))
                        mm.then_inc(s_mm.h, 1)
                    gw += 1
        s_mm.v += np_ * oc_n * wp_windows

        if epi == "residual":
            @blk.vector
            def _(ve):
                if nsc:
                    ve.wait_ge(s_pre.h, b_pre + 16 * nsc)
                g = 0
                for p in range(np_):
                    for oc in range(oc_n):
                        for h in range(wp_windows):
                            bank = (g % 2) * 2 + h
                            gi = g * wp_windows + h
                            ve.wait_ge(s_mm.h, b_mm + gi + 1)
                            r_sl.wait(ve, p)
                            if gi >= NOB:
                                r_st.wait(ve, gi - NOB)
                            oslice = oldw[:, oc, h * WT:(h + 1) * WT]
                            if scale is not None:
                                ve.scalar_tensor_tensor(
                                    ob[gi % NOB], pb[:, bank, :],
                                    scale_t[:, oc:oc + 1], oslice,
                                    OP.mult, OP.add).then_inc(s_ev.h, 1)
                            elif bias is not None:
                                ve.scalar_tensor_tensor(
                                    ob[gi % NOB], pb[:, bank, :],
                                    bias_t[:, oc:oc + 1], oslice,
                                    OP.add, OP.add).then_inc(s_ev.h, 1)
                            else:
                                ve.tensor_add(ob[gi % NOB], pb[:, bank, :],
                                              oslice).then_inc(s_ev.h, 1)
                        g += 1
            s_ev.v += np_ * oc_n * wp_windows
        else:
            @blk.scalar
            def _(sc):
                if nsc:
                    sc.wait_ge(s_pre.h, b_pre + 16 * nsc)
                g = 0
                for p in range(np_):
                    for oc in range(oc_n):
                        for h in range(wp_windows):
                            bank = (g % 2) * 2 + h
                            gi = g * wp_windows + h
                            sc.wait_ge(s_mm.h, b_mm + gi + 1)
                            if gi >= NOB:
                                r_st.wait(sc, gi - NOB)
                            kw = {}
                            if scale is not None:
                                kw["scale"] = scale_t[:, oc:oc + 1]
                            if bias is not None:
                                kw["bias"] = bias_t[:, oc:oc + 1]
                            fn = act if act is not None else AF.Copy
                            sc.activation(ob[gi % NOB], pb[:, bank, :], fn,
                                          **kw).then_inc(s_ev.h, 1)
                        g += 1
            s_ev.v += np_ * oc_n * wp_windows

        @blk.gpsimd
        def _(gp):
            gi = 0
            for p in range(np_):
                for oc in range(oc_n):
                    for h in range(wp_windows):
                        gp.wait_ge(s_ev.h, b_ev + gi + 1)
                        d = gp.dma_start(
                            out_v[:, oc, p * WP + h * WT:p * WP + (h + 1) * WT],
                            ob[gi % NOB][:, :])
                        r_st.inc(d, gi)
                        gi += 1
            r_st.wait_all(gp)



def emit_proj_tm(P, src_dram, w_dram, d_in, d_out, name, out_dram, t_total=T):
    nc = P.nc
    nw = t_total // WT
    kci = d_in // 128
    ow_n = d_out // WT
    src_v = fm_tiles(src_dram, d_in)

    s_wl = P.sem("wl"); s_mm = P.sem("mm"); s_ev = P.sem("ev")
    b_wl, b_mm, b_ev = s_wl.v, s_mm.v, s_ev.v
    r_ld = Ring(P, "ldr", 2)
    NOB = 4
    r_st = Ring(P, "str", NOB)

    P.pass_begin(name)
    wsl = P.alloc3(kci, d_out, F32R)
    xw = [P.alloc3(kci, WT, F32R) for _ in range(2)]
    ob = [P.alloc(WT) for _ in range(NOB)]
    out_t = out_dram.rearrange("(w g p) e -> w g p e", g=4, p=128)
    NPB = 4
    gr_per_w = 4 * ow_n

    with (
        nc.psum_tensor(f"{name}_pb", [128, NPB, WT], F32) as pb,
        nc.Block() as blk,
    ):
        @blk.scalar
        def _(sc):
            sc.dma_start(wsl[:, :, :],
                         w_tiles(w_dram, d_in, d_out).bitcast(F32R)
                         ).then_inc(s_wl.h, 16)
        s_wl.v += 16

        @blk.sync
        def _(sy):
            for w in range(nw):
                if w >= 2:
                    sy.wait_ge(s_mm.h, b_mm + (w - 1) * gr_per_w)
                d = sy.dma_start(xw[w % 2][:, :, :],
                                 src_v[:, :, w * WT:(w + 1) * WT].bitcast(F32R))
                r_ld.inc(d, w)

        @blk.tensor
        def _(te):
            te.wait_ge(s_wl.h, b_wl + 16)
            g = 0
            for w in range(nw):
                r_ld.wait(te, w)
                for tc in range(4):
                    for ow in range(ow_n):
                        bank = g % NPB
                        if g >= NPB:
                            te.wait_ge(s_ev.h, b_ev + g - NPB + 1)
                        for kc in range(kci):
                            mm = te.matmul(
                                pb[:, bank, :],
                                xw[w % 2][:, kc, tc * 128:(tc + 1) * 128],
                                wsl[:, kc, ow * WT:(ow + 1) * WT],
                                start=(kc == 0), stop=(kc == kci - 1))
                        mm.then_inc(s_mm.h, 1)
                        g += 1
        s_mm.v += nw * gr_per_w

        @blk.vector
        def _(ve):
            g = 0
            for w in range(nw):
                for tc in range(4):
                    for ow in range(ow_n):
                        ve.wait_ge(s_mm.h, b_mm + g + 1)
                        if g >= NOB:
                            r_st.wait(ve, g - NOB)
                        ve.tensor_copy(ob[g % NOB], pb[:, g % NPB, :]
                                       ).then_inc(s_ev.h, 1)
                        g += 1
        s_ev.v += nw * gr_per_w

        @blk.gpsimd
        def _(gp):
            g = 0
            for w in range(nw):
                for tc in range(4):
                    for ow in range(ow_n):
                        gp.wait_ge(s_ev.h, b_ev + g + 1)
                        d = gp.dma_start(out_t[w, tc, :, ow * WT:(ow + 1) * WT],
                                         ob[g % NOB][:, :])
                        r_st.inc(d, g)
                        g += 1
            r_st.wait_all(gp)



def emit_l1_attn_a_attention(P, q_dram, k_dram, v_dram, bv, x_old_dram,
                             x_out_dram, name):
    nc = P.nc
    q_v = fm_tiles(q_dram, D)
    k_v = fm_tiles(k_dram, D)
    v_t = v_dram.rearrange("(w g p) e -> w p g e", g=4, p=128)
    old_v = fm_tiles(x_old_dram, D)
    out_v = fm_tiles(x_out_dram, D)

    s_pre = P.sem("pre"); s_mm = P.sem("mm"); s_ev = P.sem("ev")
    s_ac = P.sem("ac"); s_tp = P.sem("tp")
    b_pre, b_mm, b_ev, b_ac, b_tp = s_pre.v, s_mm.v, s_ev.v, s_ac.v, s_tp.v
    r_ld = Ring(P, "ldr", 2, step=16 * 4)
    r_st = Ring(P, "str", 2)

    P.pass_begin(name)
    qw = [P.alloc3(KC, WT, F32R) for _ in range(2)]
    kw = [P.alloc3(KC, WT, F32R) for _ in range(2)]
    vw = [P.alloc3(4, D, F32R) for _ in range(2)]
    ow_ = [P.alloc3(KC, WT) for _ in range(2)]
    E = [P.alloc(128) for _ in range(2)]
    ET = [P.alloc(128, F32R) for _ in range(2)]
    ob = [P.alloc3(KC, 128) for _ in range(2)]
    sm = P.alloc(4)
    ident = P.alloc(128)
    bvv = P.alloc(KC)
    c_id = P.const("ident128", np.eye(128, dtype=np.float32))

    with (
        nc.psum_tensor(f"{name}_ps", [128, WT], F32) as ps,
        nc.psum_tensor(f"{name}_pt", [128, 128], F32) as pt,
        nc.psum_tensor(f"{name}_po", [128, 4, WT], F32) as po,
        nc.Block() as blk,
    ):
        @blk.sync
        def _(sy):
            sy.dma_start(ident, c_id).then_inc(s_pre.h, 16)
            sy.dma_start(bvv, bv).then_inc(s_pre.h, 16)
            for w in range(NW):
                if w >= 2:
                    sy.wait_ge(s_ev.h, b_ev + ((w - 2) * 4 + 3) * 5 + 5)
                d = sy.dma_start(qw[w % 2][:, :, :],
                                 q_v[:, :, w * WT:(w + 1) * WT].bitcast(F32R))
                r_ld.inc(d, w)
                d = sy.dma_start(kw[w % 2][:, :, :],
                                 k_v[:, :, w * WT:(w + 1) * WT].bitcast(F32R))
                r_ld.inc(d, w)
                d = sy.dma_start(vw[w % 2][:, :, :],
                                 v_t[w, :, :, :].bitcast(F32R))
                r_ld.inc(d, w)
                d = sy.dma_start(ow_[w % 2][:, :, :],
                                 old_v[:, :, w * WT:(w + 1) * WT])
                r_ld.inc(d, w)
        s_pre.v += 32

        @blk.tensor
        def _(te):
            te.wait_ge(s_pre.h, b_pre + 32)
            u = 0
            for w in range(NW):
                r_ld.wait(te, w)
                for tc in range(4):
                    if u >= 1:
                        te.wait_ge(s_ev.h, b_ev + (u - 1) * 5 + 4)
                        te.wait_ge(s_ac.h, b_ac + u)
                    for kc in range(KC):
                        mm = te.matmul(ps[:, 0:128],
                                       qw[w % 2][:, kc, tc * 128:(tc + 1) * 128],
                                       kw[w % 2][:, kc, tc * 128:(tc + 1) * 128],
                                       start=(kc == 0), stop=(kc == KC - 1))
                    mm.then_inc(s_mm.h, 1)
                    te.wait_ge(s_ev.h, b_ev + u * 5 + 3)
                    if u >= 1:
                        te.wait_ge(s_ev.h, b_ev + (u - 1) * 5 + 4)
                    te.transpose(pt[:], E[u % 2], ident).then_inc(s_tp.h, 1)
                    te.wait_ge(s_ev.h, b_ev + u * 5 + 4)
                    if u >= 2:
                        te.wait_ge(s_ev.h, b_ev + (u - 2) * 5 + 5)
                    for g4 in range(2):
                        bk = (u % 2) * 2 + g4
                        for oc in range(4):
                            mm = te.matmul(
                                po[:, bk, oc * 128:(oc + 1) * 128],
                                vw[w % 2][:, tc, (g4 * 4 + oc) * 128:(g4 * 4 + oc + 1) * 128],
                                ET[u % 2],
                                start=True, stop=True)
                        mm.then_inc(s_mm.h, 1)
                    u += 1
        s_mm.v += 64 * 3

        @blk.vector
        def _(ve):
            u = 0
            for w in range(NW):
                for tc in range(4):
                    ve.wait_ge(s_mm.h, b_mm + u * 3 + 1)
                    ve.reduce_max(sm[:, 0:1], ps[:, 0:128], axis=AX.X,
                                  negate=True).then_inc(s_ev.h, 1)
                    ve.wait_ge(s_ac.h, b_ac + u + 1)
                    ve.reciprocal(sm[:, 2:3], sm[:, 1:2]).then_inc(s_ev.h, 1)
                    ve.wait_ge(s_ev.h, b_ev + u * 5 + 2)
                    ve.tensor_scalar_mul(E[u % 2], E[u % 2],
                                         sm[:, 2:3]).then_inc(s_ev.h, 1)
                    ve.wait_ge(s_tp.h, b_tp + u + 1)
                    ve.tensor_copy(ET[u % 2], pt[:]).then_inc(s_ev.h, 1)
                    ve.wait_ge(s_mm.h, b_mm + u * 3 + 3)
                    if u >= 2:
                        r_st.wait(ve, u - 2)
                    for oc in range(KC):
                        bk = (u % 2) * 2 + oc // 4
                        mmv = ve.scalar_tensor_tensor(
                            ob[u % 2][:, oc, :],
                            po[:, bk, (oc % 4) * 128:(oc % 4 + 1) * 128],
                            bvv[:, oc:oc + 1],
                            ow_[w % 2][:, oc, tc * 128:(tc + 1) * 128],
                            OP.add, OP.add)
                    mmv.then_inc(s_ev.h, 1)
                    u += 1
        s_ev.v += 64 * 5

        @blk.scalar
        def _(sc):
            sc.wait_ge(s_pre.h, b_pre + 32)
            u = 0
            for w in range(NW):
                for tc in range(4):
                    sc.wait_ge(s_ev.h, b_ev + u * 5 + 1)
                    if u >= 2:
                        sc.wait_ge(s_ev.h, b_ev + (u - 1) * 5 + 3)
                    sc.activation(E[u % 2], ps[:, 0:128], AF.Exp,
                                  bias=sm[:, 0:1],
                                  accum_out=sm[:, 1:2]).then_inc(s_ac.h, 1)
                    u += 1
        s_ac.v += 64

        @blk.gpsimd
        def _(gp):
            u = 0
            for w in range(NW):
                for tc in range(4):
                    gp.wait_ge(s_ev.h, b_ev + u * 5 + 5)
                    d = gp.dma_start(
                        out_v[:, :, w * WT + tc * 128:w * WT + (tc + 1) * 128],
                        ob[u % 2][:, :, :])
                    r_st.inc(d, u)
                    u += 1
            r_st.wait_all(gp)


def emit_l1_attn_a(P, x_src_view, x_old_dram, x_out_dram, w, tag):
    nc = P.nc
    xn = nc.dram_tensor(f"{tag}_xn", [D, T], F32).ap()
    qq = nc.dram_tensor(f"{tag}_q", [D, T], F32).ap()
    kk = nc.dram_tensor(f"{tag}_k", [D, T], F32).ap()
    vv = nc.dram_tensor(f"{tag}_v", [T, D], F32).ap()
    emit_ln(P, x_src_view, xn, name=f"{tag}_ln")
    emit_proj(P, xn, w["wq"], D, D, f"{tag}_pq", out_dram=qq,
              epi="f32r", bias=w["bq"])
    emit_proj(P, xn, w["wk"], D, D, f"{tag}_pk", out_dram=kk, epi="f32r")
    emit_proj_tm(P, xn, w["wv"], D, D, f"{tag}_pv", out_dram=vv)
    emit_l1_attn_a_attention(P, qq, kk, vv, w["bv"], x_old_dram, x_out_dram,
                             f"{tag}_at")



def emit_permute(P, src_dram, dst_dram, name, forward=True,
                 old_view=None, out_dram=None):
    nc = P.nc
    s_ev = P.sem("ev"); s_sf = P.sem("sf")
    b_ev = s_ev.v
    r_ld = Ring(P, "ldr", 2, step=16 * (2 if old_view is not None else 1))
    r_st = Ring(P, "str", 2)
    P.pass_begin(name)
    ti = [P.alloc(T) for _ in range(2)]
    to = [P.alloc(T) for _ in range(2)]
    told = [P.alloc(T) for _ in range(2)] if old_view is not None else None
    src_r = src_dram.rearrange("(c p) t -> c p t", p=128)
    dst_r = (dst_dram if out_dram is None else out_dram
             ).rearrange("(c p) t -> c p t", p=128)
    with nc.Block() as blk:
        @blk.sync
        def _(sy):
            for c in range(KC):
                if c >= 2:
                    sy.wait_ge(s_ev.h, b_ev + c - 1)
                d = sy.dma_start(ti[c % 2][:, :], src_r[c, :, :])
                r_ld.inc(d, c)
                if old_view is not None:
                    d = sy.dma_start(told[c % 2][:, :], old_view[c, :, :])
                    r_ld.inc(d, c)

        @blk.vector
        def _(ve):
            for c in range(KC):
                r_ld.wait(ve, c)
                if c >= 2:
                    r_st.wait(ve, c - 2)
                if forward:
                    iv = ti[c % 2].rearrange("q (b p) -> q b p", b=32
                                             ).transpose([0, 2, 1])
                else:
                    iv = ti[c % 2].rearrange("q (p b) -> q p b", p=256
                                             ).transpose([0, 2, 1])
                if old_view is None:
                    ve.tensor_copy(to[c % 2][:, :], iv).then_inc(s_ev.h, 1)
                else:
                    ve.tensor_copy(to[c % 2][:, :], iv).then_inc(s_sf.h, 1)
                    s_sf.v += 1
                    ve.wait_ge(s_sf.h, s_sf.v)
                    ve.tensor_add(to[c % 2][:, :], to[c % 2][:, :],
                                  told[c % 2][:, :]).then_inc(s_ev.h, 1)
        s_ev.v += KC

        @blk.gpsimd
        def _(gp):
            for c in range(KC):
                gp.wait_ge(s_ev.h, b_ev + c + 1)
                d = gp.dma_start(dst_r[c, :, :], to[c % 2][:, :])
                r_st.inc(d, c)
            r_st.wait_all(gp)


def emit_l0_attn_b_attention(P, q_dram, k_dram, v_dram, bv, ao_dram, name):
    nc = P.nc
    NH = 16
    q_v = fm_tiles(q_dram, D)
    k_v = fm_tiles(k_dram, D)
    v_t = v_dram.rearrange("(w g p) e -> w p g e", g=16, p=32)
    ao_v = q_dram
    ao_r = ao_dram.rearrange("(hc r) t -> r hc t", r=128)

    s_pre = P.sem("pre"); s_mm = P.sem("mm"); s_ev = P.sem("ev")
    s_ac = P.sem("ac"); s_sf = P.sem("sf")
    b_pre, b_mm, b_ev, b_ac = s_pre.v, s_mm.v, s_ev.v, s_ac.v
    r_qk = Ring(P, "ldr", 2, step=32)
    r_v = Ring(P, "qr", 1)
    r_st = Ring(P, "str", 2)

    P.pass_begin(name)
    qsl = [P.alloc3(KC, WT, F32R) for _ in range(2)]
    ksl = [P.alloc3(KC, WT, F32R) for _ in range(2)]
    vsl = P.alloc(16 * D, F32R, parts=32).rearrange(
        "p (a b) -> p a b", a=16)
    E = [P.alloc(WT, F32R, parts=32) for _ in range(2)]
    osb = [P.alloc(WT) for _ in range(2)]
    rden = P.alloc(WT, F32R, parts=1)
    bvv = P.alloc(KC)
    ones32 = P.alloc(1, F32R, parts=32)
    ones64 = P.alloc(64, F32R, parts=1)
    c_ones32 = P.const("ones32x1", np.ones((32, 1), np.float32))
    c_ones64 = P.const("ones1x64", np.ones((1, 64), np.float32))

    with (
        nc.psum_tensor(f"{name}_pst", [32, 2, WT], F32) as pst,
        nc.psum_tensor(f"{name}_pav", [64, 2, WT], F32) as pav,
        nc.psum_tensor(f"{name}_pdn", [1, 2, WT], F32) as pdn,
        nc.psum_tensor(f"{name}_pdb", [64, 2, WT], F32) as pdb,
        nc.Block() as blk,
    ):
        @blk.sync
        def _(sy):
            sy.dma_start(ones32, c_ones32.bitcast(F32R)).then_inc(s_pre.h, 16)
            sy.dma_start(ones64, c_ones64.bitcast(F32R)).then_inc(s_pre.h, 16)
            sy.dma_start(bvv, bv).then_inc(s_pre.h, 16)
            for w in range(NW):
                if w >= 2:
                    sy.wait_ge(s_mm.h, b_mm + (w - 1) * NH * 3)
                d = sy.dma_start(qsl[w % 2][:, :, :],
                                 q_v[:, :, w * WT:(w + 1) * WT].bitcast(F32R))
                r_qk.inc(d, w)
                d = sy.dma_start(ksl[w % 2][:, :, :],
                                 k_v[:, :, w * WT:(w + 1) * WT].bitcast(F32R))
                r_qk.inc(d, w)
                if w >= 1:
                    sy.wait_ge(s_mm.h, b_mm + (w * NH - 1) * 3 + 2)
                d = sy.dma_start(vsl[:, :, :], v_t[w, :, :, :].bitcast(F32R))
                r_v.inc(d, w)
        s_pre.v += 48

        @blk.tensor
        def _(te):
            te.wait_ge(s_pre.h, b_pre + 48)
            i = 0
            for w in range(NW):
                r_qk.wait(te, w)
                r_v.wait(te, w)
                for h in range(NH):
                    hb = (h % 2) * 64
                    hc = h // 2
                    if i >= 2:
                        te.wait_ge(s_ac.h, b_ac + i - 1)
                    for pp in range(16):
                        mm = te.matmul(
                            pst[:, i % 2, pp * 32:(pp + 1) * 32],
                            ksl[w % 2][hb:hb + 64, hc, pp * 32:(pp + 1) * 32],
                            qsl[w % 2][hb:hb + 64, hc, pp * 32:(pp + 1) * 32],
                            start=True, stop=True)
                    mm.then_inc(s_mm.h, 1)
                    te.wait_ge(s_ac.h, b_ac + i + 1)
                    if i >= 2:
                        te.wait_ge(s_ev.h, b_ev + (i - 2) * 2 + 2)
                    te.matmul(pdn[:, i % 2, :], ones32, E[i % 2],
                              start=True, stop=True)
                    for pp in range(16):
                        mm = te.matmul(
                            pav[:, i % 2, pp * 32:(pp + 1) * 32],
                            vsl[:, pp, h * 64:(h + 1) * 64],
                            E[i % 2][:, pp * 32:(pp + 1) * 32],
                            start=True, stop=True)
                    mm.then_inc(s_mm.h, 1)
                    te.wait_ge(s_ev.h, b_ev + i * 2 + 1)
                    mm = te.matmul(pdb[:, i % 2, :], ones64[0:1, :],
                                   rden[0:1, :], start=True, stop=True)
                    mm.then_inc(s_mm.h, 1)
                    i += 1
        s_mm.v += NW * NH * 3

        @blk.scalar
        def _(sc):
            sc.wait_ge(s_pre.h, b_pre + 48)
            i = 0
            for w in range(NW):
                for h in range(NH):
                    sc.wait_ge(s_mm.h, b_mm + i * 3 + 1)
                    if i >= 2:
                        sc.wait_ge(s_mm.h, b_mm + (i - 2) * 3 + 2)
                    sc.activation(E[i % 2], pst[:, i % 2, :], AF.Exp,
                                  scale=0.125).then_inc(s_ac.h, 1)
                    i += 1
        s_ac.v += NW * NH

        @blk.vector
        def _(ve):
            i = 0
            for w in range(NW):
                for h in range(NH):
                    hb = (h % 2) * 64
                    hc = h // 2
                    ve.wait_ge(s_mm.h, b_mm + i * 3 + 2)
                    ve.reciprocal(rden[0:1, :],
                                  pdn[0:1, i % 2, :]).then_inc(s_ev.h, 1)
                    ve.wait_ge(s_mm.h, b_mm + i * 3 + 3)
                    if i >= 2:
                        r_st.wait(ve, i - 2)
                    ve.tensor_copy(osb[i % 2][0:64, :],
                                   pav[:, i % 2, :]).then_inc(s_sf.h, 1)
                    s_sf.v += 1
                    ve.wait_ge(s_sf.h, s_sf.v)
                    ve.tensor_mul(osb[i % 2][0:64, :], osb[i % 2][0:64, :],
                                  pdb[:, i % 2, :]).then_inc(s_sf.h, 1)
                    s_sf.v += 1
                    ve.wait_ge(s_sf.h, s_sf.v)
                    ve.tensor_scalar_add(osb[i % 2][0:64, :],
                                         osb[i % 2][0:64, :],
                                         bvv[hb:hb + 64, hc:hc + 1]
                                         ).then_inc(s_ev.h, 1)
                    i += 1
        s_ev.v += NW * NH * 2

        @blk.gpsimd
        def _(gp):
            i = 0
            for w in range(NW):
                for h in range(NH):
                    gp.wait_ge(s_ev.h, b_ev + i * 2 + 2)
                    d = gp.dma_start(
                        ao_r[(h % 2) * 64:(h % 2) * 64 + 64, h // 2,
                             w * WT:(w + 1) * WT],
                        osb[i % 2][0:64, :])
                    r_st.inc(d, i)
                    i += 1
            r_st.wait_all(gp)


def emit_l0_attn_b(P, x_src_view, x_old_dram, x_out_dram, ws, tag):
    nc = P.nc
    xn0 = nc.dram_tensor(f"{tag}_xn0", [D, T], F32).ap()
    xb = [nc.dram_tensor(f"{tag}_xb{i}", [D, T], F32).ap() for i in range(2)]
    xr = nc.dram_tensor(f"{tag}_xr", [D, T], F32).ap()
    qq = nc.dram_tensor(f"{tag}_q", [D, T], F32).ap()
    kk = nc.dram_tensor(f"{tag}_k", [D, T], F32).ap()
    vv = nc.dram_tensor(f"{tag}_v", [T, D], F32).ap()
    ao = nc.dram_tensor(f"{tag}_ao", [D, T], F32).ap()
    ff = nc.dram_tensor(f"{tag}_ff", [4096, T], F32).ap()
    emit_ln(P, x_src_view, xr, name=f"{tag}_ln0")
    emit_permute(P, xr, xb[0], f"{tag}_pm0", forward=True)
    cur = 0
    for l in range(4):
        w = ws[l]
        emit_proj(P, xb[cur], w["wqkv"][:, 0:1024], D, D, f"{tag}_q{l}",
                  out_dram=qq, epi="f32r", bias=w["bq"])
        emit_proj(P, xb[cur], w["wqkv"][:, 1024:2048], D, D, f"{tag}_k{l}",
                  out_dram=kk, epi="f32r")
        emit_proj_tm(P, xb[cur], w["wqkv"][:, 2048:3072], D, D, f"{tag}_v{l}",
                     out_dram=vv)
        emit_l0_attn_b_attention(P, qq, kk, vv, w["bv"], ao, f"{tag}_at{l}")
        emit_proj(P, ao, w["wo"], D, D, f"{tag}_o{l}", out_dram=xr,
                  epi="residual", bias=w["bo"], old_dram=xb[cur])
        emit_ln(P, fm_tiles(xr, D), xb[1 - cur], gamma=w["g1"], beta=w["be1"],
                name=f"{tag}_l1{l}")
        cur = 1 - cur
        emit_proj(P, xb[cur], w["w1"], D, 4096, f"{tag}_f1{l}", out_dram=ff,
                  epi="f32r", act=AF.Relu, bias=w["b1"])
        emit_bigmm(P, ff, w["w2"], 4096, D, f"{tag}_f2{l}", out_dram=xr,
                   epi="residual", bias=w["b2"], old_dram=xb[cur],
                   wp_windows=1)
        emit_ln(P, fm_tiles(xr, D), xb[1 - cur], gamma=w["g2"], beta=w["be2"],
                name=f"{tag}_l2{l}")
        cur = 1 - cur
    emit_permute(P, xb[cur], None, f"{tag}_pmb", forward=False,
                 old_view=fm_tiles(x_old_dram, D).rearrange("p c t -> c p t")
                 if False else x_old_dram.rearrange("(c p) t -> c p t", p=128),
                 out_dram=x_out_dram)



def _attn_a_consts():
    sel = np.zeros((128, 8 * 400), np.float32)
    for v in range(8):
        base = 16 * v
        o = v * 400
        for m in range(128):
            sel[base + m // 16, o + m] = 1.0
            sel[base + 8 + m // 16, o + 128 + m] = 1.0
            sel[base + m % 16, o + 256 + m] = 1.0
        for m in range(16):
            sel[base + m, o + 384 + m] = 1.0
    return sel


def emit_l0_attn_a_attention(P, q_dram, k_dram, v_dram, y_dram, name):
    nc = P.nc
    NB, NCH, NH = 4, 16, 16
    NU = NB * NCH * NH
    S2 = 1.0 / (4.0 * math.sqrt(2.0))
    q_v = fm_tiles(q_dram, 256)
    k_v = fm_tiles(k_dram, 256)
    v_t = v_dram.rearrange("(ch p) e -> ch p e", p=128)
    y_r = y_dram.rearrange("(hp r) t -> r hp t", r=128)

    s_pre = P.sem("pre"); s_mm = P.sem("mm"); s_ev = P.sem("ev")
    s_tp = P.sem("tp"); s_sf = P.sem("sf")
    b_pre, b_mm, b_ev, b_tp = s_pre.v, s_mm.v, s_ev.v, s_tp.v
    r_qk = Ring(P, "ldr", 2, step=16 * 4)
    r_vo = Ring(P, "qr", 2)
    r_st = Ring(P, "str", 2)

    P.pass_begin(name)
    psel = P.alloc(3200, F32R)
    qk = [P.alloc3(4, 128, F32R) for _ in range(2)]
    vo = [P.alloc(72, F32R) for _ in range(2)]
    c0 = [P.alloc(128, F32R) for _ in range(2)]
    c1 = [P.alloc(128, F32R) for _ in range(2)]
    k0 = [P.alloc(128, F32R) for _ in range(2)]
    k1 = [P.alloc(128, F32R) for _ in range(2)]
    c2q = [P.alloc(128, F32R, parts=17) for _ in range(2)]
    c2k = [P.alloc(128, F32R, parts=17) for _ in range(2)]
    rjq = P.alloc(128)
    rjk = P.alloc(128)
    k16 = [P.alloc(128) for _ in range(2)]
    ktm = P.alloc(16)
    kftm = [P.alloc(289, F32R) for _ in range(2)]
    ET = [P.alloc(128, F32R) for _ in range(2)]
    sA = P.alloc(NH * 130, F32R)
    s2A = P.alloc(NH * 65, F32R, parts=17)
    ysb = P.alloc(128, parts=64)
    zz = P.alloc(128, parts=1)
    zr = P.alloc(128, F32R, parts=1)
    ob = [P.alloc3(8, 128) for _ in range(2)]
    mask = P.alloc(128)
    ident = P.alloc(128)
    ones64 = P.alloc(64, F32R, parts=1)

    c_sel = P.const("attn_a_sel", _attn_a_consts())
    c_mask = P.const("triu128", np.triu(np.ones((128, 128), np.float32)))
    c_id = P.const("ident128", np.eye(128, dtype=np.float32))
    c_ones64 = P.const("ones1x64", np.ones((1, 64), np.float32))
    c_ones2 = P.const("ones128x2", np.ones((128, 2), np.float32))
    c_ones1 = P.const("ones1x1", np.ones((1, 1), np.float32))
    c_onesr17 = P.const("ones17r", np.ones((1, 128), np.float32))

    NPRE = 10

    with (
        nc.psum_tensor(f"{name}_psq", [128, 2, WT], F32) as psq,
        nc.psum_tensor(f"{name}_psk", [128, 2, WT], F32) as psk,
        nc.psum_tensor(f"{name}_ppt", [128, 128], F32) as ppt,
        nc.psum_tensor(f"{name}_pat", [128, 128], F32) as pat,
        nc.psum_tensor(f"{name}_pyz", [128, 256], F32) as pyz,
        nc.psum_tensor(f"{name}_psc", [128, 198], F32) as psc,
        nc.Block() as blk,
    ):
        npre = 0

        @blk.sync
        def _(sy):
            nonlocal_npre = 0
            for d_, s_ in [
                (psel, c_sel.bitcast(F32R)),
                (mask, c_mask), (ident, c_id),
                (ones64, c_ones64.bitcast(F32R)),
                (vo[0][:, 64:65], c_ones1.bitcast(F32R)[0:1, :].broadcast_to([128, 1])),
            ]:
                pass
            sy.dma_start(psel, c_sel.bitcast(F32R)).then_inc(s_pre.h, 16)
            sy.dma_start(mask, c_mask).then_inc(s_pre.h, 16)
            sy.dma_start(ident, c_id).then_inc(s_pre.h, 16)
            sy.dma_start(ones64, c_ones64.bitcast(F32R)).then_inc(s_pre.h, 16)
            for i in range(2):
                sy.dma_start(vo[i][:, 64:66],
                             c_ones2.bitcast(F32R)).then_inc(s_pre.h, 16)
                sy.dma_start(kftm[i][:, 272:273],
                             c_onesr17.bitcast(F32R).rearrange("a b -> b a")
                             ).then_inc(s_pre.h, 16)
                sy.dma_start(c2q[i][16:17, :],
                             c_onesr17.bitcast(F32R)).then_inc(s_pre.h, 16)
                sy.dma_start(c2k[i][16:17, :],
                             c_onesr17.bitcast(F32R)).then_inc(s_pre.h, 16)
            u = 0
            for G in range(NB * NCH):
                if G >= 2:
                    sy.wait_ge(s_mm.h, b_mm + (G - 1) * NH * 6)
                b, c = G // NCH, G % NCH
                t0 = b * 2048 + c * 128
                for j in range(2):
                    d = sy.dma_start(qk[G % 2][:, j, :],
                                     q_v[:, j, t0:t0 + 128].bitcast(F32R))
                    r_qk.inc(d, G)
                    d = sy.dma_start(qk[G % 2][:, 2 + j, :],
                                     k_v[:, j, t0:t0 + 128].bitcast(F32R))
                    r_qk.inc(d, G)
                for h in range(NH):
                    if u >= 2:
                        sy.wait_ge(s_mm.h, b_mm + (u - 1) * 6)
                    d = sy.dma_start(vo[u % 2][:, 0:64],
                                     v_t[G, :, h * 64:(h + 1) * 64].bitcast(F32R))
                    r_vo.inc(d, u)
                    u += 1
        s_pre.v += 16 * 12

        @blk.gpsimd
        def _(gp):
            gp.memset(k16[0], 0.0)
            gp.memset(k16[1], 0.0).then_inc(s_pre.h, 1)
        s_pre.v += 1

        @blk.tensor
        def _(te):
            te.wait_ge(s_pre.h, b_pre + 16 * 12 + 1)
            u = 0
            for G in range(NB * NCH):
                r_qk.wait(te, G)
                b, c = G // NCH, G % NCH
                for h in range(NH):
                    v = h % 8
                    qc = h // 8
                    so = v * 400
                    if u >= 2:
                        te.wait_ge(s_ev.h, b_ev + (u - 2) * 7 + 2)
                    bq = psq[:, u % 2, :]
                    te.matmul(bq[:, 0:128], psel[:, so:so + 128],
                              qk[G % 2][:, qc, :], start=True, stop=True)
                    te.matmul(bq[:, 128:256], psel[:, so + 128:so + 256],
                              qk[G % 2][:, qc, :], start=True, stop=True)
                    te.matmul(bq[:, 256:384], psel[:, so + 256:so + 384],
                              qk[G % 2][:, qc, :], start=True, stop=True)
                    mm = te.matmul(bq[0:16, 384:512], psel[:, so + 384:so + 400],
                                   qk[G % 2][:, qc, :], start=True, stop=True)
                    mm.then_inc(s_mm.h, 1)
                    bk = psk[:, u % 2, :]
                    te.matmul(bk[:, 0:128], psel[:, so:so + 128],
                              qk[G % 2][:, 2 + qc, :], start=True, stop=True)
                    te.matmul(bk[:, 128:256], psel[:, so + 128:so + 256],
                              qk[G % 2][:, 2 + qc, :], start=True, stop=True)
                    te.matmul(bk[:, 256:384], psel[:, so + 256:so + 384],
                              qk[G % 2][:, 2 + qc, :], start=True, stop=True)
                    mm = te.matmul(bk[0:16, 384:512], psel[:, so + 384:so + 400],
                                   qk[G % 2][:, 2 + qc, :], start=True, stop=True)
                    mm.then_inc(s_mm.h, 1)
                    te.wait_ge(s_ev.h, b_ev + u * 7 + 1)
                    if u >= 1:
                        te.wait_ge(s_ev.h, b_ev + (u - 1) * 7 + 3)
                    te.transpose(ppt[:], k16[u % 2], ident).then_inc(s_tp.h, 1)
                    te.wait_ge(s_ev.h, b_ev + u * 7 + 2)
                    if u >= 1:
                        te.wait_ge(s_ev.h, b_ev + (u - 1) * 7 + 4)
                    te.matmul(pat[:], k0[u % 2], c0[u % 2], start=True, stop=False)
                    te.matmul(pat[:], k1[u % 2], c1[u % 2], start=False, stop=False)
                    mm = te.matmul(pat[:], c2k[u % 2][:, :], c2q[u % 2][:, :],
                                   start=False, stop=True)
                    mm.then_inc(s_mm.h, 1)
                    r_vo.wait(te, u)
                    te.wait_ge(s_ev.h, b_ev + u * 7 + 4)
                    if u >= 1:
                        te.wait_ge(s_ev.h, b_ev + (u - 1) * 7 + 7)
                    if c >= 1:
                        te.wait_ge(s_ev.h, b_ev + (u - NH) * 7 + 5)
                    one_shot = (c == 0)
                    te.matmul(pyz[0:65, 0:128], vo[u % 2][:, 0:65], ET[u % 2],
                              start=True, stop=one_shot)
                    if not one_shot:
                        te.matmul(pyz[0:65, 0:128], sA[:, h * 130:h * 130 + 65],
                                  c0[u % 2], start=False, stop=False)
                        te.matmul(pyz[0:65, 0:128],
                                  sA[:, h * 130 + 65:h * 130 + 130],
                                  c1[u % 2], start=False, stop=False)
                        te.matmul(pyz[0:65, 0:128],
                                  s2A[:, h * 65:(h + 1) * 65],
                                  c2q[u % 2][:, :], start=False, stop=True)
                    nc.tensor.nop().then_inc(s_mm.h, 1)
                    te.wait_ge(s_ev.h, b_ev + u * 7 + 3)
                    if u >= 1:
                        te.wait_ge(s_ev.h, b_ev + (u - 1) * 7 + 5)
                    te.matmul(psc[:, 0:66], kftm[u % 2][:, 0:128],
                              vo[u % 2][:, 0:66], start=True, stop=True)
                    te.matmul(psc[:, 66:132], kftm[u % 2][:, 128:256],
                              vo[u % 2][:, 0:66], start=True, stop=True)
                    mm = te.matmul(psc[0:17, 132:198], kftm[u % 2][:, 256:273],
                                   vo[u % 2][:, 0:66], start=True, stop=True)
                    mm.then_inc(s_mm.h, 1)
                    te.wait_ge(s_ev.h, b_ev + u * 7 + 6)
                    mm = te.matmul(pyz[0:64, 128:256], ones64[0:1, :],
                                   zr[0:1, :], start=True, stop=True)
                    mm.then_inc(s_mm.h, 1)
                    u += 1
        s_mm.v += NU * 6
        s_tp.v += NU

        @blk.vector
        def _(ve):
            rj_done = [0, 0, 0, 0, 0]
            u = 0
            for G in range(NB * NCH):
                b, c = G // NCH, G % NCH
                for h in range(NH):
                    bq = psq[:, u % 2, :]
                    bk = psk[:, u % 2, :]
                    ve.wait_ge(s_mm.h, b_mm + u * 6 + 2)
                    if u >= 2:
                        ve.wait_ge(s_tp.h, b_tp + u - 1)
                    ve.tensor_copy(k16[u % 2][0:16, :],
                                   bk[0:16, 384:512]).then_inc(s_ev.h, 1)
                    ve.wait_ge(s_mm.h, b_mm + u * 6 + 1)
                    if u >= 1:
                        ve.wait_ge(s_sf.h, rj_done[0])
                    ve.tensor_copy(rjq, bq[:, 256:384]).then_inc(s_sf.h, 1)
                    s_sf.v += 1
                    ve.wait_ge(s_sf.h, s_sf.v)
                    ve.scalar_tensor_tensor(c0[u % 2], rjq, S2, bq[:, 0:128],
                                            OP.mult, OP.mult)
                    ve.scalar_tensor_tensor(c1[u % 2], rjq, S2, bq[:, 128:256],
                                            OP.mult, OP.mult).then_inc(s_sf.h, 1)
                    s_sf.v += 1
                    rj_done[0] = s_sf.v
                    ve.tensor_scalar_mul(c2q[u % 2][0:16, :], bq[0:16, 384:512],
                                         0.5)
                    if u >= 1:
                        ve.wait_ge(s_sf.h, rj_done[1])
                    ve.tensor_copy(rjk, bk[:, 256:384]).then_inc(s_sf.h, 1)
                    s_sf.v += 1
                    ve.wait_ge(s_sf.h, s_sf.v)
                    ve.scalar_tensor_tensor(k0[u % 2], rjk, S2, bk[:, 0:128],
                                            OP.mult, OP.mult)
                    ve.scalar_tensor_tensor(k1[u % 2], rjk, S2, bk[:, 128:256],
                                            OP.mult, OP.mult).then_inc(s_sf.h, 1)
                    s_sf.v += 1
                    rj_done[1] = s_sf.v
                    ve.tensor_scalar_mul(c2k[u % 2][0:16, :], bk[0:16, 384:512],
                                         0.5).then_inc(s_ev.h, 1)
                    ve.wait_ge(s_tp.h, b_tp + u + 1)
                    if u >= 1:
                        ve.wait_ge(s_sf.h, rj_done[2])
                    ve.tensor_copy(ktm, ppt[:, 0:16]).then_inc(s_sf.h, 1)
                    s_sf.v += 1
                    ve.wait_ge(s_sf.h, s_sf.v)
                    a16 = ktm.unsqueeze(2).broadcast_to([128, 16, 16])
                    b16 = ktm.unsqueeze(1).broadcast_to([128, 16, 16])
                    ve.scalar_tensor_tensor(
                        kftm[u % 2][:, 0:256].rearrange("p (i j) -> p i j", i=16),
                        a16, S2, b16, OP.mult, OP.mult)
                    ve.tensor_scalar_mul(kftm[u % 2][:, 256:272], ktm,
                                         0.5).then_inc(s_ev.h, 1)
                    nc.vector.engine_nop().then_inc(s_sf.h, 1)
                    s_sf.v += 1
                    rj_done[2] = s_sf.v
                    ve.wait_ge(s_mm.h, b_mm + u * 6 + 3)
                    ve.tensor_mul(ET[u % 2], pat[:], mask).then_inc(s_ev.h, 1)
                    ve.wait_ge(s_mm.h, b_mm + u * 6 + 5)
                    if c == 0:
                        ve.tensor_copy(sA[:, h * 130:h * 130 + 65], psc[:, 0:65])
                        ve.tensor_copy(sA[:, h * 130 + 65:h * 130 + 130],
                                       psc[:, 66:131])
                        mv = ve.tensor_copy(s2A[:, h * 65:(h + 1) * 65],
                                            psc[0:17, 132:197])
                    else:
                        ve.tensor_add(sA[:, h * 130:h * 130 + 65],
                                      sA[:, h * 130:h * 130 + 65], psc[:, 0:65])
                        ve.tensor_add(sA[:, h * 130 + 65:h * 130 + 130],
                                      sA[:, h * 130 + 65:h * 130 + 130],
                                      psc[:, 66:131])
                        mv = ve.tensor_add(s2A[:, h * 65:(h + 1) * 65],
                                           s2A[:, h * 65:(h + 1) * 65],
                                           psc[0:17, 132:197])
                    mv.then_inc(s_ev.h, 1)
                    ve.wait_ge(s_mm.h, b_mm + u * 6 + 4)
                    if u >= 1:
                        ve.wait_ge(s_sf.h, rj_done[3])
                    ve.tensor_scalar_add(zz[0:1, :], pyz[64:65, 0:128],
                                         1e-12).then_inc(s_sf.h, 1)
                    s_sf.v += 1
                    ve.wait_ge(s_sf.h, s_sf.v)
                    ve.reciprocal(zr[0:1, :], zz[0:1, :]).then_inc(s_sf.h, 1)
                    s_sf.v += 1
                    rj_done[3] = s_sf.v
                    nc.vector.engine_nop().then_inc(s_ev.h, 1)
                    ve.wait_ge(s_mm.h, b_mm + u * 6 + 6)
                    if h == 0 and u >= 2:
                        r_st.wait(ve, G - 2)
                    if u >= 1:
                        ve.wait_ge(s_sf.h, rj_done[4])
                    ve.tensor_copy(ysb, pyz[0:64, 0:128]).then_inc(s_sf.h, 1)
                    s_sf.v += 1
                    ve.wait_ge(s_sf.h, s_sf.v)
                    ve.tensor_mul(ob[G % 2][(h % 2) * 64:(h % 2) * 64 + 64,
                                            h // 2, :],
                                  ysb, pyz[0:64, 128:256]).then_inc(s_sf.h, 1)
                    s_sf.v += 1
                    rj_done[4] = s_sf.v
                    nc.vector.engine_nop().then_inc(s_ev.h, 1)
                    u += 1
        s_ev.v += NU * 7

        @blk.gpsimd
        def _(gp):
            for G in range(NB * NCH):
                b, c = G // NCH, G % NCH
                t0 = b * 2048 + c * 128
                gp.wait_ge(s_ev.h, b_ev + (G + 1) * NH * 7)
                d = gp.dma_start(y_r[:, :, t0:t0 + 128], ob[G % 2][:, :, :])
                r_st.inc(d, G)
            r_st.wait_all(gp)


def emit_l0_attn_a(P, x_src_view, x_old_dram, x_out_dram, w, tag):
    nc = P.nc
    xn = nc.dram_tensor(f"{tag}_xn", [D, T], F32).ap()
    qq = nc.dram_tensor(f"{tag}_q", [256, T], F32).ap()
    kk = nc.dram_tensor(f"{tag}_k", [256, T], F32).ap()
    vv = nc.dram_tensor(f"{tag}_v", [T, D], F32).ap()
    yy = nc.dram_tensor(f"{tag}_y", [D, T], F32).ap()
    emit_ln(P, x_src_view, xn, name=f"{tag}_ln")
    emit_proj(P, xn, w["wq"], D, 256, f"{tag}_pq", out_dram=qq, epi="f32r")
    emit_proj(P, xn, w["wk"], D, 256, f"{tag}_pk", out_dram=kk, epi="f32r")
    emit_proj_tm(P, xn, w["wv"], D, D, f"{tag}_pv", out_dram=vv)
    emit_l0_attn_a_attention(P, qq, kk, vv, yy, f"{tag}_at")
    emit_proj(P, yy, w["wo"], D, D, f"{tag}_po", out_dram=x_out_dram,
              epi="residual", old_dram=x_old_dram)



def emit_l1_attn_b_attention(P, q_dram, k_dram, v_dram, ao_dram, name):
    nc = P.nc
    NCK = 32
    q_v = fm_tiles(q_dram, D)
    k_v = fm_tiles(k_dram, D)
    v_t = v_dram.rearrange("(kc p) e -> p kc e", p=128)
    ao_v = fm_tiles(ao_dram, D)

    s_pre = P.sem("pre"); s_mm = P.sem("mm"); s_av = P.sem("av")
    s_ev = P.sem("ev"); s_ac = P.sem("ac"); s_sf = P.sem("sf")
    b_pre, b_mm, b_av, b_ev, b_ac = s_pre.v, s_mm.v, s_av.v, s_ev.v, s_ac.v
    r_kv = Ring(P, "ldr", 2, step=32)
    r_q = Ring(P, "qr", 2)
    r_st = Ring(P, "str", 2)
    MU = NCK + 1

    P.pass_begin(name)
    ksl = [P.alloc3(NCK, 128, F32R) for _ in range(2)]
    vsl = [P.alloc3(NCK, 128, F32R) for _ in range(2)]
    qt = [P.alloc(WT, F32R) for _ in range(2)]
    E = [P.alloc(WT, F32R) for _ in range(2)]
    osb = [P.alloc(WT) for _ in range(2)]
    rden_t = P.alloc(WT, F32R)
    rden = rden_t[0:1, :]
    ones_col = P.alloc(1, F32R)
    ones_row_t = P.alloc(128, F32R)
    ones_row = ones_row_t[0:1, :]
    c_ones_col = P.const("ones128x1", np.ones((128, 1), np.float32))
    c_ones_row = P.const("ones1x128", np.ones((1, 128), np.float32))

    with (
        nc.psum_tensor(f"{name}_pst", [128, 2, WT], F32) as pst,
        nc.psum_tensor(f"{name}_pav", [128, 2, WT], F32) as pav,
        nc.psum_tensor(f"{name}_pdn", [1, 2, WT], F32) as pdn,
        nc.psum_tensor(f"{name}_pdb", [128, 2, WT], F32) as pdb,
        nc.Block() as blk,
    ):
        @blk.sync
        def _(sy):
            sy.dma_start(ones_col, c_ones_col.bitcast(F32R)).then_inc(s_pre.h, 16)
            sy.dma_start(ones_row, c_ones_row.bitcast(F32R)).then_inc(s_pre.h, 16)
            u = 0
            for h in range(8):
                if h >= 2:
                    sy.wait_ge(s_mm.h, b_mm + (h - 1) * NW * MU)
                    sy.wait_ge(s_av.h, b_av + (h - 1) * NW * NCK)
                d = sy.dma_start(ksl[h % 2][:, :, :],
                                 k_v[:, h, :].rearrange("p (a b) -> p a b",
                                                        a=NCK).bitcast(F32R))
                r_kv.inc(d, h)
                d = sy.dma_start(vsl[h % 2][:, :, :],
                                 v_t[:, :, h * 128:(h + 1) * 128].bitcast(F32R))
                r_kv.inc(d, h)
                for w in range(NW):
                    if u >= 2:
                        sy.wait_ge(s_mm.h, b_mm + (u - 1) * MU)
                    d = sy.dma_start(qt[u % 2][:, :],
                                     q_v[:, h, w * WT:(w + 1) * WT].bitcast(F32R))
                    r_q.inc(d, u)
                    u += 1
        s_pre.v += 32

        @blk.tensor
        def _(te):
            te.wait_ge(s_pre.h, b_pre + 32)
            u = 0
            for h in range(8):
                r_kv.wait(te, h)
                for w in range(NW):
                    r_q.wait(te, u)
                    if u >= 2:
                        te.wait_ge(s_ev.h, b_ev + (u - 2) * 2 + 2)
                    for kc in range(NCK):
                        g = u * NCK + kc
                        if g >= 2:
                            te.wait_ge(s_ac.h, b_ac + g - 1)
                        te.matmul(pst[:, kc % 2, :], ksl[h % 2][:, kc, :],
                                  qt[u % 2][:, :],
                                  start=True, stop=True).then_inc(s_mm.h, 1)
                        te.wait_ge(s_ac.h, b_ac + g + 1)
                        te.matmul(pdn[:, u % 2, :], ones_col, E[kc % 2],
                                  start=(kc == 0), stop=(kc == NCK - 1))
                        te.matmul(pav[:, u % 2, :],
                                  vsl[h % 2][:, kc, :], E[kc % 2],
                                  start=(kc == 0),
                                  stop=(kc == NCK - 1)).then_inc(s_av.h, 1)
                    te.wait_ge(s_ev.h, b_ev + u * 2 + 1)
                    te.matmul(pdb[:, u % 2, :], ones_row, rden,
                              start=True, stop=True).then_inc(s_mm.h, 1)
                    u += 1
        s_mm.v += 8 * NW * MU
        s_av.v += 8 * NW * NCK

        @blk.scalar
        def _(sc):
            u = 0
            for h in range(8):
                for w in range(NW):
                    for kc in range(NCK):
                        g = u * NCK + kc
                        sc.wait_ge(s_mm.h, b_mm + u * MU + kc + 1)
                        if g >= 2:
                            sc.wait_ge(s_av.h, b_av + g - 1)
                        sc.activation(E[kc % 2], pst[:, kc % 2, :],
                                      AF.Exp).then_inc(s_ac.h, 1)
                    u += 1
        s_ac.v += 8 * NW * NCK

        @blk.vector
        def _(ve):
            u = 0
            for h in range(8):
                for w in range(NW):
                    ve.wait_ge(s_av.h, b_av + (u + 1) * NCK)
                    ve.reciprocal(rden,
                                  pdn[0:1, u % 2, :]).then_inc(s_ev.h, 1)
                    ve.wait_ge(s_mm.h, b_mm + (u + 1) * MU)
                    if u >= 2:
                        r_st.wait(ve, u - 2)
                    ve.tensor_copy(osb[u % 2],
                                   pav[:, u % 2, :]).then_inc(s_sf.h, 1)
                    s_sf.v += 1
                    ve.wait_ge(s_sf.h, s_sf.v)
                    ve.tensor_mul(osb[u % 2], osb[u % 2],
                                  pdb[:, u % 2, :]).then_inc(s_ev.h, 1)
                    u += 1
        s_ev.v += 8 * NW * 2

        @blk.gpsimd
        def _(gp):
            u = 0
            for h in range(8):
                for w in range(NW):
                    gp.wait_ge(s_ev.h, b_ev + u * 2 + 2)
                    d = gp.dma_start(ao_v[:, h, w * WT:(w + 1) * WT],
                                     osb[u % 2][:, :])
                    r_st.inc(d, u)
                    u += 1
            r_st.wait_all(gp)


def emit_l1_attn_b(P, x_src_view, x_old_dram, x_out_dram, ctx_fm, w, tag):
    nc = P.nc
    xn = nc.dram_tensor(f"{tag}_xn", [D, T], F32).ap()
    qq = nc.dram_tensor(f"{tag}_q", [D, T], F32).ap()
    kk = nc.dram_tensor(f"{tag}_k", [D, 4096], F32).ap()
    vv = nc.dram_tensor(f"{tag}_v", [4096, D], F32).ap()
    ao = nc.dram_tensor(f"{tag}_ao", [D, T], F32).ap()
    emit_ln(P, x_src_view, xn, name=f"{tag}_ln")
    emit_proj(P, xn, w["wq"], D, D, f"{tag}_pq", out_dram=qq, epi="f32r",
              scale_const=1.5 / math.sqrt(128.0))
    emit_proj(P, ctx_fm, w["wk"], 768, D, f"{tag}_pk", out_dram=kk,
              epi="f32r", t_total=4096)
    emit_proj_tm(P, ctx_fm, w["wv"], 768, D, f"{tag}_pv", out_dram=vv,
                 t_total=4096)
    emit_l1_attn_b_attention(P, qq, kk, vv, ao, f"{tag}_at")
    emit_proj(P, ao, w["wout"], D, D, f"{tag}_po", out_dram=x_out_dram,
              epi="residual", bias=w["bout"], old_dram=x_old_dram)



def emit_l0_ffn(P, x_src_view, x_old_dram, x_out_dram, wg_dq, wu_dq, wd_dq, tag):
    nc = P.nc
    xn = nc.dram_tensor(f"{tag}_xn", [D, T], F32).ap()
    hh = nc.dram_tensor(f"{tag}_h", [3072, T], F32).ap()
    emit_ln(P, x_src_view, xn, name=f"{tag}_ln")
    for half in range(2):
        o0 = half * 1536
        emit_proj(P, xn, wg_dq[:, o0:o0 + 1536], D, 1536, f"{tag}_gu{half}",
                  out_dram=hh[o0:o0 + 1536, :], epi="gateup",
                  w2_dram=wu_dq[:, o0:o0 + 1536])
    emit_bigmm(P, hh, wd_dq, 3072, D, f"{tag}_dn", out_dram=x_out_dram,
               epi="residual", old_dram=x_old_dram, wp_windows=2)


def emit_l1_ffn(P, x_src_view, x_old_dram, x_out_dram, w, tag):
    nc = P.nc
    xn = nc.dram_tensor(f"{tag}_xn", [D, T], F32).ap()
    hh = nc.dram_tensor(f"{tag}_h", [4096, T], F32).ap()
    g1 = nc.dram_tensor(f"{tag}_g1", [4096, T], F32).ap()
    g2 = nc.dram_tensor(f"{tag}_g2", [4096, T], F32).ap()
    emit_ln(P, x_src_view, xn, name=f"{tag}_ln")
    for half in range(2):
        o0 = half * 2048
        emit_proj(P, xn, w["w1"][:, o0:o0 + 2048], D, 2048, f"{tag}_gu{half}",
                  out_dram=hh[o0:o0 + 2048, :], epi="gateup",
                  scale=w["w1s"][:, half * 16:(half + 1) * 16],
                  w2_dram=w["w2"][:, o0:o0 + 2048],
                  scale2=w["w2s"][:, half * 16:(half + 1) * 16])
    emit_bigmm(P, hh, w["h1"], 4096, 4096, f"{tag}_h1", out_dram=g1,
               act=AF.Gelu, scale=w["h1s"], wp_windows=2)
    emit_bigmm(P, g1, w["h2"], 4096, 4096, f"{tag}_h2", out_dram=g2,
               act=AF.Gelu, scale=w["h2s"], wp_windows=2)
    emit_bigmm(P, g2, w["w3"], 4096, D, f"{tag}_w3", out_dram=x_out_dram,
               epi="residual", scale=w["w3s"], old_dram=x_old_dram,
               wp_windows=1)



def build_l0_ffn_launch():
    P = Prog("l0ffn")
    nc = P.nc
    xin = nc.dram_tensor("xin", [D, T], F32, kind="ExternalInput").ap()
    wg_i = nc.dram_tensor("wg", [D, 3072], I32, kind="ExternalInput").ap()
    wu_i = nc.dram_tensor("wu", [D, 3072], I32, kind="ExternalInput").ap()
    wd_i = nc.dram_tensor("wd", [3072, D], I32, kind="ExternalInput").ap()
    xout = nc.dram_tensor("xout", [D, T], F32, kind="ExternalOutput").ap()
    wg = nc.dram_tensor("wg_dq", [D, 3072], F32).ap()
    wu = nc.dram_tensor("wu_dq", [D, 3072], F32).ap()
    wd = nc.dram_tensor("wd_dq", [3072, D], F32).ap()
    emit_dequant(P, wg_i, wg, D, 3072, "dqg")
    emit_dequant(P, wu_i, wu, D, 3072, "dqu")
    emit_dequant(P, wd_i, wd, 3072, D, "dqd")
    emit_l0_ffn(P, fm_tiles(xin, D), xin, xout, wg, wu, wd, "f0")
    P.pass_end()
    return P


def build_l1_ffn_launch():
    P = Prog("l1ffn")
    nc = P.nc
    xin = nc.dram_tensor("xin", [D, T], F32, kind="ExternalInput").ap()
    w = {}
    w["w1"] = nc.dram_tensor("w1", [D, 4096], F32, kind="ExternalInput").ap()
    w["w2"] = nc.dram_tensor("w2", [D, 4096], F32, kind="ExternalInput").ap()
    w["h1"] = nc.dram_tensor("h1", [4096, 4096], F32, kind="ExternalInput").ap()
    w["h2"] = nc.dram_tensor("h2", [4096, 4096], F32, kind="ExternalInput").ap()
    w["w3"] = nc.dram_tensor("w3", [4096, D], F32, kind="ExternalInput").ap()
    for k in ["w1s", "w2s", "h1s", "h2s"]:
        w[k] = nc.dram_tensor(k, [128, 32], F32, kind="ExternalInput").ap()
    w["w3s"] = nc.dram_tensor("w3s", [128, 8], F32, kind="ExternalInput").ap()
    xout = nc.dram_tensor("xout", [D, T], F32, kind="ExternalOutput").ap()
    emit_l1_ffn(P, fm_tiles(xin, D), xin, xout, w, "f1")
    P.pass_end()
    return P


def build_l1_attn_a_launch():
    P = Prog("l1attna")
    nc = P.nc
    xin = nc.dram_tensor("xin", [D, T], F32, kind="ExternalInput").ap()
    w = {}
    for k in ["wq", "wk", "wv"]:
        w[k] = nc.dram_tensor(k, [D, D], F32, kind="ExternalInput").ap()
    for k in ["bq", "bv"]:
        w[k] = nc.dram_tensor(k, [128, KC], F32, kind="ExternalInput").ap()
    xout = nc.dram_tensor("xout", [D, T], F32, kind="ExternalOutput").ap()
    emit_l1_attn_a(P, fm_tiles(xin, D), xin, xout, w, "a1")
    P.pass_end()
    return P


def build_l1_attn_b_launch():
    P = Prog("l1attnb")
    nc = P.nc
    xin = nc.dram_tensor("xin", [D, T], F32, kind="ExternalInput").ap()
    ctx = nc.dram_tensor("ctx", [768, 4096], F32, kind="ExternalInput").ap()
    w = {}
    w["wq"] = nc.dram_tensor("wq", [D, D], F32, kind="ExternalInput").ap()
    w["wk"] = nc.dram_tensor("wk", [768, D], F32, kind="ExternalInput").ap()
    w["wv"] = nc.dram_tensor("wv", [768, D], F32, kind="ExternalInput").ap()
    w["wout"] = nc.dram_tensor("wout", [D, D], F32, kind="ExternalInput").ap()
    w["bout"] = nc.dram_tensor("bout", [128, KC], F32, kind="ExternalInput").ap()
    xout = nc.dram_tensor("xout", [D, T], F32, kind="ExternalOutput").ap()
    emit_l1_attn_b(P, fm_tiles(xin, D), xin, xout, ctx, w, "b1")
    P.pass_end()
    return P


def build_l0_attn_a_launch():
    P = Prog("l0attna")
    nc = P.nc
    xin = nc.dram_tensor("xin", [D, T], F32, kind="ExternalInput").ap()
    w = {}
    w["wq"] = nc.dram_tensor("wq", [D, 256], F32, kind="ExternalInput").ap()
    w["wk"] = nc.dram_tensor("wk", [D, 256], F32, kind="ExternalInput").ap()
    w["wv"] = nc.dram_tensor("wv", [D, D], F32, kind="ExternalInput").ap()
    w["wo"] = nc.dram_tensor("wo", [D, D], F32, kind="ExternalInput").ap()
    xout = nc.dram_tensor("xout", [D, T], F32, kind="ExternalOutput").ap()
    emit_l0_attn_a(P, fm_tiles(xin, D), xin, xout, w, "a0")
    P.pass_end()
    return P


def build_l0_attn_b_launch():
    P = Prog("l0attnb")
    nc = P.nc
    xin = nc.dram_tensor("xin", [D, T], F32, kind="ExternalInput").ap()
    ws = []
    for l in range(4):
        w = {}
        w["wqkv"] = nc.dram_tensor(f"wqkv{l}", [D, 3072], F32,
                                   kind="ExternalInput").ap()
        w["wo"] = nc.dram_tensor(f"wo{l}", [D, D], F32, kind="ExternalInput").ap()
        w["w1"] = nc.dram_tensor(f"w1{l}", [D, 4096], F32,
                                 kind="ExternalInput").ap()
        w["w2"] = nc.dram_tensor(f"w2{l}", [4096, D], F32,
                                 kind="ExternalInput").ap()
        for k, n in [("bq", KC), ("bv", KC), ("bo", KC), ("b1", 32),
                     ("b2", KC), ("g1", KC), ("be1", KC), ("g2", KC),
                     ("be2", KC)]:
            w[k] = nc.dram_tensor(f"{k}{l}", [128, n], F32,
                                  kind="ExternalInput").ap()
        ws.append(w)
    xout = nc.dram_tensor("xout", [D, T], F32, kind="ExternalOutput").ap()
    emit_l0_attn_b(P, fm_tiles(xin, D), xin, xout, ws, "b0")
    P.pass_end()
    return P



def emit_repartition(P, xa_h, gg_h, xb_h, name="rp", no_cc=False, no_dyn=False):
    nc = P.nc
    s_cc = P.sem("cc"); s_st = P.sem("st")
    P.pass_begin(name)
    b_st = s_st.v

    def piece_dmas(eng, pieces):
        with eng.register("rpoff") as reg:
            first = True
            prev = 0
            for (r, bp) in pieces:
                stat = r * 1024 * T + bp * 2048
                if first:
                    eng.reg_alu(reg, eng.partition_id(), 256, OP.mult)
                    if stat:
                        eng.reg_alu(reg, reg, stat, OP.add)
                    first = False
                else:
                    eng.reg_alu(reg, reg, stat - prev, OP.add)
                prev = stat
                srcap = bass.AP(gg_h, reg, [[T, 1024], [1, 256]])
                dstap = bass.AP(xb_h, (r * 4 + bp) * 256, [[T, 1024], [1, 256]])
                eng.dma_start(dstap, srcap).then_inc(s_st.h, 16)

    pieces = [(r, bp) for r in range(8) for bp in range(4)]
    with nc.Block() as blk:
        @blk.gpsimd
        def _(gp):
            gp.collective_compute("AllGather", OP.bypass, [list(range(8))],
                                  ins=[xa_h.ap().opt()],
                                  outs=[gg_h.ap().opt()]).then_inc(s_cc.h)
            s_cc.v += 1
            gp.wait_ge(s_cc.h, s_cc.v)
            piece_dmas(gp, pieces[0:11])

        @blk.sync
        def _(sy):
            sy.wait_ge(s_cc.h, s_cc.v)
            piece_dmas(sy, pieces[11:22])

        @blk.scalar
        def _(sc):
            sc.wait_ge(s_cc.h, s_cc.v)
            piece_dmas(sc, pieces[22:32])

        @blk.gpsimd
        def _(gp):
            gp.wait_ge(s_st.h, b_st + 16 * 32)
    s_st.v += 16 * 32


def build_fused(stop_after=None):
    P = Prog("mk")
    nc = P.nc
    xin = nc.dram_tensor("xin", [D, T], F32, kind="ExternalInput").ap()
    xout = nc.dram_tensor("xout", [D, T], F32, kind="ExternalOutput").ap()

    wa0 = {k: nc.dram_tensor(f"a0{k}", s, F32, kind="ExternalInput").ap()
           for k, s in [("wq", [D, 256]), ("wk", [D, 256]),
                        ("wv", [D, D]), ("wo", [D, D])]}
    f0i = {k: nc.dram_tensor(f"f0{k}", s, I32, kind="ExternalInput").ap()
           for k, s in [("wg", [D, 3072]), ("wu", [D, 3072]),
                        ("wd", [3072, D])]}
    ws_b0 = []
    for l in range(4):
        w = {}
        w["wqkv"] = nc.dram_tensor(f"b0wqkv{l}", [D, 3072], F32,
                                   kind="ExternalInput").ap()
        w["wo"] = nc.dram_tensor(f"b0wo{l}", [D, D], F32,
                                 kind="ExternalInput").ap()
        w["w1"] = nc.dram_tensor(f"b0w1{l}", [D, 4096], F32,
                                 kind="ExternalInput").ap()
        w["w2"] = nc.dram_tensor(f"b0w2{l}", [8, 128, 4096], F32,
                                 kind="ExternalInput").ap()
        for k, n in [("bq", KC), ("bv", KC), ("bo", KC), ("b1", 32),
                     ("b2", KC), ("g1", KC), ("be1", KC), ("g2", KC),
                     ("be2", KC)]:
            w[k] = nc.dram_tensor(f"b0{k}{l}", [128, n], F32,
                                  kind="ExternalInput").ap()
        ws_b0.append(w)
    wa1 = {}
    for k in ["wq", "wk", "wv"]:
        wa1[k] = nc.dram_tensor(f"a1{k}", [D, D], F32,
                                kind="ExternalInput").ap()
    for k in ["bq", "bv"]:
        wa1[k] = nc.dram_tensor(f"a1{k}", [128, KC], F32,
                                kind="ExternalInput").ap()
    wf1 = {}
    for k, s in [("w1", [D, 4096]), ("w2", [D, 4096]),
                 ("h1", [32, 128, 4096]), ("h2", [32, 128, 4096]),
                 ("w3", [8, 128, 4096])]:
        wf1[k] = nc.dram_tensor(f"f1{k}", s, F32, kind="ExternalInput").ap()
    for k, n in [("w1s", 32), ("w2s", 32), ("h1s", 32), ("h2s", 32),
                 ("w3s", KC)]:
        wf1[k] = nc.dram_tensor(f"f1{k}", [128, n], F32,
                                kind="ExternalInput").ap()
    wb1 = {}
    for k, s in [("wq", [D, D]), ("wk", [768, D]), ("wv", [768, D]),
                 ("wout", [D, D])]:
        wb1[k] = nc.dram_tensor(f"b1{k}", s, F32, kind="ExternalInput").ap()
    wb1["bout"] = nc.dram_tensor("b1bout", [128, KC], F32,
                                 kind="ExternalInput").ap()
    ctx = nc.dram_tensor("ctx", [768, 4096], F32, kind="ExternalInput").ap()

    xa_h = nc.dram_tensor("xa", [D, T], F32)
    gg_h = nc.dram_tensor("gg", [8 * D, T], F32)
    xb_h = nc.dram_tensor("xb", [D, T], F32)
    xb = xb_h.ap()
    s_ = [nc.dram_tensor(f"strm{i}", [D, T], F32).ap() for i in range(6)]
    wg = nc.dram_tensor("f0wg_dq", [D, 3072], F32).ap()
    wu = nc.dram_tensor("f0wu_dq", [D, 3072], F32).ap()
    wd = nc.dram_tensor("f0wd_dq", [8, 128, 3072], F32).ap()

    def _fin(cur):
        sx = P.sem("fx")
        P.pass_begin("fin")
        with nc.Block() as blk:
            @blk.gpsimd
            def _(gp):
                gp.dma_start(xout, cur).then_inc(sx.h, 16)
                sx.v += 16
                gp.wait_ge(sx.h, sx.v)
        P.pass_end()
        return P

    emit_dequant(P, f0i["wg"], wg, D, 3072, "dqg")
    emit_dequant(P, f0i["wu"], wu, D, 3072, "dqu")
    emit_dequant(P, f0i["wd"], wd, 3072, D, "dqd", out_tiled=True)
    emit_l0_attn_a(P, fm_tiles(xin, D), xin, xa_h.ap(), wa0, "a0")
    if stop_after == "a0":
        return _fin(xa_h.ap())
    emit_repartition(P, xa_h, gg_h, xb_h,
                     no_cc=(stop_after == "rp_nocc"),
                     no_dyn=(stop_after == "rp_nodyn"))
    if stop_after in ("rp", "rp_nocc", "rp_nodyn"):
        return _fin(xb)
    emit_l0_ffn(P, fm_tiles(xb, D), xb, s_[0], wg, wu, wd, "f0a")
    if stop_after == "f0a":
        return _fin(s_[0])
    emit_l0_attn_b(P, fm_tiles(s_[0], D), s_[0], s_[1], ws_b0, "b0")
    if stop_after == "b0":
        return _fin(s_[1])
    emit_l0_ffn(P, fm_tiles(s_[1], D), s_[1], s_[2], wg, wu, wd, "f0b")
    if stop_after == "f0b":
        return _fin(s_[2])
    emit_l1_attn_a(P, fm_tiles(s_[2], D), s_[2], s_[3], wa1, "a1")
    if stop_after == "a1":
        return _fin(s_[3])
    emit_l1_ffn(P, fm_tiles(s_[3], D), s_[3], s_[4], wf1, "f1a")
    if stop_after == "f1a":
        return _fin(s_[4])
    emit_l1_attn_b(P, fm_tiles(s_[4], D), s_[4], s_[5], ctx, wb1, "b1")
    if stop_after == "b1":
        return _fin(s_[5])
    emit_l1_ffn(P, fm_tiles(s_[5], D), s_[5], xout, wf1, "f1b")
    P.pass_end()
    return P


def kernel(**z):
    x = np.asarray(z["L0_attn_a_x"], np.float32)
    P = build_fused()
    im = {}
    im["a0wq"] = np.asarray(z["L0_attn_a_Wq"], np.float32)
    im["a0wk"] = np.asarray(z["L0_attn_a_Wk"], np.float32)
    im["a0wv"] = np.asarray(z["L0_attn_a_Wv"], np.float32)
    im["a0wo"] = np.asarray(z["L0_attn_a_Wo"], np.float32)
    im["f0wg"] = _t(z["L0_ffn_w_gate"])
    im["f0wu"] = _t(z["L0_ffn_w_up"])
    im["f0wd"] = _t(z["L0_ffn_w_down"])
    for l in range(4):
        im[f"b0wqkv{l}"] = _t(z["L0_attn_b_Wqkv"][l])
        im[f"b0wo{l}"] = _t(z["L0_attn_b_Wo"][l])
        im[f"b0w1{l}"] = _t(z["L0_attn_b_W1"][l])
        im[f"b0w2{l}"] = wtile(_t(z["L0_attn_b_W2"][l]))
        im[f"b0bq{l}"] = vec2(z["L0_attn_b_bqkv"][l][0:1024])
        im[f"b0bv{l}"] = vec2(z["L0_attn_b_bqkv"][l][2048:3072])
        im[f"b0bo{l}"] = vec2(z["L0_attn_b_bo"][l])
        im[f"b0b1{l}"] = vec2(z["L0_attn_b_b1"][l])
        im[f"b0b2{l}"] = vec2(z["L0_attn_b_b2"][l])
        im[f"b0g1{l}"] = vec2(z["L0_attn_b_g1"][l])
        im[f"b0be1{l}"] = vec2(z["L0_attn_b_be1"][l])
        im[f"b0g2{l}"] = vec2(z["L0_attn_b_g2"][l])
        im[f"b0be2{l}"] = vec2(z["L0_attn_b_be2"][l])
    im["a1wq"] = _t(z["L1_attn_a_Wq"])
    im["a1wk"] = _t(z["L1_attn_a_Wk"])
    im["a1wv"] = _t(z["L1_attn_a_Wv"])
    im["a1bq"] = vec2(z["L1_attn_a_bq"])
    im["a1bv"] = vec2(z["L1_attn_a_bv"])
    im["f1w1"] = _t(z["L1_ffn_w1_w"])
    im["f1w2"] = _t(z["L1_ffn_w2_w"])
    im["f1h1"] = wtile(_t(z["L1_ffn_h1_w"]))
    im["f1h2"] = wtile(_t(z["L1_ffn_h2_w"]))
    im["f1w3"] = wtile(_t(z["L1_ffn_w3_w"]))
    im["f1w1s"] = vec2(z["L1_ffn_w1_s"])
    im["f1w2s"] = vec2(z["L1_ffn_w2_s"])
    im["f1h1s"] = vec2(z["L1_ffn_h1_s"])
    im["f1h2s"] = vec2(z["L1_ffn_h2_s"])
    im["f1w3s"] = vec2(z["L1_ffn_w3_s"])
    im["b1wq"] = np.asarray(z["L1_attn_b_Wq"], np.float32)
    im["b1wk"] = np.asarray(z["L1_attn_b_Wk"], np.float32)
    im["b1wv"] = np.asarray(z["L1_attn_b_Wv"], np.float32)
    im["b1wout"] = np.asarray(z["L1_attn_b_Wout"], np.float32)
    im["b1bout"] = vec2(z["L1_attn_b_bout"])
    im["ctx"] = _t(np.asarray(z["L1_attn_b_context"], np.float32
                              ).reshape(4096, 768))

    in_maps = []
    for c in range(8):
        xa_c = np.ascontiguousarray(x[4 * c:4 * c + 4].reshape(8192, 1024).T)
        in_maps.append(dict(im, xin=xa_c))
    res = run_bass_kernel_spmd(P.nc, in_maps, core_ids=list(range(8)))

    out = np.empty((32, 2048, 1024), np.float32)
    for c in range(8):
        out[:, 256 * c:256 * (c + 1), :] = (
            res.results[c]["xout"].reshape(1024, 32, 256).transpose(1, 2, 0))
    return out



def _run(P, in_maps):
    return run_bass_kernel_spmd(P.nc, in_maps, core_ids=list(range(8)))


def kernel_unfused(**z):
    x = np.asarray(z["L0_attn_a_x"], np.float32)

    xa = [np.ascontiguousarray(x[4 * c:4 * c + 4].reshape(8192, 1024).T)
          for c in range(8)]

    P1 = build_l0_attn_a_launch()
    r = _run(P1, [{"xin": xa[c], "wq": z["L0_attn_a_Wq"],
                   "wk": z["L0_attn_a_Wk"], "wv": z["L0_attn_a_Wv"],
                   "wo": z["L0_attn_a_Wo"]} for c in range(8)])
    xa = [r.results[c]["xout"] for c in range(8)]

    G = np.concatenate([a.reshape(1024, 4, 2048) for a in xa], axis=1)
    xb = [np.ascontiguousarray(G[:, :, 256 * c:256 * (c + 1)].reshape(1024, 8192))
          for c in range(8)]
    del G, xa

    P2 = build_l0_ffn_launch()
    ffn0 = {"wg": _t(z["L0_ffn_w_gate"]), "wu": _t(z["L0_ffn_w_up"]),
            "wd": _t(z["L0_ffn_w_down"])}
    r = _run(P2, [dict(ffn0, xin=xb[c]) for c in range(8)])
    xb = [r.results[c]["xout"] for c in range(8)]

    P3 = build_l0_attn_b_launch()
    ab = {}
    for l in range(4):
        ab[f"wqkv{l}"] = _t(z["L0_attn_b_Wqkv"][l])
        ab[f"wo{l}"] = _t(z["L0_attn_b_Wo"][l])
        ab[f"w1{l}"] = _t(z["L0_attn_b_W1"][l])
        ab[f"w2{l}"] = _t(z["L0_attn_b_W2"][l])
        ab[f"bq{l}"] = vec2(z["L0_attn_b_bqkv"][l][0:1024])
        ab[f"bv{l}"] = vec2(z["L0_attn_b_bqkv"][l][2048:3072])
        ab[f"bo{l}"] = vec2(z["L0_attn_b_bo"][l])
        ab[f"b1{l}"] = vec2(z["L0_attn_b_b1"][l])
        ab[f"b2{l}"] = vec2(z["L0_attn_b_b2"][l])
        ab[f"g1{l}"] = vec2(z["L0_attn_b_g1"][l])
        ab[f"be1{l}"] = vec2(z["L0_attn_b_be1"][l])
        ab[f"g2{l}"] = vec2(z["L0_attn_b_g2"][l])
        ab[f"be2{l}"] = vec2(z["L0_attn_b_be2"][l])
    r = _run(P3, [dict(ab, xin=xb[c]) for c in range(8)])
    xb = [r.results[c]["xout"] for c in range(8)]

    r = _run(P2, [dict(ffn0, xin=xb[c]) for c in range(8)])
    xb = [r.results[c]["xout"] for c in range(8)]

    P5 = build_l1_attn_a_launch()
    a1 = {"wq": _t(z["L1_attn_a_Wq"]), "wk": _t(z["L1_attn_a_Wk"]),
          "wv": _t(z["L1_attn_a_Wv"]), "bq": vec2(z["L1_attn_a_bq"]),
          "bv": vec2(z["L1_attn_a_bv"])}
    r = _run(P5, [dict(a1, xin=xb[c]) for c in range(8)])
    xb = [r.results[c]["xout"] for c in range(8)]

    P6 = build_l1_ffn_launch()
    f1 = {"w1": _t(z["L1_ffn_w1_w"]), "w2": _t(z["L1_ffn_w2_w"]),
          "h1": _t(z["L1_ffn_h1_w"]), "h2": _t(z["L1_ffn_h2_w"]),
          "w3": _t(z["L1_ffn_w3_w"]),
          "w1s": vec2(z["L1_ffn_w1_s"]), "w2s": vec2(z["L1_ffn_w2_s"]),
          "h1s": vec2(z["L1_ffn_h1_s"]), "h2s": vec2(z["L1_ffn_h2_s"]),
          "w3s": vec2(z["L1_ffn_w3_s"])}
    r = _run(P6, [dict(f1, xin=xb[c]) for c in range(8)])
    xb = [r.results[c]["xout"] for c in range(8)]

    P7 = build_l1_attn_b_launch()
    ctx_fm = _t(np.asarray(z["L1_attn_b_context"], np.float32
                           ).reshape(4096, 768))
    b1_ = {"ctx": ctx_fm, "wq": z["L1_attn_b_Wq"], "wk": z["L1_attn_b_Wk"],
           "wv": z["L1_attn_b_Wv"], "wout": z["L1_attn_b_Wout"],
           "bout": vec2(z["L1_attn_b_bout"])}
    r = _run(P7, [dict(b1_, xin=xb[c]) for c in range(8)])
    xb = [r.results[c]["xout"] for c in range(8)]

    r = _run(P6, [dict(f1, xin=xb[c]) for c in range(8)])
    xb = [r.results[c]["xout"] for c in range(8)]

    out = np.empty((32, 2048, 1024), np.float32)
    for c in range(8):
        out[:, 256 * c:256 * (c + 1), :] = (
            xb[c].reshape(1024, 32, 256).transpose(1, 2, 0))
    return out


# version 05 (final)
# speedup vs baseline: 1.0173x; 1.0173x over previous
import math
import numpy as np
import concourse.bass as bass
import concourse.mybir as mybir
from concourse.bass_utils import run_bass_kernel_spmd

F32 = mybir.dt.float32
F32R = mybir.dt.float32r
BF16 = mybir.dt.bfloat16
I32 = mybir.dt.int32
AF = mybir.ActivationFunctionType
OP = mybir.AluOpType
AX = mybir.AxisListType

D = 1024
T = 8192
WT = 512
NW = T // WT
KC = D // 128
LN_EPS = 1e-5


class Sem:
    __slots__ = ("h", "v")
    def __init__(self, h):
        self.h = h
        self.v = 0


class Ring:
    def __init__(self, P, name, n, step=16):
        self.ss = [P.sem(f"{name}{i}") for i in range(n)]
        self.n = n
        self.step = step
        self.base = [s.v for s in self.ss]
        self.last = -1

    def inc(self, inst, i, amount=16):
        s = self.ss[i % self.n]
        inst.then_inc(s.h, amount)
        s.v += amount
        self.last = max(self.last, i)

    def wait(self, eng, i):
        j = i % self.n
        eng.wait_ge(self.ss[j].h, self.base[j] + self.step * (i // self.n + 1))

    def wait_all(self, eng):
        for j in range(self.n):
            if self.last < j:
                continue
            i = self.last - ((self.last - j) % self.n)
            eng.wait_ge(self.ss[j].h, self.base[j] + self.step * (i // self.n + 1))


class Prog:
    def __init__(self, name="prog"):
        self.nc = bass.Bass("TRN2", target_bir_lowering=False, name=name)
        self.nc._allow_low_precision_reason = "f32r matmul operands by design"
        self._sems = {}
        self._consts = {}
        self._guards = []
        self._tname = "p"
        self._tn = 0

    def sem(self, name):
        if name not in self._sems:
            self._sems[name] = Sem(self.nc.alloc_semaphore(name))
        return self._sems[name]

    def pass_begin(self, name):
        self.pass_end()
        self._tname = name
        self._tn = 0

    def pass_end(self):
        for g in reversed(self._guards):
            g.__exit__(None, None, None)
        self._guards = []

    def alloc(self, cols, dt=F32, parts=128):
        g = self.nc.sbuf_tensor(f"{self._tname}_s{self._tn}", [parts, cols], dt)
        self._tn += 1
        h = g.__enter__()
        self._guards.append(g)
        return h.ap()

    def alloc3(self, c0, c1, dt=F32):
        return self.alloc(c0 * c1, dt).rearrange("p (a b) -> p a b", a=c0)

    def const(self, key, data):
        if key not in self._consts:
            self._consts[key] = self.nc.inline_tensor(
                np.ascontiguousarray(data), name=f"c_{key}").ap()
        return self._consts[key]


def fm_tiles(dram_ap, d_feat):
    return dram_ap.rearrange("(c p) t -> p c t", p=128)


def w_tiles(dram_ap, d_in, d_out):
    return dram_ap.rearrange("(c p) o -> p c o", p=128)


def vec2(v):
    v = np.asarray(v)
    return np.ascontiguousarray(v.reshape(-1, 128).T)


def _t(a):
    return np.ascontiguousarray(np.asarray(a).T)


def wtile(w):
    w = np.asarray(w)
    kci = w.shape[0] // 128
    ocn = w.shape[1] // 128
    return np.ascontiguousarray(
        w.reshape(kci, 128, ocn, 128).transpose(2, 1, 0, 3
                                                ).reshape(ocn, 128, kci * 128))



def emit_ln(P, src_view, xn_dram, t_total=T, gamma=None, beta=None, name="ln"):
    nc = P.nc
    nw = t_total // WT
    xn_v = fm_tiles(xn_dram, D)
    s_pre = P.sem("pre"); s_mm = P.sem("mm"); s_ev = P.sem("ev")
    s_ac = P.sem("ac"); s_sf = P.sem("sf")
    b_pre, b_mm, b_ev, b_ac = s_pre.v, s_mm.v, s_ev.v, s_ac.v
    r_ld = Ring(P, "ldr", 2)
    r_st = Ring(P, "str", 2)

    P.pass_begin(name)
    xw = [P.alloc3(KC, WT) for _ in range(2)]
    sq = P.alloc3(KC, WT)
    xn = [P.alloc3(KC, WT) for _ in range(2)]
    tmp = P.alloc3(KC, WT)
    strip_r = P.alloc(WT * 2)
    strip_f = P.alloc(WT * 3)
    mu_sb = strip_r[0:1, 0:WT]
    rs_sb = strip_r[0:1, WT:2 * WT]
    musq_sb = strip_f[0:1, 0:WT]
    var_sb = strip_f[0:1, WT:2 * WT]
    sd_sb = strip_f[0:1, 2 * WT:3 * WT]
    gb = P.alloc(2 * KC) if gamma is not None else None
    ones_col = P.alloc(1, F32R)
    ones_row_t = P.alloc(128, F32R)
    ones_row = ones_row_t[0:1, :]
    eps_t = P.alloc(1)

    c_ones_col = P.const("ones128x1", np.ones((128, 1), np.float32))
    c_ones_row = P.const("ones1x128", np.ones((1, 128), np.float32))
    c_eps = P.const("eps_ln", np.full((128, 1), LN_EPS, np.float32))
    nld = 3 + (2 if gamma is not None else 0)

    with (
        nc.psum_tensor(f"{name}_pmu", [1, WT], F32) as pmu,
        nc.psum_tensor(f"{name}_pvar", [1, WT], F32) as pvar,
        nc.psum_tensor(f"{name}_pmub", [128, WT], F32) as pmub,
        nc.psum_tensor(f"{name}_prsb", [128, WT], F32) as prsb,
        nc.Block() as blk,
    ):
        @blk.sync
        def _(sy):
            sy.dma_start(ones_col, c_ones_col.bitcast(F32R)).then_inc(s_pre.h, 16)
            sy.dma_start(ones_row, c_ones_row.bitcast(F32R)).then_inc(s_pre.h, 16)
            sy.dma_start(eps_t, c_eps).then_inc(s_pre.h, 16)
            if gamma is not None:
                sy.dma_start(gb[:, 0:KC], gamma).then_inc(s_pre.h, 16)
                sy.dma_start(gb[:, KC:2 * KC], beta).then_inc(s_pre.h, 16)
            for w in range(nw):
                if w >= 2:
                    sy.wait_ge(s_ev.h, b_ev + (w - 2) * 4 + 4)
                d = sy.dma_start(xw[w % 2][:, :, :].bitcast(F32R),
                                 src_view[:, :, w * WT:(w + 1) * WT].bitcast(F32R))
                r_ld.inc(d, w)
        s_pre.v += 16 * nld

        @blk.scalar
        def _(sc):
            sc.wait_ge(s_pre.h, b_pre + 16 * nld)
            for w in range(nw):
                r_ld.wait(sc, w)
                if w >= 1:
                    sc.wait_ge(s_mm.h, b_mm + (w - 1) * 3 + 2)
                sc.activation(sq.rearrange("p a b -> p (a b)").bitcast(F32R),
                              xw[w % 2].rearrange("p a b -> p (a b)"),
                              AF.Square).then_inc(s_ac.h, 1)
                sc.wait_ge(s_ev.h, b_ev + w * 4 + 2)
                sc.activation(sd_sb, var_sb, AF.Sqrt,
                              bias=eps_t[0:1, :]).then_inc(s_ac.h, 1)
        s_ac.v += 2 * nw

        @blk.tensor
        def _(te):
            te.wait_ge(s_pre.h, b_pre + 16 * nld)
            for w in range(nw):
                r_ld.wait(te, w)
                if w >= 1:
                    te.wait_ge(s_ev.h, b_ev + (w - 1) * 4 + 1)
                for c in range(KC):
                    mm = te.matmul(pmu[:], ones_col,
                                   xw[w % 2][:, c, :].bitcast(F32R),
                                   start=(c == 0), stop=(c == KC - 1))
                mm.then_inc(s_mm.h, 1)
                te.wait_ge(s_ac.h, b_ac + 2 * w + 1)
                if w >= 1:
                    te.wait_ge(s_ev.h, b_ev + (w - 1) * 4 + 2)
                for c in range(KC):
                    mm = te.matmul(pvar[:], ones_col, sq[:, c, :].bitcast(F32R),
                                   start=(c == 0), stop=(c == KC - 1))
                mm.then_inc(s_mm.h, 1)
                if w >= 1:
                    te.wait_ge(s_ev.h, b_ev + (w - 1) * 4 + 4)
                te.wait_ge(s_ev.h, b_ev + w * 4 + 1)
                te.matmul(pmub[:], ones_row, mu_sb.bitcast(F32R),
                          start=True, stop=True)
                te.wait_ge(s_ev.h, b_ev + w * 4 + 3)
                mm = te.matmul(prsb[:], ones_row, rs_sb.bitcast(F32R),
                               start=True, stop=True)
                mm.then_inc(s_mm.h, 1)
        s_mm.v += 3 * nw

        @blk.vector
        def _(ve):
            for w in range(nw):
                ve.wait_ge(s_mm.h, b_mm + w * 3 + 1)
                ve.tensor_scalar_mul(mu_sb.bitcast(F32R), pmu[:],
                                     1.0 / D).then_inc(s_ev.h, 1)
                ve.wait_ge(s_ev.h, b_ev + w * 4 + 1)
                ve.tensor_mul(musq_sb, mu_sb, mu_sb).then_inc(s_sf.h, 1)
                s_sf.v += 1
                ve.wait_ge(s_mm.h, b_mm + w * 3 + 2)
                ve.wait_ge(s_sf.h, s_sf.v)
                ve.scalar_tensor_tensor(var_sb, pvar[:], 1.0 / D, musq_sb,
                                        OP.mult, OP.subtract).then_inc(s_ev.h, 1)
                ve.wait_ge(s_ac.h, b_ac + 2 * w + 2)
                ve.reciprocal(rs_sb.bitcast(F32R), sd_sb).then_inc(s_ev.h, 1)
                ve.wait_ge(s_mm.h, b_mm + w * 3 + 3)
                if w >= 2:
                    r_st.wait(ve, w - 2)
                for c in range(KC):
                    ve.tensor_sub(tmp[:, c, :], xw[w % 2][:, c, :],
                                  pmub[:]).then_inc(s_sf.h, 1)
                    s_sf.v += 1
                    ve.wait_ge(s_sf.h, s_sf.v)
                    if gamma is None:
                        mmv = ve.tensor_mul(xn[w % 2][:, c, :].bitcast(F32R),
                                            tmp[:, c, :], prsb[:])
                    else:
                        ve.tensor_mul(tmp[:, c, :], tmp[:, c, :],
                                      prsb[:]).then_inc(s_sf.h, 1)
                        s_sf.v += 1
                        ve.wait_ge(s_sf.h, s_sf.v)
                        mmv = ve.tensor_scalar(xn[w % 2][:, c, :].bitcast(F32R),
                                               tmp[:, c, :], gb[:, c:c + 1],
                                               gb[:, KC + c:KC + c + 1],
                                               OP.mult, OP.add)
                    if c == KC - 1:
                        mmv.then_inc(s_ev.h, 1)
        s_ev.v += 4 * nw

        @blk.gpsimd
        def _(gp):
            for w in range(nw):
                gp.wait_ge(s_ev.h, b_ev + w * 4 + 4)
                d = gp.dma_start(xn_v[:, :, w * WT:(w + 1) * WT],
                                 xn[w % 2][:, :, :])
                r_st.inc(d, w)
            r_st.wait_all(gp)



def emit_proj(P, src_dram, w_dram, d_in, d_out, name,
              out_dram=None, t_total=T,
              epi="f32r", bias=None, scale=None, act=None, scale_const=None,
              old_dram=None, w2_dram=None, scale2=None, out_bf16=False):
    nc = P.nc
    nw = t_total // WT
    kci = d_in // 128
    oc_n = d_out // 128
    nch = max(1, d_out // 512)
    chw = d_out // nch
    src_v = fm_tiles(src_dram, d_in)
    out_v = fm_tiles(out_dram, d_out)
    old_v = fm_tiles(old_dram, d_out) if old_dram is not None else None
    w_list = [w_dram] + ([w2_dram] if w2_dram is not None else [])
    ncht = nch * len(w_list)

    s_wl = P.sem("wl"); s_mm = P.sem("mm"); s_ev = P.sem("ev"); s_ac = P.sem("ac")
    s_wc = P.sem("wc"); s_xc = P.sem("xc")
    b_wl, b_mm, b_ev, b_ac = s_wl.v, s_mm.v, s_ev.v, s_ac.v
    b_wc, b_xc = s_wc.v, s_xc.v
    r_ld = Ring(P, "ldr", 2, step=16 * (2 if old_dram is not None else 1))
    r_ws = Ring(P, "wsr", 2)
    NOB = 4
    r_st = Ring(P, "str", NOB)

    P.pass_begin(name)
    wbf = [P.alloc(kci * d_out, BF16) for _ in w_list]
    wb4 = [b.rearrange("p (j a b) -> p j a b", j=nch, a=kci) for b in wbf]
    wst = [P.alloc(kci * chw) for _ in range(2)]
    xw32f = [P.alloc(kci * WT) for _ in range(2)]
    xw32 = [b.rearrange("p (a b) -> p a b", a=kci) for b in xw32f]
    xwbf = [P.alloc(kci * WT, BF16) for _ in range(2)]
    xwb = [b.rearrange("p (a b) -> p a b", a=kci) for b in xwbf]
    oldw = [P.alloc3(oc_n, WT) for _ in range(2)] if old_dram is not None else None
    ob = [P.alloc(WT, BF16 if out_bf16 else F32) for _ in range(NOB)]
    sg = [P.alloc(WT) for _ in range(2)] if epi == "gateup" else None
    vecs = P.alloc(3 * oc_n)
    bias_t = vecs[:, 0:oc_n]
    scale_t = vecs[:, oc_n:2 * oc_n]
    scale2_t = vecs[:, 2 * oc_n:3 * oc_n]
    NPB = 4

    nwl = ((1 if bias is not None else 0) + (1 if scale is not None else 0)
           + (1 if scale2 is not None else 0))
    mm_per_g = 2 if w2_dram is not None else 1
    use_affine = (epi == "f32r" and act is None
                  and (bias is not None or scale is not None))
    cast_on_vector = (epi == "f32r" and not use_affine)

    def cast_sc(eng, dst, srcb, sem):
        eng.activation(dst, srcb, AF.Copy).then_inc(sem.h, 1)

    def cast_ve(eng, dst, srcb, sem):
        eng.tensor_copy(dst, srcb).then_inc(sem.h, 1)

    with (
        nc.psum_tensor(f"{name}_pb", [128, NPB, WT], F32) as pb,
        nc.Block() as blk,
    ):
        if nwl:
            @blk.scalar
            def _(sc):
                if bias is not None:
                    sc.dma_start(bias_t, bias).then_inc(s_wl.h, 16)
                if scale is not None:
                    sc.dma_start(scale_t, scale).then_inc(s_wl.h, 16)
                if scale2 is not None:
                    sc.dma_start(scale2_t, scale2).then_inc(s_wl.h, 16)
            s_wl.v += 16 * nwl

        def wcast(eng, castf):
            q = 0
            for wi in range(len(w_list)):
                for j in range(nch):
                    r_ws.wait(eng, q)
                    castf(eng, wbf[wi][:, j * kci * chw:(j + 1) * kci * chw],
                          wst[q % 2], s_wc)
                    q += 1

        def xcast(eng, castf, w):
            r_ld.wait(eng, w)
            if w >= 2:
                eng.wait_ge(s_mm.h, b_mm + (w - 1) * oc_n * mm_per_g)
            castf(eng, xwbf[w % 2], xw32f[w % 2], s_xc)

        def all_casts(eng, castf):
            wcast(eng, castf)
            if epi != "gateup":
                for w in range(nw):
                    xcast(eng, castf, w)

        if cast_on_vector:
            @blk.vector
            def _(ve):
                all_casts(ve, cast_ve)
        else:
            @blk.scalar
            def _(sc):
                all_casts(sc, cast_sc)
        s_wc.v += ncht
        if epi != "gateup":
            s_xc.v += nw

        @blk.sync
        def _(sy):
            q = 0
            for wi, wsrc in enumerate(w_list):
                wv = w_tiles(wsrc, d_in, d_out)
                for j in range(nch):
                    if q >= 2:
                        sy.wait_ge(s_wc.h, b_wc + q - 1)
                    d = sy.dma_start(
                        wst[q % 2].rearrange("p (a b) -> p a b", a=kci),
                        wv[:, :, j * chw:(j + 1) * chw])
                    r_ws.inc(d, q)
                    q += 1
            for w in range(nw):
                if w >= 2:
                    sy.wait_ge(s_xc.h, b_xc + (w - 1))
                    if old_dram is not None:
                        sy.wait_ge(s_ev.h, b_ev + (w - 1) * oc_n)
                d = sy.dma_start(xw32[w % 2][:, :, :],
                                 src_v[:, :, w * WT:(w + 1) * WT])
                r_ld.inc(d, w)
                if old_dram is not None:
                    d = sy.dma_start(oldw[w % 2][:, :, :],
                                     old_v[:, :, w * WT:(w + 1) * WT])
                    r_ld.inc(d, w)

        @blk.tensor
        def _(te):
            te.wait_ge(s_wc.h, b_wc + ncht)
            g = 0
            for w in range(nw):
                te.wait_ge(s_xc.h, b_xc + w + 1)
                for oc in range(oc_n):
                    jj = (oc * 128) // chw
                    col = (oc * 128) % chw
                    if w2_dram is None:
                        bank = g % NPB
                        if g >= NPB:
                            te.wait_ge(s_ev.h, b_ev + g - NPB + 1)
                        for kc in range(kci):
                            mm = te.matmul(pb[:, bank, :],
                                           wb4[0][:, jj, kc, col:col + 128],
                                           xwb[w % 2][:, kc, :],
                                           start=(kc == 0), stop=(kc == kci - 1))
                        mm.then_inc(s_mm.h, 1)
                    else:
                        bank = (g % 2) * 2
                        if g >= 2:
                            te.wait_ge(s_ev.h, b_ev + g - 1)
                        for kc in range(kci):
                            mm = te.matmul(pb[:, bank, :],
                                           wb4[0][:, jj, kc, col:col + 128],
                                           xwb[w % 2][:, kc, :],
                                           start=(kc == 0), stop=(kc == kci - 1))
                        mm.then_inc(s_mm.h, 1)
                        for kc in range(kci):
                            mm = te.matmul(pb[:, bank + 1, :],
                                           wb4[1][:, jj, kc, col:col + 128],
                                           xwb[w % 2][:, kc, :],
                                           start=(kc == 0), stop=(kc == kci - 1))
                        mm.then_inc(s_mm.h, 1)
                    g += 1
        s_mm.v += nw * oc_n * mm_per_g

        if epi == "gateup":
            @blk.scalar
            def _(sc):
                xcast(sc, cast_sc, 0)
                g = 0
                for w in range(nw):
                    if w + 1 < nw:
                        xcast(sc, cast_sc, w + 1)
                    for oc in range(oc_n):
                        bank = (g % 2) * 2
                        sc.wait_ge(s_mm.h, b_mm + g * 2 + 1)
                        if g >= 2:
                            sc.wait_ge(s_ev.h, b_ev + g - 1)
                        kw = {}
                        if scale is not None:
                            kw["scale"] = scale_t[:, oc:oc + 1]
                        sc.activation(sg[g % 2], pb[:, bank, :], AF.Silu,
                                      **kw).then_inc(s_ac.h, 1)
                        g += 1
            s_ac.v += nw * oc_n
            s_xc.v += nw

            @blk.vector
            def _(ve):
                g = 0
                for w in range(nw):
                    for oc in range(oc_n):
                        bank = (g % 2) * 2
                        ve.wait_ge(s_mm.h, b_mm + g * 2 + 2)
                        ve.wait_ge(s_ac.h, b_ac + g + 1)
                        if g >= NOB:
                            r_st.wait(ve, g - NOB)
                        if scale2 is not None:
                            ve.scalar_tensor_tensor(
                                ob[g % NOB], pb[:, bank + 1, :],
                                scale2_t[:, oc:oc + 1], sg[g % 2],
                                OP.mult, OP.mult).then_inc(s_ev.h, 1)
                        else:
                            ve.tensor_mul(ob[g % NOB], pb[:, bank + 1, :],
                                          sg[g % 2]).then_inc(s_ev.h, 1)
                        g += 1
            s_ev.v += nw * oc_n
        elif epi == "residual":
            @blk.vector
            def _(ve):
                if nwl:
                    ve.wait_ge(s_wl.h, b_wl + 16 * nwl)
                g = 0
                for w in range(nw):
                    for oc in range(oc_n):
                        bank = g % NPB
                        ve.wait_ge(s_mm.h, b_mm + g + 1)
                        r_ld.wait(ve, w)
                        if g >= NOB:
                            r_st.wait(ve, g - NOB)
                        if scale is not None:
                            ve.scalar_tensor_tensor(
                                ob[g % NOB], pb[:, bank, :],
                                scale_t[:, oc:oc + 1], oldw[w % 2][:, oc, :],
                                OP.mult, OP.add).then_inc(s_ev.h, 1)
                        elif bias is not None:
                            ve.scalar_tensor_tensor(
                                ob[g % NOB], pb[:, bank, :],
                                bias_t[:, oc:oc + 1], oldw[w % 2][:, oc, :],
                                OP.add, OP.add).then_inc(s_ev.h, 1)
                        else:
                            ve.tensor_add(ob[g % NOB], pb[:, bank, :],
                                          oldw[w % 2][:, oc, :]).then_inc(s_ev.h, 1)
                        g += 1
            s_ev.v += nw * oc_n
        elif use_affine:
            @blk.vector
            def _(ve):
                if nwl:
                    ve.wait_ge(s_wl.h, b_wl + 16 * nwl)
                g = 0
                for w in range(nw):
                    for oc in range(oc_n):
                        bank = g % NPB
                        ve.wait_ge(s_mm.h, b_mm + g + 1)
                        if g >= NOB:
                            r_st.wait(ve, g - NOB)
                        s1 = scale_t[:, oc:oc + 1] if scale is not None else 1.0
                        s2 = bias_t[:, oc:oc + 1] if bias is not None else 0.0
                        ve.tensor_scalar(ob[g % NOB], pb[:, bank, :], s1, s2,
                                         OP.mult, OP.add).then_inc(s_ev.h, 1)
                        g += 1
            s_ev.v += nw * oc_n
        else:
            @blk.scalar
            def _(sc):
                g = 0
                for w in range(nw):
                    for oc in range(oc_n):
                        bank = g % NPB
                        sc.wait_ge(s_mm.h, b_mm + g + 1)
                        if g >= NOB:
                            r_st.wait(sc, g - NOB)
                        kw = {}
                        if bias is not None:
                            kw["bias"] = bias_t[:, oc:oc + 1]
                        if scale is not None:
                            kw["scale"] = scale_t[:, oc:oc + 1]
                        elif scale_const is not None:
                            kw["scale"] = float(scale_const)
                        fn = act if act is not None else AF.Copy
                        sc.activation(ob[g % NOB], pb[:, bank, :],
                                      fn, **kw).then_inc(s_ev.h, 1)
                        g += 1
            s_ev.v += nw * oc_n

        @blk.gpsimd
        def _(gp):
            g = 0
            for w in range(nw):
                for oc in range(oc_n):
                    gp.wait_ge(s_ev.h, b_ev + g + 1)
                    d = gp.dma_start(out_v[:, oc, w * WT:(w + 1) * WT],
                                     ob[g % NOB][:, :])
                    r_st.inc(d, g)
                    g += 1
            r_st.wait_all(gp)



def emit_dequant(P, wint_dram, wout_dram, rows, cols, name="dq",
                 out_tiled=False):
    nc = P.nc
    rc = rows // 128
    s_ac = P.sem("ac")
    b_ac = s_ac.v
    r_ld = Ring(P, "ldr", 2)
    r_st = Ring(P, "str", 2, step=16 * (cols // 128 if out_tiled else 1))
    P.pass_begin(name)
    nseg = 1
    while (rc // nseg) * cols > 4096 or rc % nseg:
        nseg += 1
        if nseg > rc:
            nseg = rc
            break
    seg = rc // nseg
    ti = [P.alloc3(seg, cols, I32) for _ in range(2)]
    to = [P.alloc3(seg, cols) for _ in range(2)]
    wi_v = w_tiles(wint_dram, rows, cols)
    wo_v = None if out_tiled else w_tiles(wout_dram, rows, cols)
    ocn = cols // 128
    with nc.Block() as blk:
        @blk.sync
        def _(sy):
            for s in range(nseg):
                if s >= 2:
                    sy.wait_ge(s_ac.h, b_ac + s - 1)
                d = sy.dma_start(ti[s % 2][:, :, :],
                                 wi_v[:, s * seg:(s + 1) * seg, :])
                r_ld.inc(d, s)

        @blk.scalar
        def _(sc):
            for s in range(nseg):
                r_ld.wait(sc, s)
                if s >= 2:
                    r_st.wait(sc, s - 2)
                sc.activation(to[s % 2].rearrange("p a b -> p (a b)"),
                              ti[s % 2].rearrange("p a b -> p (a b)"),
                              AF.Copy, scale=1.0 / 127.0).then_inc(s_ac.h, 1)
        s_ac.v += nseg

        @blk.gpsimd
        def _(gp):
            for s in range(nseg):
                gp.wait_ge(s_ac.h, b_ac + s + 1)
                if out_tiled:
                    for oc in range(ocn):
                        dst = bass.AP(
                            wout_dram.tensor,
                            oc * 128 * rc * 128 + s * seg * 128,
                            [[rc * 128, 128], [128, seg], [1, 128]])
                        d = gp.dma_start(
                            dst, to[s % 2].rearrange(
                                "p a (o m) -> p a o m", m=128)[:, :, oc, :])
                        r_st.inc(d, s, amount=0 if False else 16)
                else:
                    d = gp.dma_start(wo_v[:, s * seg:(s + 1) * seg, :],
                                     to[s % 2][:, :, :])
                    r_st.inc(d, s)
            r_st.wait_all(gp)



def emit_bigmm(P, src_dram, w_dram, d_in, d_out, name, out_dram,
               epi="f32r", act=None, scale=None, bias=None, old_dram=None,
               wp_windows=2, t_total=T, w_tiled=True,
               src_bf16=True, out_bf16=False):
    nc = P.nc
    assert src_bf16, "bigmm sources are bf16 streams"
    WP = wp_windows * WT
    np_ = t_total // WP
    kci = d_in // 128
    oc_n = d_out // 128
    ngw = np_ * oc_n
    src_v = fm_tiles(src_dram, d_in)
    out_v = fm_tiles(out_dram, d_out)
    old_v = fm_tiles(old_dram, d_out) if old_dram is not None else None
    w_v = None if w_tiled else w_tiles(w_dram, d_in, d_out)

    s_pre = P.sem("pre"); s_mm = P.sem("mm"); s_ev = P.sem("ev")
    s_wc = P.sem("wc")
    b_pre, b_mm, b_ev, b_wc = s_pre.v, s_mm.v, s_ev.v, s_wc.v
    nold = 1 if old_dram is not None else 0
    r_sl = Ring(P, "slr", 1, step=16 * (kci + nold))
    r_wt = Ring(P, "wtr", 2)
    NOB = 4
    r_st = Ring(P, "str", NOB)

    P.pass_begin(name)
    hsl = P.alloc3(kci, WP, BF16)
    wt32 = [P.alloc(kci * 128) for _ in range(2)]
    wtbf = [P.alloc(kci * 128, BF16) for _ in range(2)]
    wtb = [b.rearrange("p (a b) -> p a b", a=kci) for b in wtbf]
    oldw = P.alloc3(oc_n, WP) if old_dram is not None else None
    ob = [P.alloc(WT, BF16 if out_bf16 else F32) for _ in range(NOB)]
    vecs = P.alloc(2 * oc_n)
    scale_t = vecs[:, 0:oc_n]
    bias_t = vecs[:, oc_n:2 * oc_n]
    nsc = (1 if scale is not None else 0) + (1 if bias is not None else 0)

    def cast_sc(eng, dst, srcb):
        eng.activation(dst, srcb, AF.Copy).then_inc(s_wc.h, 1)

    def cast_ve(eng, dst, srcb):
        eng.tensor_copy(dst, srcb).then_inc(s_wc.h, 1)

    with (
        nc.psum_tensor(f"{name}_pb", [128, 4, WT], F32) as pb,
        nc.Block() as blk,
    ):
        @blk.sync
        def _(sy):
            if scale is not None:
                sy.dma_start(scale_t, scale).then_inc(s_pre.h, 16)
            if bias is not None:
                sy.dma_start(bias_t, bias).then_inc(s_pre.h, 16)
            gw = 0
            for p in range(np_):
                if p >= 1:
                    sy.wait_ge(s_mm.h, b_mm + p * oc_n * wp_windows)
                    if old_dram is not None:
                        sy.wait_ge(s_ev.h, b_ev + p * oc_n * wp_windows)
                for kc in range(kci):
                    d = sy.dma_start(hsl[:, kc, :],
                                     src_v[:, kc, p * WP:(p + 1) * WP])
                    r_sl.inc(d, p)
                if old_dram is not None:
                    d = sy.dma_start(oldw[:, :, :],
                                     old_v[:, :, p * WP:(p + 1) * WP])
                    r_sl.inc(d, p)
                for oc in range(oc_n):
                    if gw >= 2:
                        sy.wait_ge(s_wc.h, b_wc + gw - 1)
                    wsrc = (w_dram[oc, :, :] if w_tiled
                            else w_v[:, :, oc * 128:(oc + 1) * 128])
                    d = sy.dma_start(wt32[gw % 2] if w_tiled
                                     else wt32[gw % 2].rearrange(
                                         "p (a b) -> p a b", a=kci), wsrc)
                    r_wt.inc(d, gw)
                    gw += 1
        s_pre.v += 16 * nsc

        def wtcast(eng, castf):
            for gw in range(ngw):
                r_wt.wait(eng, gw)
                if gw >= 2:
                    eng.wait_ge(s_mm.h, b_mm + (gw - 1) * wp_windows)
                castf(eng, wtbf[gw % 2], wt32[gw % 2])

        if epi == "residual":
            @blk.scalar
            def _(sc):
                wtcast(sc, cast_sc)
        else:
            @blk.vector
            def _(ve):
                wtcast(ve, cast_ve)
        s_wc.v += ngw

        @blk.tensor
        def _(te):
            gw = 0
            for p in range(np_):
                r_sl.wait(te, p)
                for oc in range(oc_n):
                    te.wait_ge(s_wc.h, b_wc + gw + 1)
                    for h in range(wp_windows):
                        bank = (gw % 2) * 2 + h
                        if gw >= 2:
                            te.wait_ge(s_ev.h,
                                       b_ev + (gw - 2) * wp_windows + h + 1)
                        for kc in range(kci):
                            mm = te.matmul(pb[:, bank, :],
                                           wtb[gw % 2][:, kc, :],
                                           hsl[:, kc, h * WT:(h + 1) * WT],
                                           start=(kc == 0), stop=(kc == kci - 1))
                        mm.then_inc(s_mm.h, 1)
                    gw += 1
        s_mm.v += np_ * oc_n * wp_windows

        if epi == "residual":
            @blk.vector
            def _(ve):
                if nsc:
                    ve.wait_ge(s_pre.h, b_pre + 16 * nsc)
                g = 0
                for p in range(np_):
                    for oc in range(oc_n):
                        for h in range(wp_windows):
                            bank = (g % 2) * 2 + h
                            gi = g * wp_windows + h
                            ve.wait_ge(s_mm.h, b_mm + gi + 1)
                            r_sl.wait(ve, p)
                            if gi >= NOB:
                                r_st.wait(ve, gi - NOB)
                            oslice = oldw[:, oc, h * WT:(h + 1) * WT]
                            if scale is not None:
                                ve.scalar_tensor_tensor(
                                    ob[gi % NOB], pb[:, bank, :],
                                    scale_t[:, oc:oc + 1], oslice,
                                    OP.mult, OP.add).then_inc(s_ev.h, 1)
                            elif bias is not None:
                                ve.scalar_tensor_tensor(
                                    ob[gi % NOB], pb[:, bank, :],
                                    bias_t[:, oc:oc + 1], oslice,
                                    OP.add, OP.add).then_inc(s_ev.h, 1)
                            else:
                                ve.tensor_add(ob[gi % NOB], pb[:, bank, :],
                                              oslice).then_inc(s_ev.h, 1)
                        g += 1
            s_ev.v += np_ * oc_n * wp_windows
        else:
            @blk.scalar
            def _(sc):
                if nsc:
                    sc.wait_ge(s_pre.h, b_pre + 16 * nsc)
                g = 0
                for p in range(np_):
                    for oc in range(oc_n):
                        for h in range(wp_windows):
                            bank = (g % 2) * 2 + h
                            gi = g * wp_windows + h
                            sc.wait_ge(s_mm.h, b_mm + gi + 1)
                            if gi >= NOB:
                                r_st.wait(sc, gi - NOB)
                            kw = {}
                            if scale is not None:
                                kw["scale"] = scale_t[:, oc:oc + 1]
                            if bias is not None:
                                kw["bias"] = bias_t[:, oc:oc + 1]
                            fn = act if act is not None else AF.Copy
                            sc.activation(ob[gi % NOB], pb[:, bank, :], fn,
                                          **kw).then_inc(s_ev.h, 1)
                        g += 1
            s_ev.v += np_ * oc_n * wp_windows

        @blk.gpsimd
        def _(gp):
            gi = 0
            for p in range(np_):
                for oc in range(oc_n):
                    for h in range(wp_windows):
                        gp.wait_ge(s_ev.h, b_ev + gi + 1)
                        d = gp.dma_start(
                            out_v[:, oc, p * WP + h * WT:p * WP + (h + 1) * WT],
                            ob[gi % NOB][:, :])
                        r_st.inc(d, gi)
                        gi += 1
            r_st.wait_all(gp)



def emit_proj_tm(P, src_dram, w_dram, d_in, d_out, name, out_dram, t_total=T):
    nc = P.nc
    nw = t_total // WT
    kci = d_in // 128
    ow_n = d_out // WT
    src_v = fm_tiles(src_dram, d_in)

    s_wl = P.sem("wl"); s_mm = P.sem("mm"); s_ev = P.sem("ev")
    b_wl, b_mm, b_ev = s_wl.v, s_mm.v, s_ev.v
    r_ld = Ring(P, "ldr", 2)
    NOB = 4
    r_st = Ring(P, "str", NOB)

    P.pass_begin(name)
    wsl = P.alloc3(kci, d_out, F32R)
    xw = [P.alloc3(kci, WT, F32R) for _ in range(2)]
    ob = [P.alloc(WT) for _ in range(NOB)]
    out_t = out_dram.rearrange("(w g p) e -> w g p e", g=4, p=128)
    NPB = 4
    gr_per_w = 4 * ow_n

    with (
        nc.psum_tensor(f"{name}_pb", [128, NPB, WT], F32) as pb,
        nc.Block() as blk,
    ):
        @blk.scalar
        def _(sc):
            sc.dma_start(wsl[:, :, :],
                         w_tiles(w_dram, d_in, d_out).bitcast(F32R)
                         ).then_inc(s_wl.h, 16)
        s_wl.v += 16

        @blk.sync
        def _(sy):
            for w in range(nw):
                if w >= 2:
                    sy.wait_ge(s_mm.h, b_mm + (w - 1) * gr_per_w)
                d = sy.dma_start(xw[w % 2][:, :, :],
                                 src_v[:, :, w * WT:(w + 1) * WT].bitcast(F32R))
                r_ld.inc(d, w)

        @blk.tensor
        def _(te):
            te.wait_ge(s_wl.h, b_wl + 16)
            g = 0
            for w in range(nw):
                r_ld.wait(te, w)
                for tc in range(4):
                    for ow in range(ow_n):
                        bank = g % NPB
                        if g >= NPB:
                            te.wait_ge(s_ev.h, b_ev + g - NPB + 1)
                        for kc in range(kci):
                            mm = te.matmul(
                                pb[:, bank, :],
                                xw[w % 2][:, kc, tc * 128:(tc + 1) * 128],
                                wsl[:, kc, ow * WT:(ow + 1) * WT],
                                start=(kc == 0), stop=(kc == kci - 1))
                        mm.then_inc(s_mm.h, 1)
                        g += 1
        s_mm.v += nw * gr_per_w

        @blk.vector
        def _(ve):
            g = 0
            for w in range(nw):
                for tc in range(4):
                    for ow in range(ow_n):
                        ve.wait_ge(s_mm.h, b_mm + g + 1)
                        if g >= NOB:
                            r_st.wait(ve, g - NOB)
                        ve.tensor_copy(ob[g % NOB], pb[:, g % NPB, :]
                                       ).then_inc(s_ev.h, 1)
                        g += 1
        s_ev.v += nw * gr_per_w

        @blk.gpsimd
        def _(gp):
            g = 0
            for w in range(nw):
                for tc in range(4):
                    for ow in range(ow_n):
                        gp.wait_ge(s_ev.h, b_ev + g + 1)
                        d = gp.dma_start(out_t[w, tc, :, ow * WT:(ow + 1) * WT],
                                         ob[g % NOB][:, :])
                        r_st.inc(d, g)
                        g += 1
            r_st.wait_all(gp)



def emit_l1_attn_a_attention(P, q_dram, k_dram, v_dram, bv, x_old_dram,
                             x_out_dram, name):
    nc = P.nc
    q_v = fm_tiles(q_dram, D)
    k_v = fm_tiles(k_dram, D)
    v_t = v_dram.rearrange("(w g p) e -> w p g e", g=4, p=128)
    old_v = fm_tiles(x_old_dram, D)
    out_v = fm_tiles(x_out_dram, D)

    s_pre = P.sem("pre"); s_mm = P.sem("mm"); s_ev = P.sem("ev")
    s_ac = P.sem("ac"); s_tp = P.sem("tp")
    b_pre, b_mm, b_ev, b_ac, b_tp = s_pre.v, s_mm.v, s_ev.v, s_ac.v, s_tp.v
    r_ld = Ring(P, "ldr", 2, step=16 * 4)
    r_st = Ring(P, "str", 2)

    P.pass_begin(name)
    qw = [P.alloc3(KC, WT, F32R) for _ in range(2)]
    kw = [P.alloc3(KC, WT, F32R) for _ in range(2)]
    vw = [P.alloc3(4, D, F32R) for _ in range(2)]
    ow_ = [P.alloc3(KC, WT) for _ in range(2)]
    E = [P.alloc(128) for _ in range(2)]
    ET = [P.alloc(128, F32R) for _ in range(2)]
    ob = [P.alloc3(KC, 128) for _ in range(2)]
    sm = P.alloc(4)
    ident = P.alloc(128)
    bvv = P.alloc(KC)
    c_id = P.const("ident128", np.eye(128, dtype=np.float32))

    with (
        nc.psum_tensor(f"{name}_ps", [128, WT], F32) as ps,
        nc.psum_tensor(f"{name}_pt", [128, 128], F32) as pt,
        nc.psum_tensor(f"{name}_po", [128, 4, WT], F32) as po,
        nc.Block() as blk,
    ):
        @blk.sync
        def _(sy):
            sy.dma_start(ident, c_id).then_inc(s_pre.h, 16)
            sy.dma_start(bvv, bv).then_inc(s_pre.h, 16)
            for w in range(NW):
                if w >= 2:
                    sy.wait_ge(s_ev.h, b_ev + ((w - 2) * 4 + 3) * 5 + 5)
                d = sy.dma_start(qw[w % 2][:, :, :],
                                 q_v[:, :, w * WT:(w + 1) * WT].bitcast(F32R))
                r_ld.inc(d, w)
                d = sy.dma_start(kw[w % 2][:, :, :],
                                 k_v[:, :, w * WT:(w + 1) * WT].bitcast(F32R))
                r_ld.inc(d, w)
                d = sy.dma_start(vw[w % 2][:, :, :],
                                 v_t[w, :, :, :].bitcast(F32R))
                r_ld.inc(d, w)
                d = sy.dma_start(ow_[w % 2][:, :, :],
                                 old_v[:, :, w * WT:(w + 1) * WT])
                r_ld.inc(d, w)
        s_pre.v += 32

        @blk.tensor
        def _(te):
            te.wait_ge(s_pre.h, b_pre + 32)
            u = 0
            for w in range(NW):
                r_ld.wait(te, w)
                for tc in range(4):
                    if u >= 1:
                        te.wait_ge(s_ev.h, b_ev + (u - 1) * 5 + 4)
                        te.wait_ge(s_ac.h, b_ac + u)
                    for kc in range(KC):
                        mm = te.matmul(ps[:, 0:128],
                                       qw[w % 2][:, kc, tc * 128:(tc + 1) * 128],
                                       kw[w % 2][:, kc, tc * 128:(tc + 1) * 128],
                                       start=(kc == 0), stop=(kc == KC - 1))
                    mm.then_inc(s_mm.h, 1)
                    te.wait_ge(s_ev.h, b_ev + u * 5 + 3)
                    if u >= 1:
                        te.wait_ge(s_ev.h, b_ev + (u - 1) * 5 + 4)
                    te.transpose(pt[:], E[u % 2], ident).then_inc(s_tp.h, 1)
                    te.wait_ge(s_ev.h, b_ev + u * 5 + 4)
                    if u >= 2:
                        te.wait_ge(s_ev.h, b_ev + (u - 2) * 5 + 5)
                    for g4 in range(2):
                        bk = (u % 2) * 2 + g4
                        for oc in range(4):
                            mm = te.matmul(
                                po[:, bk, oc * 128:(oc + 1) * 128],
                                vw[w % 2][:, tc, (g4 * 4 + oc) * 128:(g4 * 4 + oc + 1) * 128],
                                ET[u % 2],
                                start=True, stop=True)
                        mm.then_inc(s_mm.h, 1)
                    u += 1
        s_mm.v += 64 * 3

        @blk.vector
        def _(ve):
            u = 0
            for w in range(NW):
                for tc in range(4):
                    ve.wait_ge(s_mm.h, b_mm + u * 3 + 1)
                    ve.reduce_max(sm[:, 0:1], ps[:, 0:128], axis=AX.X,
                                  negate=True).then_inc(s_ev.h, 1)
                    ve.wait_ge(s_ac.h, b_ac + u + 1)
                    ve.reciprocal(sm[:, 2:3], sm[:, 1:2]).then_inc(s_ev.h, 1)
                    ve.wait_ge(s_ev.h, b_ev + u * 5 + 2)
                    ve.tensor_scalar_mul(E[u % 2], E[u % 2],
                                         sm[:, 2:3]).then_inc(s_ev.h, 1)
                    ve.wait_ge(s_tp.h, b_tp + u + 1)
                    ve.tensor_copy(ET[u % 2], pt[:]).then_inc(s_ev.h, 1)
                    ve.wait_ge(s_mm.h, b_mm + u * 3 + 3)
                    if u >= 2:
                        r_st.wait(ve, u - 2)
                    for oc in range(KC):
                        bk = (u % 2) * 2 + oc // 4
                        mmv = ve.scalar_tensor_tensor(
                            ob[u % 2][:, oc, :],
                            po[:, bk, (oc % 4) * 128:(oc % 4 + 1) * 128],
                            bvv[:, oc:oc + 1],
                            ow_[w % 2][:, oc, tc * 128:(tc + 1) * 128],
                            OP.add, OP.add)
                    mmv.then_inc(s_ev.h, 1)
                    u += 1
        s_ev.v += 64 * 5

        @blk.scalar
        def _(sc):
            sc.wait_ge(s_pre.h, b_pre + 32)
            u = 0
            for w in range(NW):
                for tc in range(4):
                    sc.wait_ge(s_ev.h, b_ev + u * 5 + 1)
                    if u >= 2:
                        sc.wait_ge(s_ev.h, b_ev + (u - 1) * 5 + 3)
                    sc.activation(E[u % 2], ps[:, 0:128], AF.Exp,
                                  bias=sm[:, 0:1],
                                  accum_out=sm[:, 1:2]).then_inc(s_ac.h, 1)
                    u += 1
        s_ac.v += 64

        @blk.gpsimd
        def _(gp):
            u = 0
            for w in range(NW):
                for tc in range(4):
                    gp.wait_ge(s_ev.h, b_ev + u * 5 + 5)
                    d = gp.dma_start(
                        out_v[:, :, w * WT + tc * 128:w * WT + (tc + 1) * 128],
                        ob[u % 2][:, :, :])
                    r_st.inc(d, u)
                    u += 1
            r_st.wait_all(gp)


def emit_l1_attn_a(P, x_src_view, x_old_dram, x_out_dram, w, tag):
    nc = P.nc
    xn = nc.dram_tensor(f"{tag}_xn", [D, T], F32).ap()
    qq = nc.dram_tensor(f"{tag}_q", [D, T], F32).ap()
    kk = nc.dram_tensor(f"{tag}_k", [D, T], F32).ap()
    vv = nc.dram_tensor(f"{tag}_v", [T, D], F32).ap()
    emit_ln(P, x_src_view, xn, name=f"{tag}_ln")
    emit_proj(P, xn, w["wq"], D, D, f"{tag}_pq", out_dram=qq,
              epi="f32r", bias=w["bq"])
    emit_proj(P, xn, w["wk"], D, D, f"{tag}_pk", out_dram=kk, epi="f32r")
    emit_proj_tm(P, xn, w["wv"], D, D, f"{tag}_pv", out_dram=vv)
    emit_l1_attn_a_attention(P, qq, kk, vv, w["bv"], x_old_dram, x_out_dram,
                             f"{tag}_at")



def emit_permute(P, src_dram, dst_dram, name, forward=True,
                 old_view=None, out_dram=None):
    nc = P.nc
    s_ev = P.sem("ev"); s_sf = P.sem("sf")
    b_ev = s_ev.v
    r_ld = Ring(P, "ldr", 2, step=16 * (2 if old_view is not None else 1))
    r_st = Ring(P, "str", 2)
    P.pass_begin(name)
    ti = [P.alloc(T) for _ in range(2)]
    to = [P.alloc(T) for _ in range(2)]
    told = [P.alloc(T) for _ in range(2)] if old_view is not None else None
    src_r = src_dram.rearrange("(c p) t -> c p t", p=128)
    dst_r = (dst_dram if out_dram is None else out_dram
             ).rearrange("(c p) t -> c p t", p=128)
    with nc.Block() as blk:
        @blk.sync
        def _(sy):
            for c in range(KC):
                if c >= 2:
                    sy.wait_ge(s_ev.h, b_ev + c - 1)
                d = sy.dma_start(ti[c % 2][:, :], src_r[c, :, :])
                r_ld.inc(d, c)
                if old_view is not None:
                    d = sy.dma_start(told[c % 2][:, :], old_view[c, :, :])
                    r_ld.inc(d, c)

        @blk.vector
        def _(ve):
            for c in range(KC):
                r_ld.wait(ve, c)
                if c >= 2:
                    r_st.wait(ve, c - 2)
                if forward:
                    iv = ti[c % 2].rearrange("q (b p) -> q b p", b=32
                                             ).transpose([0, 2, 1])
                else:
                    iv = ti[c % 2].rearrange("q (p b) -> q p b", p=256
                                             ).transpose([0, 2, 1])
                if old_view is None:
                    ve.tensor_copy(to[c % 2][:, :], iv).then_inc(s_ev.h, 1)
                else:
                    ve.tensor_copy(to[c % 2][:, :], iv).then_inc(s_sf.h, 1)
                    s_sf.v += 1
                    ve.wait_ge(s_sf.h, s_sf.v)
                    ve.tensor_add(to[c % 2][:, :], to[c % 2][:, :],
                                  told[c % 2][:, :]).then_inc(s_ev.h, 1)
        s_ev.v += KC

        @blk.gpsimd
        def _(gp):
            for c in range(KC):
                gp.wait_ge(s_ev.h, b_ev + c + 1)
                d = gp.dma_start(dst_r[c, :, :], to[c % 2][:, :])
                r_st.inc(d, c)
            r_st.wait_all(gp)


def emit_l0_attn_b_attention(P, q_dram, k_dram, v_dram, bv, ao_dram, name):
    nc = P.nc
    NH = 16
    q_v = fm_tiles(q_dram, D)
    k_v = fm_tiles(k_dram, D)
    v_t = v_dram.rearrange("(w g p) e -> w p g e", g=16, p=32)
    ao_v = q_dram
    ao_r = ao_dram.rearrange("(hc r) t -> r hc t", r=128)

    s_pre = P.sem("pre"); s_mm = P.sem("mm"); s_ev = P.sem("ev")
    s_ac = P.sem("ac"); s_sf = P.sem("sf")
    b_pre, b_mm, b_ev, b_ac = s_pre.v, s_mm.v, s_ev.v, s_ac.v
    r_qk = Ring(P, "ldr", 2, step=32)
    r_v = Ring(P, "qr", 1)
    r_st = Ring(P, "str", 2)

    P.pass_begin(name)
    qsl = [P.alloc3(KC, WT, F32R) for _ in range(2)]
    ksl = [P.alloc3(KC, WT, F32R) for _ in range(2)]
    vsl = P.alloc(16 * D, F32R, parts=32).rearrange(
        "p (a b) -> p a b", a=16)
    E = [P.alloc(WT, F32R, parts=32) for _ in range(2)]
    osb = [P.alloc(WT) for _ in range(2)]
    rden = P.alloc(WT, F32R, parts=1)
    bvv = P.alloc(KC)
    ones32 = P.alloc(1, F32R, parts=32)
    ones64 = P.alloc(64, F32R, parts=1)
    c_ones32 = P.const("ones32x1", np.ones((32, 1), np.float32))
    c_ones64 = P.const("ones1x64", np.ones((1, 64), np.float32))

    with (
        nc.psum_tensor(f"{name}_pst", [32, 2, WT], F32) as pst,
        nc.psum_tensor(f"{name}_pav", [64, 2, WT], F32) as pav,
        nc.psum_tensor(f"{name}_pdn", [1, 2, WT], F32) as pdn,
        nc.psum_tensor(f"{name}_pdb", [64, 2, WT], F32) as pdb,
        nc.Block() as blk,
    ):
        @blk.sync
        def _(sy):
            sy.dma_start(ones32, c_ones32.bitcast(F32R)).then_inc(s_pre.h, 16)
            sy.dma_start(ones64, c_ones64.bitcast(F32R)).then_inc(s_pre.h, 16)
            sy.dma_start(bvv, bv).then_inc(s_pre.h, 16)
            for w in range(NW):
                if w >= 2:
                    sy.wait_ge(s_mm.h, b_mm + (w - 1) * NH * 3)
                d = sy.dma_start(qsl[w % 2][:, :, :],
                                 q_v[:, :, w * WT:(w + 1) * WT].bitcast(F32R))
                r_qk.inc(d, w)
                d = sy.dma_start(ksl[w % 2][:, :, :],
                                 k_v[:, :, w * WT:(w + 1) * WT].bitcast(F32R))
                r_qk.inc(d, w)
                if w >= 1:
                    sy.wait_ge(s_mm.h, b_mm + (w * NH - 1) * 3 + 2)
                d = sy.dma_start(vsl[:, :, :], v_t[w, :, :, :].bitcast(F32R))
                r_v.inc(d, w)
        s_pre.v += 48

        @blk.tensor
        def _(te):
            te.wait_ge(s_pre.h, b_pre + 48)
            i = 0
            for w in range(NW):
                r_qk.wait(te, w)
                r_v.wait(te, w)
                for h in range(NH):
                    hb = (h % 2) * 64
                    hc = h // 2
                    if i >= 2:
                        te.wait_ge(s_ac.h, b_ac + i - 1)
                    for pp in range(16):
                        mm = te.matmul(
                            pst[:, i % 2, pp * 32:(pp + 1) * 32],
                            ksl[w % 2][hb:hb + 64, hc, pp * 32:(pp + 1) * 32],
                            qsl[w % 2][hb:hb + 64, hc, pp * 32:(pp + 1) * 32],
                            start=True, stop=True)
                    mm.then_inc(s_mm.h, 1)
                    te.wait_ge(s_ac.h, b_ac + i + 1)
                    if i >= 2:
                        te.wait_ge(s_ev.h, b_ev + (i - 2) * 2 + 2)
                    te.matmul(pdn[:, i % 2, :], ones32, E[i % 2],
                              start=True, stop=True)
                    for pp in range(16):
                        mm = te.matmul(
                            pav[:, i % 2, pp * 32:(pp + 1) * 32],
                            vsl[:, pp, h * 64:(h + 1) * 64],
                            E[i % 2][:, pp * 32:(pp + 1) * 32],
                            start=True, stop=True)
                    mm.then_inc(s_mm.h, 1)
                    te.wait_ge(s_ev.h, b_ev + i * 2 + 1)
                    mm = te.matmul(pdb[:, i % 2, :], ones64[0:1, :],
                                   rden[0:1, :], start=True, stop=True)
                    mm.then_inc(s_mm.h, 1)
                    i += 1
        s_mm.v += NW * NH * 3

        @blk.scalar
        def _(sc):
            sc.wait_ge(s_pre.h, b_pre + 48)
            i = 0
            for w in range(NW):
                for h in range(NH):
                    sc.wait_ge(s_mm.h, b_mm + i * 3 + 1)
                    if i >= 2:
                        sc.wait_ge(s_mm.h, b_mm + (i - 2) * 3 + 2)
                    sc.activation(E[i % 2], pst[:, i % 2, :], AF.Exp,
                                  scale=0.125).then_inc(s_ac.h, 1)
                    i += 1
        s_ac.v += NW * NH

        @blk.vector
        def _(ve):
            i = 0
            for w in range(NW):
                for h in range(NH):
                    hb = (h % 2) * 64
                    hc = h // 2
                    ve.wait_ge(s_mm.h, b_mm + i * 3 + 2)
                    ve.reciprocal(rden[0:1, :],
                                  pdn[0:1, i % 2, :]).then_inc(s_ev.h, 1)
                    ve.wait_ge(s_mm.h, b_mm + i * 3 + 3)
                    if i >= 2:
                        r_st.wait(ve, i - 2)
                    ve.tensor_copy(osb[i % 2][0:64, :],
                                   pav[:, i % 2, :]).then_inc(s_sf.h, 1)
                    s_sf.v += 1
                    ve.wait_ge(s_sf.h, s_sf.v)
                    ve.tensor_mul(osb[i % 2][0:64, :], osb[i % 2][0:64, :],
                                  pdb[:, i % 2, :]).then_inc(s_sf.h, 1)
                    s_sf.v += 1
                    ve.wait_ge(s_sf.h, s_sf.v)
                    ve.tensor_scalar_add(osb[i % 2][0:64, :],
                                         osb[i % 2][0:64, :],
                                         bvv[hb:hb + 64, hc:hc + 1]
                                         ).then_inc(s_ev.h, 1)
                    i += 1
        s_ev.v += NW * NH * 2

        @blk.gpsimd
        def _(gp):
            i = 0
            for w in range(NW):
                for h in range(NH):
                    gp.wait_ge(s_ev.h, b_ev + i * 2 + 2)
                    d = gp.dma_start(
                        ao_r[(h % 2) * 64:(h % 2) * 64 + 64, h // 2,
                             w * WT:(w + 1) * WT],
                        osb[i % 2][0:64, :])
                    r_st.inc(d, i)
                    i += 1
            r_st.wait_all(gp)


def emit_l0_attn_b(P, x_src_view, x_old_dram, x_out_dram, ws, tag):
    nc = P.nc
    xn0 = nc.dram_tensor(f"{tag}_xn0", [D, T], F32).ap()
    xb = [nc.dram_tensor(f"{tag}_xb{i}", [D, T], F32).ap() for i in range(2)]
    xr = nc.dram_tensor(f"{tag}_xr", [D, T], F32).ap()
    qq = nc.dram_tensor(f"{tag}_q", [D, T], F32).ap()
    kk = nc.dram_tensor(f"{tag}_k", [D, T], F32).ap()
    vv = nc.dram_tensor(f"{tag}_v", [T, D], F32).ap()
    ao = nc.dram_tensor(f"{tag}_ao", [D, T], F32).ap()
    ff = nc.dram_tensor(f"{tag}_ff", [4096, T], BF16).ap()
    emit_ln(P, x_src_view, xr, name=f"{tag}_ln0")
    emit_permute(P, xr, xb[0], f"{tag}_pm0", forward=True)
    cur = 0
    for l in range(4):
        w = ws[l]
        emit_proj(P, xb[cur], w["wqkv"][:, 0:1024], D, D, f"{tag}_q{l}",
                  out_dram=qq, epi="f32r", bias=w["bq"])
        emit_proj(P, xb[cur], w["wqkv"][:, 1024:2048], D, D, f"{tag}_k{l}",
                  out_dram=kk, epi="f32r")
        emit_proj_tm(P, xb[cur], w["wqkv"][:, 2048:3072], D, D, f"{tag}_v{l}",
                     out_dram=vv)
        emit_l0_attn_b_attention(P, qq, kk, vv, w["bv"], ao, f"{tag}_at{l}")
        emit_proj(P, ao, w["wo"], D, D, f"{tag}_o{l}", out_dram=xr,
                  epi="residual", bias=w["bo"], old_dram=xb[cur])
        emit_ln(P, fm_tiles(xr, D), xb[1 - cur], gamma=w["g1"], beta=w["be1"],
                name=f"{tag}_l1{l}")
        cur = 1 - cur
        emit_proj(P, xb[cur], w["w1"], D, 4096, f"{tag}_f1{l}", out_dram=ff,
                  epi="f32r", act=AF.Relu, bias=w["b1"], out_bf16=True)
        emit_bigmm(P, ff, w["w2"], 4096, D, f"{tag}_f2{l}", out_dram=xr,
                   epi="residual", bias=w["b2"], old_dram=xb[cur],
                   wp_windows=1)
        emit_ln(P, fm_tiles(xr, D), xb[1 - cur], gamma=w["g2"], beta=w["be2"],
                name=f"{tag}_l2{l}")
        cur = 1 - cur
    emit_permute(P, xb[cur], None, f"{tag}_pmb", forward=False,
                 old_view=fm_tiles(x_old_dram, D).rearrange("p c t -> c p t")
                 if False else x_old_dram.rearrange("(c p) t -> c p t", p=128),
                 out_dram=x_out_dram)



def _attn_a_consts():
    sel = np.zeros((128, 8 * 400), np.float32)
    for v in range(8):
        base = 16 * v
        o = v * 400
        for m in range(128):
            sel[base + m // 16, o + m] = 1.0
            sel[base + 8 + m // 16, o + 128 + m] = 1.0
            sel[base + m % 16, o + 256 + m] = 1.0
        for m in range(16):
            sel[base + m, o + 384 + m] = 1.0
    return sel


def emit_l0_attn_a_attention(P, q_dram, k_dram, v_dram, y_dram, name):
    nc = P.nc
    NB, NCH, NH = 4, 16, 16
    NU = NB * NCH * NH
    S2 = 1.0 / (4.0 * math.sqrt(2.0))
    q_v = fm_tiles(q_dram, 256)
    k_v = fm_tiles(k_dram, 256)
    v_t = v_dram.rearrange("(ch p) e -> ch p e", p=128)
    y_r = y_dram.rearrange("(hp r) t -> r hp t", r=128)

    s_pre = P.sem("pre"); s_mm = P.sem("mm"); s_ev = P.sem("ev")
    s_tp = P.sem("tp"); s_sf = P.sem("sf")
    b_pre, b_mm, b_ev, b_tp = s_pre.v, s_mm.v, s_ev.v, s_tp.v
    r_qk = Ring(P, "ldr", 2, step=16 * 4)
    r_vo = Ring(P, "qr", 2)
    r_st = Ring(P, "str", 2)

    P.pass_begin(name)
    psel = P.alloc(3200, F32R)
    qk = [P.alloc3(4, 128, F32R) for _ in range(2)]
    vo = [P.alloc(72, F32R) for _ in range(2)]
    c0 = [P.alloc(128, F32R) for _ in range(2)]
    c1 = [P.alloc(128, F32R) for _ in range(2)]
    k0 = [P.alloc(128, F32R) for _ in range(2)]
    k1 = [P.alloc(128, F32R) for _ in range(2)]
    c2q = [P.alloc(128, F32R, parts=17) for _ in range(2)]
    c2k = [P.alloc(128, F32R, parts=17) for _ in range(2)]
    rjq = P.alloc(128)
    rjk = P.alloc(128)
    k16 = [P.alloc(128) for _ in range(2)]
    ktm = P.alloc(16)
    kftm = [P.alloc(289, F32R) for _ in range(2)]
    ET = [P.alloc(128, F32R) for _ in range(2)]
    sA = P.alloc(NH * 130, F32R)
    s2A = P.alloc(NH * 65, F32R, parts=17)
    ysb = P.alloc(128, parts=64)
    zz = P.alloc(128, parts=1)
    zr = P.alloc(128, F32R, parts=1)
    ob = [P.alloc3(8, 128) for _ in range(2)]
    mask = P.alloc(128)
    ident = P.alloc(128)
    ones64 = P.alloc(64, F32R, parts=1)

    c_sel = P.const("attn_a_sel", _attn_a_consts())
    c_mask = P.const("triu128", np.triu(np.ones((128, 128), np.float32)))
    c_id = P.const("ident128", np.eye(128, dtype=np.float32))
    c_ones64 = P.const("ones1x64", np.ones((1, 64), np.float32))
    c_ones2 = P.const("ones128x2", np.ones((128, 2), np.float32))
    c_ones1 = P.const("ones1x1", np.ones((1, 1), np.float32))
    c_onesr17 = P.const("ones17r", np.ones((1, 128), np.float32))

    NPRE = 10

    with (
        nc.psum_tensor(f"{name}_psq", [128, 2, WT], F32) as psq,
        nc.psum_tensor(f"{name}_psk", [128, 2, WT], F32) as psk,
        nc.psum_tensor(f"{name}_ppt", [128, 128], F32) as ppt,
        nc.psum_tensor(f"{name}_pat", [128, 128], F32) as pat,
        nc.psum_tensor(f"{name}_pyz", [128, 256], F32) as pyz,
        nc.psum_tensor(f"{name}_psc", [128, 198], F32) as psc,
        nc.Block() as blk,
    ):
        npre = 0

        @blk.sync
        def _(sy):
            nonlocal_npre = 0
            for d_, s_ in [
                (psel, c_sel.bitcast(F32R)),
                (mask, c_mask), (ident, c_id),
                (ones64, c_ones64.bitcast(F32R)),
                (vo[0][:, 64:65], c_ones1.bitcast(F32R)[0:1, :].broadcast_to([128, 1])),
            ]:
                pass
            sy.dma_start(psel, c_sel.bitcast(F32R)).then_inc(s_pre.h, 16)
            sy.dma_start(mask, c_mask).then_inc(s_pre.h, 16)
            sy.dma_start(ident, c_id).then_inc(s_pre.h, 16)
            sy.dma_start(ones64, c_ones64.bitcast(F32R)).then_inc(s_pre.h, 16)
            for i in range(2):
                sy.dma_start(vo[i][:, 64:66],
                             c_ones2.bitcast(F32R)).then_inc(s_pre.h, 16)
                sy.dma_start(kftm[i][:, 272:273],
                             c_onesr17.bitcast(F32R).rearrange("a b -> b a")
                             ).then_inc(s_pre.h, 16)
                sy.dma_start(c2q[i][16:17, :],
                             c_onesr17.bitcast(F32R)).then_inc(s_pre.h, 16)
                sy.dma_start(c2k[i][16:17, :],
                             c_onesr17.bitcast(F32R)).then_inc(s_pre.h, 16)
            u = 0
            for G in range(NB * NCH):
                if G >= 2:
                    sy.wait_ge(s_mm.h, b_mm + (G - 1) * NH * 6)
                b, c = G // NCH, G % NCH
                t0 = b * 2048 + c * 128
                for j in range(2):
                    d = sy.dma_start(qk[G % 2][:, j, :],
                                     q_v[:, j, t0:t0 + 128].bitcast(F32R))
                    r_qk.inc(d, G)
                    d = sy.dma_start(qk[G % 2][:, 2 + j, :],
                                     k_v[:, j, t0:t0 + 128].bitcast(F32R))
                    r_qk.inc(d, G)
                for h in range(NH):
                    if u >= 2:
                        sy.wait_ge(s_mm.h, b_mm + (u - 1) * 6)
                    d = sy.dma_start(vo[u % 2][:, 0:64],
                                     v_t[G, :, h * 64:(h + 1) * 64].bitcast(F32R))
                    r_vo.inc(d, u)
                    u += 1
        s_pre.v += 16 * 12

        @blk.gpsimd
        def _(gp):
            gp.memset(k16[0], 0.0)
            gp.memset(k16[1], 0.0).then_inc(s_pre.h, 1)
        s_pre.v += 1

        @blk.tensor
        def _(te):
            te.wait_ge(s_pre.h, b_pre + 16 * 12 + 1)
            u = 0
            for G in range(NB * NCH):
                r_qk.wait(te, G)
                b, c = G // NCH, G % NCH
                for h in range(NH):
                    v = h % 8
                    qc = h // 8
                    so = v * 400
                    if u >= 2:
                        te.wait_ge(s_ev.h, b_ev + (u - 2) * 7 + 2)
                    bq = psq[:, u % 2, :]
                    te.matmul(bq[:, 0:128], psel[:, so:so + 128],
                              qk[G % 2][:, qc, :], start=True, stop=True)
                    te.matmul(bq[:, 128:256], psel[:, so + 128:so + 256],
                              qk[G % 2][:, qc, :], start=True, stop=True)
                    te.matmul(bq[:, 256:384], psel[:, so + 256:so + 384],
                              qk[G % 2][:, qc, :], start=True, stop=True)
                    mm = te.matmul(bq[0:16, 384:512], psel[:, so + 384:so + 400],
                                   qk[G % 2][:, qc, :], start=True, stop=True)
                    mm.then_inc(s_mm.h, 1)
                    bk = psk[:, u % 2, :]
                    te.matmul(bk[:, 0:128], psel[:, so:so + 128],
                              qk[G % 2][:, 2 + qc, :], start=True, stop=True)
                    te.matmul(bk[:, 128:256], psel[:, so + 128:so + 256],
                              qk[G % 2][:, 2 + qc, :], start=True, stop=True)
                    te.matmul(bk[:, 256:384], psel[:, so + 256:so + 384],
                              qk[G % 2][:, 2 + qc, :], start=True, stop=True)
                    mm = te.matmul(bk[0:16, 384:512], psel[:, so + 384:so + 400],
                                   qk[G % 2][:, 2 + qc, :], start=True, stop=True)
                    mm.then_inc(s_mm.h, 1)
                    te.wait_ge(s_ev.h, b_ev + u * 7 + 1)
                    if u >= 1:
                        te.wait_ge(s_ev.h, b_ev + (u - 1) * 7 + 3)
                    te.transpose(ppt[:], k16[u % 2], ident).then_inc(s_tp.h, 1)
                    te.wait_ge(s_ev.h, b_ev + u * 7 + 2)
                    if u >= 1:
                        te.wait_ge(s_ev.h, b_ev + (u - 1) * 7 + 4)
                    te.matmul(pat[:], k0[u % 2], c0[u % 2], start=True, stop=False)
                    te.matmul(pat[:], k1[u % 2], c1[u % 2], start=False, stop=False)
                    mm = te.matmul(pat[:], c2k[u % 2][:, :], c2q[u % 2][:, :],
                                   start=False, stop=True)
                    mm.then_inc(s_mm.h, 1)
                    r_vo.wait(te, u)
                    te.wait_ge(s_ev.h, b_ev + u * 7 + 4)
                    if u >= 1:
                        te.wait_ge(s_ev.h, b_ev + (u - 1) * 7 + 7)
                    if c >= 1:
                        te.wait_ge(s_ev.h, b_ev + (u - NH) * 7 + 5)
                    one_shot = (c == 0)
                    te.matmul(pyz[0:65, 0:128], vo[u % 2][:, 0:65], ET[u % 2],
                              start=True, stop=one_shot)
                    if not one_shot:
                        te.matmul(pyz[0:65, 0:128], sA[:, h * 130:h * 130 + 65],
                                  c0[u % 2], start=False, stop=False)
                        te.matmul(pyz[0:65, 0:128],
                                  sA[:, h * 130 + 65:h * 130 + 130],
                                  c1[u % 2], start=False, stop=False)
                        te.matmul(pyz[0:65, 0:128],
                                  s2A[:, h * 65:(h + 1) * 65],
                                  c2q[u % 2][:, :], start=False, stop=True)
                    nc.tensor.nop().then_inc(s_mm.h, 1)
                    te.wait_ge(s_ev.h, b_ev + u * 7 + 3)
                    if u >= 1:
                        te.wait_ge(s_ev.h, b_ev + (u - 1) * 7 + 5)
                    te.matmul(psc[:, 0:66], kftm[u % 2][:, 0:128],
                              vo[u % 2][:, 0:66], start=True, stop=True)
                    te.matmul(psc[:, 66:132], kftm[u % 2][:, 128:256],
                              vo[u % 2][:, 0:66], start=True, stop=True)
                    mm = te.matmul(psc[0:17, 132:198], kftm[u % 2][:, 256:273],
                                   vo[u % 2][:, 0:66], start=True, stop=True)
                    mm.then_inc(s_mm.h, 1)
                    te.wait_ge(s_ev.h, b_ev + u * 7 + 6)
                    mm = te.matmul(pyz[0:64, 128:256], ones64[0:1, :],
                                   zr[0:1, :], start=True, stop=True)
                    mm.then_inc(s_mm.h, 1)
                    u += 1
        s_mm.v += NU * 6
        s_tp.v += NU

        @blk.vector
        def _(ve):
            rj_done = [0, 0, 0, 0, 0]
            u = 0
            for G in range(NB * NCH):
                b, c = G // NCH, G % NCH
                for h in range(NH):
                    bq = psq[:, u % 2, :]
                    bk = psk[:, u % 2, :]
                    ve.wait_ge(s_mm.h, b_mm + u * 6 + 2)
                    if u >= 2:
                        ve.wait_ge(s_tp.h, b_tp + u - 1)
                    ve.tensor_copy(k16[u % 2][0:16, :],
                                   bk[0:16, 384:512]).then_inc(s_ev.h, 1)
                    ve.wait_ge(s_mm.h, b_mm + u * 6 + 1)
                    if u >= 1:
                        ve.wait_ge(s_sf.h, rj_done[0])
                    ve.tensor_copy(rjq, bq[:, 256:384]).then_inc(s_sf.h, 1)
                    s_sf.v += 1
                    ve.wait_ge(s_sf.h, s_sf.v)
                    ve.scalar_tensor_tensor(c0[u % 2], rjq, S2, bq[:, 0:128],
                                            OP.mult, OP.mult)
                    ve.scalar_tensor_tensor(c1[u % 2], rjq, S2, bq[:, 128:256],
                                            OP.mult, OP.mult).then_inc(s_sf.h, 1)
                    s_sf.v += 1
                    rj_done[0] = s_sf.v
                    ve.tensor_scalar_mul(c2q[u % 2][0:16, :], bq[0:16, 384:512],
                                         0.5)
                    if u >= 1:
                        ve.wait_ge(s_sf.h, rj_done[1])
                    ve.tensor_copy(rjk, bk[:, 256:384]).then_inc(s_sf.h, 1)
                    s_sf.v += 1
                    ve.wait_ge(s_sf.h, s_sf.v)
                    ve.scalar_tensor_tensor(k0[u % 2], rjk, S2, bk[:, 0:128],
                                            OP.mult, OP.mult)
                    ve.scalar_tensor_tensor(k1[u % 2], rjk, S2, bk[:, 128:256],
                                            OP.mult, OP.mult).then_inc(s_sf.h, 1)
                    s_sf.v += 1
                    rj_done[1] = s_sf.v
                    ve.tensor_scalar_mul(c2k[u % 2][0:16, :], bk[0:16, 384:512],
                                         0.5).then_inc(s_ev.h, 1)
                    ve.wait_ge(s_tp.h, b_tp + u + 1)
                    if u >= 1:
                        ve.wait_ge(s_sf.h, rj_done[2])
                    ve.tensor_copy(ktm, ppt[:, 0:16]).then_inc(s_sf.h, 1)
                    s_sf.v += 1
                    ve.wait_ge(s_sf.h, s_sf.v)
                    a16 = ktm.unsqueeze(2).broadcast_to([128, 16, 16])
                    b16 = ktm.unsqueeze(1).broadcast_to([128, 16, 16])
                    ve.scalar_tensor_tensor(
                        kftm[u % 2][:, 0:256].rearrange("p (i j) -> p i j", i=16),
                        a16, S2, b16, OP.mult, OP.mult)
                    ve.tensor_scalar_mul(kftm[u % 2][:, 256:272], ktm,
                                         0.5).then_inc(s_ev.h, 1)
                    nc.vector.engine_nop().then_inc(s_sf.h, 1)
                    s_sf.v += 1
                    rj_done[2] = s_sf.v
                    ve.wait_ge(s_mm.h, b_mm + u * 6 + 3)
                    ve.tensor_mul(ET[u % 2], pat[:], mask).then_inc(s_ev.h, 1)
                    ve.wait_ge(s_mm.h, b_mm + u * 6 + 5)
                    if c == 0:
                        ve.tensor_copy(sA[:, h * 130:h * 130 + 65], psc[:, 0:65])
                        ve.tensor_copy(sA[:, h * 130 + 65:h * 130 + 130],
                                       psc[:, 66:131])
                        mv = ve.tensor_copy(s2A[:, h * 65:(h + 1) * 65],
                                            psc[0:17, 132:197])
                    else:
                        ve.tensor_add(sA[:, h * 130:h * 130 + 65],
                                      sA[:, h * 130:h * 130 + 65], psc[:, 0:65])
                        ve.tensor_add(sA[:, h * 130 + 65:h * 130 + 130],
                                      sA[:, h * 130 + 65:h * 130 + 130],
                                      psc[:, 66:131])
                        mv = ve.tensor_add(s2A[:, h * 65:(h + 1) * 65],
                                           s2A[:, h * 65:(h + 1) * 65],
                                           psc[0:17, 132:197])
                    mv.then_inc(s_ev.h, 1)
                    ve.wait_ge(s_mm.h, b_mm + u * 6 + 4)
                    if u >= 1:
                        ve.wait_ge(s_sf.h, rj_done[3])
                    ve.tensor_scalar_add(zz[0:1, :], pyz[64:65, 0:128],
                                         1e-12).then_inc(s_sf.h, 1)
                    s_sf.v += 1
                    ve.wait_ge(s_sf.h, s_sf.v)
                    ve.reciprocal(zr[0:1, :], zz[0:1, :]).then_inc(s_sf.h, 1)
                    s_sf.v += 1
                    rj_done[3] = s_sf.v
                    nc.vector.engine_nop().then_inc(s_ev.h, 1)
                    ve.wait_ge(s_mm.h, b_mm + u * 6 + 6)
                    if h == 0 and u >= 2:
                        r_st.wait(ve, G - 2)
                    if u >= 1:
                        ve.wait_ge(s_sf.h, rj_done[4])
                    ve.tensor_copy(ysb, pyz[0:64, 0:128]).then_inc(s_sf.h, 1)
                    s_sf.v += 1
                    ve.wait_ge(s_sf.h, s_sf.v)
                    ve.tensor_mul(ob[G % 2][(h % 2) * 64:(h % 2) * 64 + 64,
                                            h // 2, :],
                                  ysb, pyz[0:64, 128:256]).then_inc(s_sf.h, 1)
                    s_sf.v += 1
                    rj_done[4] = s_sf.v
                    nc.vector.engine_nop().then_inc(s_ev.h, 1)
                    u += 1
        s_ev.v += NU * 7

        @blk.gpsimd
        def _(gp):
            for G in range(NB * NCH):
                b, c = G // NCH, G % NCH
                t0 = b * 2048 + c * 128
                gp.wait_ge(s_ev.h, b_ev + (G + 1) * NH * 7)
                d = gp.dma_start(y_r[:, :, t0:t0 + 128], ob[G % 2][:, :, :])
                r_st.inc(d, G)
            r_st.wait_all(gp)


def emit_l0_attn_a(P, x_src_view, x_old_dram, x_out_dram, w, tag):
    nc = P.nc
    xn = nc.dram_tensor(f"{tag}_xn", [D, T], F32).ap()
    qq = nc.dram_tensor(f"{tag}_q", [256, T], F32).ap()
    kk = nc.dram_tensor(f"{tag}_k", [256, T], F32).ap()
    vv = nc.dram_tensor(f"{tag}_v", [T, D], F32).ap()
    yy = nc.dram_tensor(f"{tag}_y", [D, T], F32).ap()
    emit_ln(P, x_src_view, xn, name=f"{tag}_ln")
    emit_proj(P, xn, w["wq"], D, 256, f"{tag}_pq", out_dram=qq, epi="f32r")
    emit_proj(P, xn, w["wk"], D, 256, f"{tag}_pk", out_dram=kk, epi="f32r")
    emit_proj_tm(P, xn, w["wv"], D, D, f"{tag}_pv", out_dram=vv)
    emit_l0_attn_a_attention(P, qq, kk, vv, yy, f"{tag}_at")
    emit_proj(P, yy, w["wo"], D, D, f"{tag}_po", out_dram=x_out_dram,
              epi="residual", old_dram=x_old_dram)



def emit_l1_attn_b_attention(P, q_dram, k_dram, v_dram, ao_dram, name):
    nc = P.nc
    NCK = 32
    q_v = fm_tiles(q_dram, D)
    k_v = fm_tiles(k_dram, D)
    v_t = v_dram.rearrange("(kc p) e -> p kc e", p=128)
    ao_v = fm_tiles(ao_dram, D)

    s_pre = P.sem("pre"); s_mm = P.sem("mm"); s_av = P.sem("av")
    s_ev = P.sem("ev"); s_ac = P.sem("ac"); s_sf = P.sem("sf")
    b_pre, b_mm, b_av, b_ev, b_ac = s_pre.v, s_mm.v, s_av.v, s_ev.v, s_ac.v
    r_kv = Ring(P, "ldr", 2, step=32)
    r_q = Ring(P, "qr", 2)
    r_st = Ring(P, "str", 2)
    MU = NCK + 1

    P.pass_begin(name)
    ksl = [P.alloc3(NCK, 128, F32R) for _ in range(2)]
    vsl = [P.alloc3(NCK, 128, F32R) for _ in range(2)]
    qt = [P.alloc(WT, F32R) for _ in range(2)]
    E = [P.alloc(WT, F32R) for _ in range(2)]
    osb = [P.alloc(WT) for _ in range(2)]
    rden_t = P.alloc(WT, F32R)
    rden = rden_t[0:1, :]
    ones_col = P.alloc(1, F32R)
    ones_row_t = P.alloc(128, F32R)
    ones_row = ones_row_t[0:1, :]
    c_ones_col = P.const("ones128x1", np.ones((128, 1), np.float32))
    c_ones_row = P.const("ones1x128", np.ones((1, 128), np.float32))

    with (
        nc.psum_tensor(f"{name}_pst", [128, 2, WT], F32) as pst,
        nc.psum_tensor(f"{name}_pav", [128, 2, WT], F32) as pav,
        nc.psum_tensor(f"{name}_pdn", [1, 2, WT], F32) as pdn,
        nc.psum_tensor(f"{name}_pdb", [128, 2, WT], F32) as pdb,
        nc.Block() as blk,
    ):
        @blk.sync
        def _(sy):
            sy.dma_start(ones_col, c_ones_col.bitcast(F32R)).then_inc(s_pre.h, 16)
            sy.dma_start(ones_row, c_ones_row.bitcast(F32R)).then_inc(s_pre.h, 16)
            u = 0
            for h in range(8):
                if h >= 2:
                    sy.wait_ge(s_mm.h, b_mm + (h - 1) * NW * MU)
                    sy.wait_ge(s_av.h, b_av + (h - 1) * NW * NCK)
                d = sy.dma_start(ksl[h % 2][:, :, :],
                                 k_v[:, h, :].rearrange("p (a b) -> p a b",
                                                        a=NCK).bitcast(F32R))
                r_kv.inc(d, h)
                d = sy.dma_start(vsl[h % 2][:, :, :],
                                 v_t[:, :, h * 128:(h + 1) * 128].bitcast(F32R))
                r_kv.inc(d, h)
                for w in range(NW):
                    if u >= 2:
                        sy.wait_ge(s_mm.h, b_mm + (u - 1) * MU)
                    d = sy.dma_start(qt[u % 2][:, :],
                                     q_v[:, h, w * WT:(w + 1) * WT].bitcast(F32R))
                    r_q.inc(d, u)
                    u += 1
        s_pre.v += 32

        @blk.tensor
        def _(te):
            te.wait_ge(s_pre.h, b_pre + 32)
            u = 0
            for h in range(8):
                r_kv.wait(te, h)
                for w in range(NW):
                    r_q.wait(te, u)
                    if u >= 2:
                        te.wait_ge(s_ev.h, b_ev + (u - 2) * 2 + 2)
                    for kc in range(NCK):
                        g = u * NCK + kc
                        if g >= 2:
                            te.wait_ge(s_ac.h, b_ac + g - 1)
                        te.matmul(pst[:, kc % 2, :], ksl[h % 2][:, kc, :],
                                  qt[u % 2][:, :],
                                  start=True, stop=True).then_inc(s_mm.h, 1)
                        te.wait_ge(s_ac.h, b_ac + g + 1)
                        te.matmul(pdn[:, u % 2, :], ones_col, E[kc % 2],
                                  start=(kc == 0), stop=(kc == NCK - 1))
                        te.matmul(pav[:, u % 2, :],
                                  vsl[h % 2][:, kc, :], E[kc % 2],
                                  start=(kc == 0),
                                  stop=(kc == NCK - 1)).then_inc(s_av.h, 1)
                    te.wait_ge(s_ev.h, b_ev + u * 2 + 1)
                    te.matmul(pdb[:, u % 2, :], ones_row, rden,
                              start=True, stop=True).then_inc(s_mm.h, 1)
                    u += 1
        s_mm.v += 8 * NW * MU
        s_av.v += 8 * NW * NCK

        @blk.scalar
        def _(sc):
            u = 0
            for h in range(8):
                for w in range(NW):
                    for kc in range(NCK):
                        g = u * NCK + kc
                        sc.wait_ge(s_mm.h, b_mm + u * MU + kc + 1)
                        if g >= 2:
                            sc.wait_ge(s_av.h, b_av + g - 1)
                        sc.activation(E[kc % 2], pst[:, kc % 2, :],
                                      AF.Exp).then_inc(s_ac.h, 1)
                    u += 1
        s_ac.v += 8 * NW * NCK

        @blk.vector
        def _(ve):
            u = 0
            for h in range(8):
                for w in range(NW):
                    ve.wait_ge(s_av.h, b_av + (u + 1) * NCK)
                    ve.reciprocal(rden,
                                  pdn[0:1, u % 2, :]).then_inc(s_ev.h, 1)
                    ve.wait_ge(s_mm.h, b_mm + (u + 1) * MU)
                    if u >= 2:
                        r_st.wait(ve, u - 2)
                    ve.tensor_copy(osb[u % 2],
                                   pav[:, u % 2, :]).then_inc(s_sf.h, 1)
                    s_sf.v += 1
                    ve.wait_ge(s_sf.h, s_sf.v)
                    ve.tensor_mul(osb[u % 2], osb[u % 2],
                                  pdb[:, u % 2, :]).then_inc(s_ev.h, 1)
                    u += 1
        s_ev.v += 8 * NW * 2

        @blk.gpsimd
        def _(gp):
            u = 0
            for h in range(8):
                for w in range(NW):
                    gp.wait_ge(s_ev.h, b_ev + u * 2 + 2)
                    d = gp.dma_start(ao_v[:, h, w * WT:(w + 1) * WT],
                                     osb[u % 2][:, :])
                    r_st.inc(d, u)
                    u += 1
            r_st.wait_all(gp)


def emit_l1_attn_b(P, x_src_view, x_old_dram, x_out_dram, ctx_fm, w, tag):
    nc = P.nc
    xn = nc.dram_tensor(f"{tag}_xn", [D, T], F32).ap()
    qq = nc.dram_tensor(f"{tag}_q", [D, T], F32).ap()
    kk = nc.dram_tensor(f"{tag}_k", [D, 4096], F32).ap()
    vv = nc.dram_tensor(f"{tag}_v", [4096, D], F32).ap()
    ao = nc.dram_tensor(f"{tag}_ao", [D, T], F32).ap()
    emit_ln(P, x_src_view, xn, name=f"{tag}_ln")
    emit_proj(P, xn, w["wq"], D, D, f"{tag}_pq", out_dram=qq, epi="f32r",
              scale_const=1.5 / math.sqrt(128.0))
    emit_proj(P, ctx_fm, w["wk"], 768, D, f"{tag}_pk", out_dram=kk,
              epi="f32r", t_total=4096)
    emit_proj_tm(P, ctx_fm, w["wv"], 768, D, f"{tag}_pv", out_dram=vv,
                 t_total=4096)
    emit_l1_attn_b_attention(P, qq, kk, vv, ao, f"{tag}_at")
    emit_proj(P, ao, w["wout"], D, D, f"{tag}_po", out_dram=x_out_dram,
              epi="residual", bias=w["bout"], old_dram=x_old_dram)



def emit_l0_ffn(P, x_src_view, x_old_dram, x_out_dram, wg_dq, wu_dq, wd_dq, tag):
    nc = P.nc
    xn = nc.dram_tensor(f"{tag}_xn", [D, T], F32).ap()
    hh = nc.dram_tensor(f"{tag}_h", [3072, T], BF16).ap()
    emit_ln(P, x_src_view, xn, name=f"{tag}_ln")
    for half in range(2):
        o0 = half * 1536
        emit_proj(P, xn, wg_dq[:, o0:o0 + 1536], D, 1536, f"{tag}_gu{half}",
                  out_dram=hh[o0:o0 + 1536, :], epi="gateup",
                  w2_dram=wu_dq[:, o0:o0 + 1536], out_bf16=True)
    emit_bigmm(P, hh, wd_dq, 3072, D, f"{tag}_dn", out_dram=x_out_dram,
               epi="residual", old_dram=x_old_dram, wp_windows=2)


def emit_l1_ffn(P, x_src_view, x_old_dram, x_out_dram, w, tag):
    nc = P.nc
    xn = nc.dram_tensor(f"{tag}_xn", [D, T], F32).ap()
    hh = nc.dram_tensor(f"{tag}_h", [4096, T], BF16).ap()
    g1 = nc.dram_tensor(f"{tag}_g1", [4096, T], BF16).ap()
    g2 = nc.dram_tensor(f"{tag}_g2", [4096, T], BF16).ap()
    emit_ln(P, x_src_view, xn, name=f"{tag}_ln")
    for half in range(2):
        o0 = half * 2048
        emit_proj(P, xn, w["w1"][:, o0:o0 + 2048], D, 2048, f"{tag}_gu{half}",
                  out_dram=hh[o0:o0 + 2048, :], epi="gateup",
                  scale=w["w1s"][:, half * 16:(half + 1) * 16],
                  w2_dram=w["w2"][:, o0:o0 + 2048],
                  scale2=w["w2s"][:, half * 16:(half + 1) * 16],
                  out_bf16=True)
    emit_bigmm(P, hh, w["h1"], 4096, 4096, f"{tag}_h1", out_dram=g1,
               act=AF.Gelu, scale=w["h1s"], wp_windows=2, out_bf16=True)
    emit_bigmm(P, g1, w["h2"], 4096, 4096, f"{tag}_h2", out_dram=g2,
               act=AF.Gelu, scale=w["h2s"], wp_windows=2, out_bf16=True)
    emit_bigmm(P, g2, w["w3"], 4096, D, f"{tag}_w3", out_dram=x_out_dram,
               epi="residual", scale=w["w3s"], old_dram=x_old_dram,
               wp_windows=1)



def build_l0_ffn_launch():
    P = Prog("l0ffn")
    nc = P.nc
    xin = nc.dram_tensor("xin", [D, T], F32, kind="ExternalInput").ap()
    wg_i = nc.dram_tensor("wg", [D, 3072], I32, kind="ExternalInput").ap()
    wu_i = nc.dram_tensor("wu", [D, 3072], I32, kind="ExternalInput").ap()
    wd_i = nc.dram_tensor("wd", [3072, D], I32, kind="ExternalInput").ap()
    xout = nc.dram_tensor("xout", [D, T], F32, kind="ExternalOutput").ap()
    wg = nc.dram_tensor("wg_dq", [D, 3072], F32).ap()
    wu = nc.dram_tensor("wu_dq", [D, 3072], F32).ap()
    wd = nc.dram_tensor("wd_dq", [3072, D], F32).ap()
    emit_dequant(P, wg_i, wg, D, 3072, "dqg")
    emit_dequant(P, wu_i, wu, D, 3072, "dqu")
    emit_dequant(P, wd_i, wd, 3072, D, "dqd")
    emit_l0_ffn(P, fm_tiles(xin, D), xin, xout, wg, wu, wd, "f0")
    P.pass_end()
    return P


def build_l1_ffn_launch():
    P = Prog("l1ffn")
    nc = P.nc
    xin = nc.dram_tensor("xin", [D, T], F32, kind="ExternalInput").ap()
    w = {}
    w["w1"] = nc.dram_tensor("w1", [D, 4096], F32, kind="ExternalInput").ap()
    w["w2"] = nc.dram_tensor("w2", [D, 4096], F32, kind="ExternalInput").ap()
    w["h1"] = nc.dram_tensor("h1", [4096, 4096], F32, kind="ExternalInput").ap()
    w["h2"] = nc.dram_tensor("h2", [4096, 4096], F32, kind="ExternalInput").ap()
    w["w3"] = nc.dram_tensor("w3", [4096, D], F32, kind="ExternalInput").ap()
    for k in ["w1s", "w2s", "h1s", "h2s"]:
        w[k] = nc.dram_tensor(k, [128, 32], F32, kind="ExternalInput").ap()
    w["w3s"] = nc.dram_tensor("w3s", [128, 8], F32, kind="ExternalInput").ap()
    xout = nc.dram_tensor("xout", [D, T], F32, kind="ExternalOutput").ap()
    emit_l1_ffn(P, fm_tiles(xin, D), xin, xout, w, "f1")
    P.pass_end()
    return P


def build_l1_attn_a_launch():
    P = Prog("l1attna")
    nc = P.nc
    xin = nc.dram_tensor("xin", [D, T], F32, kind="ExternalInput").ap()
    w = {}
    for k in ["wq", "wk", "wv"]:
        w[k] = nc.dram_tensor(k, [D, D], F32, kind="ExternalInput").ap()
    for k in ["bq", "bv"]:
        w[k] = nc.dram_tensor(k, [128, KC], F32, kind="ExternalInput").ap()
    xout = nc.dram_tensor("xout", [D, T], F32, kind="ExternalOutput").ap()
    emit_l1_attn_a(P, fm_tiles(xin, D), xin, xout, w, "a1")
    P.pass_end()
    return P


def build_l1_attn_b_launch():
    P = Prog("l1attnb")
    nc = P.nc
    xin = nc.dram_tensor("xin", [D, T], F32, kind="ExternalInput").ap()
    ctx = nc.dram_tensor("ctx", [768, 4096], F32, kind="ExternalInput").ap()
    w = {}
    w["wq"] = nc.dram_tensor("wq", [D, D], F32, kind="ExternalInput").ap()
    w["wk"] = nc.dram_tensor("wk", [768, D], F32, kind="ExternalInput").ap()
    w["wv"] = nc.dram_tensor("wv", [768, D], F32, kind="ExternalInput").ap()
    w["wout"] = nc.dram_tensor("wout", [D, D], F32, kind="ExternalInput").ap()
    w["bout"] = nc.dram_tensor("bout", [128, KC], F32, kind="ExternalInput").ap()
    xout = nc.dram_tensor("xout", [D, T], F32, kind="ExternalOutput").ap()
    emit_l1_attn_b(P, fm_tiles(xin, D), xin, xout, ctx, w, "b1")
    P.pass_end()
    return P


def build_l0_attn_a_launch():
    P = Prog("l0attna")
    nc = P.nc
    xin = nc.dram_tensor("xin", [D, T], F32, kind="ExternalInput").ap()
    w = {}
    w["wq"] = nc.dram_tensor("wq", [D, 256], F32, kind="ExternalInput").ap()
    w["wk"] = nc.dram_tensor("wk", [D, 256], F32, kind="ExternalInput").ap()
    w["wv"] = nc.dram_tensor("wv", [D, D], F32, kind="ExternalInput").ap()
    w["wo"] = nc.dram_tensor("wo", [D, D], F32, kind="ExternalInput").ap()
    xout = nc.dram_tensor("xout", [D, T], F32, kind="ExternalOutput").ap()
    emit_l0_attn_a(P, fm_tiles(xin, D), xin, xout, w, "a0")
    P.pass_end()
    return P


def build_l0_attn_b_launch():
    P = Prog("l0attnb")
    nc = P.nc
    xin = nc.dram_tensor("xin", [D, T], F32, kind="ExternalInput").ap()
    ws = []
    for l in range(4):
        w = {}
        w["wqkv"] = nc.dram_tensor(f"wqkv{l}", [D, 3072], F32,
                                   kind="ExternalInput").ap()
        w["wo"] = nc.dram_tensor(f"wo{l}", [D, D], F32, kind="ExternalInput").ap()
        w["w1"] = nc.dram_tensor(f"w1{l}", [D, 4096], F32,
                                 kind="ExternalInput").ap()
        w["w2"] = nc.dram_tensor(f"w2{l}", [4096, D], F32,
                                 kind="ExternalInput").ap()
        for k, n in [("bq", KC), ("bv", KC), ("bo", KC), ("b1", 32),
                     ("b2", KC), ("g1", KC), ("be1", KC), ("g2", KC),
                     ("be2", KC)]:
            w[k] = nc.dram_tensor(f"{k}{l}", [128, n], F32,
                                  kind="ExternalInput").ap()
        ws.append(w)
    xout = nc.dram_tensor("xout", [D, T], F32, kind="ExternalOutput").ap()
    emit_l0_attn_b(P, fm_tiles(xin, D), xin, xout, ws, "b0")
    P.pass_end()
    return P



def emit_repartition(P, xa_h, gg_h, xb_h, name="rp", no_cc=False, no_dyn=False):
    nc = P.nc
    s_cc = P.sem("cc"); s_st = P.sem("st")
    P.pass_begin(name)
    b_st = s_st.v

    def piece_dmas(eng, pieces):
        with eng.register("rpoff") as reg:
            first = True
            prev = 0
            for (r, bp) in pieces:
                stat = r * 1024 * T + bp * 2048
                if first:
                    eng.reg_alu(reg, eng.partition_id(), 256, OP.mult)
                    if stat:
                        eng.reg_alu(reg, reg, stat, OP.add)
                    first = False
                else:
                    eng.reg_alu(reg, reg, stat - prev, OP.add)
                prev = stat
                srcap = bass.AP(gg_h, reg, [[T, 1024], [1, 256]])
                dstap = bass.AP(xb_h, (r * 4 + bp) * 256, [[T, 1024], [1, 256]])
                eng.dma_start(dstap, srcap).then_inc(s_st.h, 16)

    pieces = [(r, bp) for r in range(8) for bp in range(4)]
    with nc.Block() as blk:
        @blk.gpsimd
        def _(gp):
            gp.collective_compute("AllGather", OP.bypass, [list(range(8))],
                                  ins=[xa_h.ap().opt()],
                                  outs=[gg_h.ap().opt()]).then_inc(s_cc.h)
            s_cc.v += 1
            gp.wait_ge(s_cc.h, s_cc.v)
            piece_dmas(gp, pieces[0:11])

        @blk.sync
        def _(sy):
            sy.wait_ge(s_cc.h, s_cc.v)
            piece_dmas(sy, pieces[11:22])

        @blk.scalar
        def _(sc):
            sc.wait_ge(s_cc.h, s_cc.v)
            piece_dmas(sc, pieces[22:32])

        @blk.gpsimd
        def _(gp):
            gp.wait_ge(s_st.h, b_st + 16 * 32)
    s_st.v += 16 * 32


def build_fused(stop_after=None):
    P = Prog("mk")
    nc = P.nc
    xin = nc.dram_tensor("xin", [D, T], F32, kind="ExternalInput").ap()
    xout = nc.dram_tensor("xout", [D, T], F32, kind="ExternalOutput").ap()

    wa0 = {k: nc.dram_tensor(f"a0{k}", s, F32, kind="ExternalInput").ap()
           for k, s in [("wq", [D, 256]), ("wk", [D, 256]),
                        ("wv", [D, D]), ("wo", [D, D])]}
    f0i = {k: nc.dram_tensor(f"f0{k}", s, I32, kind="ExternalInput").ap()
           for k, s in [("wg", [D, 3072]), ("wu", [D, 3072]),
                        ("wd", [3072, D])]}
    ws_b0 = []
    for l in range(4):
        w = {}
        w["wqkv"] = nc.dram_tensor(f"b0wqkv{l}", [D, 3072], F32,
                                   kind="ExternalInput").ap()
        w["wo"] = nc.dram_tensor(f"b0wo{l}", [D, D], F32,
                                 kind="ExternalInput").ap()
        w["w1"] = nc.dram_tensor(f"b0w1{l}", [D, 4096], F32,
                                 kind="ExternalInput").ap()
        w["w2"] = nc.dram_tensor(f"b0w2{l}", [8, 128, 4096], F32,
                                 kind="ExternalInput").ap()
        for k, n in [("bq", KC), ("bv", KC), ("bo", KC), ("b1", 32),
                     ("b2", KC), ("g1", KC), ("be1", KC), ("g2", KC),
                     ("be2", KC)]:
            w[k] = nc.dram_tensor(f"b0{k}{l}", [128, n], F32,
                                  kind="ExternalInput").ap()
        ws_b0.append(w)
    wa1 = {}
    for k in ["wq", "wk", "wv"]:
        wa1[k] = nc.dram_tensor(f"a1{k}", [D, D], F32,
                                kind="ExternalInput").ap()
    for k in ["bq", "bv"]:
        wa1[k] = nc.dram_tensor(f"a1{k}", [128, KC], F32,
                                kind="ExternalInput").ap()
    wf1 = {}
    for k, s in [("w1", [D, 4096]), ("w2", [D, 4096]),
                 ("h1", [32, 128, 4096]), ("h2", [32, 128, 4096]),
                 ("w3", [8, 128, 4096])]:
        wf1[k] = nc.dram_tensor(f"f1{k}", s, F32, kind="ExternalInput").ap()
    for k, n in [("w1s", 32), ("w2s", 32), ("h1s", 32), ("h2s", 32),
                 ("w3s", KC)]:
        wf1[k] = nc.dram_tensor(f"f1{k}", [128, n], F32,
                                kind="ExternalInput").ap()
    wb1 = {}
    for k, s in [("wq", [D, D]), ("wk", [768, D]), ("wv", [768, D]),
                 ("wout", [D, D])]:
        wb1[k] = nc.dram_tensor(f"b1{k}", s, F32, kind="ExternalInput").ap()
    wb1["bout"] = nc.dram_tensor("b1bout", [128, KC], F32,
                                 kind="ExternalInput").ap()
    ctx = nc.dram_tensor("ctx", [768, 4096], F32, kind="ExternalInput").ap()

    xa_h = nc.dram_tensor("xa", [D, T], F32)
    gg_h = nc.dram_tensor("gg", [8 * D, T], F32)
    xb_h = nc.dram_tensor("xb", [D, T], F32)
    xb = xb_h.ap()
    s_ = [nc.dram_tensor(f"strm{i}", [D, T], F32).ap() for i in range(6)]
    wg = nc.dram_tensor("f0wg_dq", [D, 3072], F32).ap()
    wu = nc.dram_tensor("f0wu_dq", [D, 3072], F32).ap()
    wd = nc.dram_tensor("f0wd_dq", [8, 128, 3072], F32).ap()

    def _fin(cur):
        sx = P.sem("fx")
        P.pass_begin("fin")
        with nc.Block() as blk:
            @blk.gpsimd
            def _(gp):
                gp.dma_start(xout, cur).then_inc(sx.h, 16)
                sx.v += 16
                gp.wait_ge(sx.h, sx.v)
        P.pass_end()
        return P

    emit_dequant(P, f0i["wg"], wg, D, 3072, "dqg")
    emit_dequant(P, f0i["wu"], wu, D, 3072, "dqu")
    emit_dequant(P, f0i["wd"], wd, 3072, D, "dqd", out_tiled=True)
    emit_l0_attn_a(P, fm_tiles(xin, D), xin, xa_h.ap(), wa0, "a0")
    if stop_after == "a0":
        return _fin(xa_h.ap())
    emit_repartition(P, xa_h, gg_h, xb_h,
                     no_cc=(stop_after == "rp_nocc"),
                     no_dyn=(stop_after == "rp_nodyn"))
    if stop_after in ("rp", "rp_nocc", "rp_nodyn"):
        return _fin(xb)
    emit_l0_ffn(P, fm_tiles(xb, D), xb, s_[0], wg, wu, wd, "f0a")
    if stop_after == "f0a":
        return _fin(s_[0])
    emit_l0_attn_b(P, fm_tiles(s_[0], D), s_[0], s_[1], ws_b0, "b0")
    if stop_after == "b0":
        return _fin(s_[1])
    emit_l0_ffn(P, fm_tiles(s_[1], D), s_[1], s_[2], wg, wu, wd, "f0b")
    if stop_after == "f0b":
        return _fin(s_[2])
    emit_l1_attn_a(P, fm_tiles(s_[2], D), s_[2], s_[3], wa1, "a1")
    if stop_after == "a1":
        return _fin(s_[3])
    emit_l1_ffn(P, fm_tiles(s_[3], D), s_[3], s_[4], wf1, "f1a")
    if stop_after == "f1a":
        return _fin(s_[4])
    emit_l1_attn_b(P, fm_tiles(s_[4], D), s_[4], s_[5], ctx, wb1, "b1")
    if stop_after == "b1":
        return _fin(s_[5])
    emit_l1_ffn(P, fm_tiles(s_[5], D), s_[5], xout, wf1, "f1b")
    P.pass_end()
    return P


def kernel(**z):
    x = np.asarray(z["L0_attn_a_x"], np.float32)
    P = build_fused()
    im = {}
    im["a0wq"] = np.asarray(z["L0_attn_a_Wq"], np.float32)
    im["a0wk"] = np.asarray(z["L0_attn_a_Wk"], np.float32)
    im["a0wv"] = np.asarray(z["L0_attn_a_Wv"], np.float32)
    im["a0wo"] = np.asarray(z["L0_attn_a_Wo"], np.float32)
    im["f0wg"] = _t(z["L0_ffn_w_gate"])
    im["f0wu"] = _t(z["L0_ffn_w_up"])
    im["f0wd"] = _t(z["L0_ffn_w_down"])
    for l in range(4):
        im[f"b0wqkv{l}"] = _t(z["L0_attn_b_Wqkv"][l])
        im[f"b0wo{l}"] = _t(z["L0_attn_b_Wo"][l])
        im[f"b0w1{l}"] = _t(z["L0_attn_b_W1"][l])
        im[f"b0w2{l}"] = wtile(_t(z["L0_attn_b_W2"][l]))
        im[f"b0bq{l}"] = vec2(z["L0_attn_b_bqkv"][l][0:1024])
        im[f"b0bv{l}"] = vec2(z["L0_attn_b_bqkv"][l][2048:3072])
        im[f"b0bo{l}"] = vec2(z["L0_attn_b_bo"][l])
        im[f"b0b1{l}"] = vec2(z["L0_attn_b_b1"][l])
        im[f"b0b2{l}"] = vec2(z["L0_attn_b_b2"][l])
        im[f"b0g1{l}"] = vec2(z["L0_attn_b_g1"][l])
        im[f"b0be1{l}"] = vec2(z["L0_attn_b_be1"][l])
        im[f"b0g2{l}"] = vec2(z["L0_attn_b_g2"][l])
        im[f"b0be2{l}"] = vec2(z["L0_attn_b_be2"][l])
    im["a1wq"] = _t(z["L1_attn_a_Wq"])
    im["a1wk"] = _t(z["L1_attn_a_Wk"])
    im["a1wv"] = _t(z["L1_attn_a_Wv"])
    im["a1bq"] = vec2(z["L1_attn_a_bq"])
    im["a1bv"] = vec2(z["L1_attn_a_bv"])
    im["f1w1"] = _t(z["L1_ffn_w1_w"])
    im["f1w2"] = _t(z["L1_ffn_w2_w"])
    im["f1h1"] = wtile(_t(z["L1_ffn_h1_w"]))
    im["f1h2"] = wtile(_t(z["L1_ffn_h2_w"]))
    im["f1w3"] = wtile(_t(z["L1_ffn_w3_w"]))
    im["f1w1s"] = vec2(z["L1_ffn_w1_s"])
    im["f1w2s"] = vec2(z["L1_ffn_w2_s"])
    im["f1h1s"] = vec2(z["L1_ffn_h1_s"])
    im["f1h2s"] = vec2(z["L1_ffn_h2_s"])
    im["f1w3s"] = vec2(z["L1_ffn_w3_s"])
    im["b1wq"] = np.asarray(z["L1_attn_b_Wq"], np.float32)
    im["b1wk"] = np.asarray(z["L1_attn_b_Wk"], np.float32)
    im["b1wv"] = np.asarray(z["L1_attn_b_Wv"], np.float32)
    im["b1wout"] = np.asarray(z["L1_attn_b_Wout"], np.float32)
    im["b1bout"] = vec2(z["L1_attn_b_bout"])
    im["ctx"] = _t(np.asarray(z["L1_attn_b_context"], np.float32
                              ).reshape(4096, 768))

    in_maps = []
    for c in range(8):
        xa_c = np.ascontiguousarray(x[4 * c:4 * c + 4].reshape(8192, 1024).T)
        in_maps.append(dict(im, xin=xa_c))
    res = run_bass_kernel_spmd(P.nc, in_maps, core_ids=list(range(8)))

    out = np.empty((32, 2048, 1024), np.float32)
    for c in range(8):
        out[:, 256 * c:256 * (c + 1), :] = (
            res.results[c]["xout"].reshape(1024, 32, 256).transpose(1, 2, 0))
    return out



def _run(P, in_maps):
    return run_bass_kernel_spmd(P.nc, in_maps, core_ids=list(range(8)))


def kernel_unfused(**z):
    x = np.asarray(z["L0_attn_a_x"], np.float32)

    xa = [np.ascontiguousarray(x[4 * c:4 * c + 4].reshape(8192, 1024).T)
          for c in range(8)]

    P1 = build_l0_attn_a_launch()
    r = _run(P1, [{"xin": xa[c], "wq": z["L0_attn_a_Wq"],
                   "wk": z["L0_attn_a_Wk"], "wv": z["L0_attn_a_Wv"],
                   "wo": z["L0_attn_a_Wo"]} for c in range(8)])
    xa = [r.results[c]["xout"] for c in range(8)]

    G = np.concatenate([a.reshape(1024, 4, 2048) for a in xa], axis=1)
    xb = [np.ascontiguousarray(G[:, :, 256 * c:256 * (c + 1)].reshape(1024, 8192))
          for c in range(8)]
    del G, xa

    P2 = build_l0_ffn_launch()
    ffn0 = {"wg": _t(z["L0_ffn_w_gate"]), "wu": _t(z["L0_ffn_w_up"]),
            "wd": _t(z["L0_ffn_w_down"])}
    r = _run(P2, [dict(ffn0, xin=xb[c]) for c in range(8)])
    xb = [r.results[c]["xout"] for c in range(8)]

    P3 = build_l0_attn_b_launch()
    ab = {}
    for l in range(4):
        ab[f"wqkv{l}"] = _t(z["L0_attn_b_Wqkv"][l])
        ab[f"wo{l}"] = _t(z["L0_attn_b_Wo"][l])
        ab[f"w1{l}"] = _t(z["L0_attn_b_W1"][l])
        ab[f"w2{l}"] = _t(z["L0_attn_b_W2"][l])
        ab[f"bq{l}"] = vec2(z["L0_attn_b_bqkv"][l][0:1024])
        ab[f"bv{l}"] = vec2(z["L0_attn_b_bqkv"][l][2048:3072])
        ab[f"bo{l}"] = vec2(z["L0_attn_b_bo"][l])
        ab[f"b1{l}"] = vec2(z["L0_attn_b_b1"][l])
        ab[f"b2{l}"] = vec2(z["L0_attn_b_b2"][l])
        ab[f"g1{l}"] = vec2(z["L0_attn_b_g1"][l])
        ab[f"be1{l}"] = vec2(z["L0_attn_b_be1"][l])
        ab[f"g2{l}"] = vec2(z["L0_attn_b_g2"][l])
        ab[f"be2{l}"] = vec2(z["L0_attn_b_be2"][l])
    r = _run(P3, [dict(ab, xin=xb[c]) for c in range(8)])
    xb = [r.results[c]["xout"] for c in range(8)]

    r = _run(P2, [dict(ffn0, xin=xb[c]) for c in range(8)])
    xb = [r.results[c]["xout"] for c in range(8)]

    P5 = build_l1_attn_a_launch()
    a1 = {"wq": _t(z["L1_attn_a_Wq"]), "wk": _t(z["L1_attn_a_Wk"]),
          "wv": _t(z["L1_attn_a_Wv"]), "bq": vec2(z["L1_attn_a_bq"]),
          "bv": vec2(z["L1_attn_a_bv"])}
    r = _run(P5, [dict(a1, xin=xb[c]) for c in range(8)])
    xb = [r.results[c]["xout"] for c in range(8)]

    P6 = build_l1_ffn_launch()
    f1 = {"w1": _t(z["L1_ffn_w1_w"]), "w2": _t(z["L1_ffn_w2_w"]),
          "h1": _t(z["L1_ffn_h1_w"]), "h2": _t(z["L1_ffn_h2_w"]),
          "w3": _t(z["L1_ffn_w3_w"]),
          "w1s": vec2(z["L1_ffn_w1_s"]), "w2s": vec2(z["L1_ffn_w2_s"]),
          "h1s": vec2(z["L1_ffn_h1_s"]), "h2s": vec2(z["L1_ffn_h2_s"]),
          "w3s": vec2(z["L1_ffn_w3_s"])}
    r = _run(P6, [dict(f1, xin=xb[c]) for c in range(8)])
    xb = [r.results[c]["xout"] for c in range(8)]

    P7 = build_l1_attn_b_launch()
    ctx_fm = _t(np.asarray(z["L1_attn_b_context"], np.float32
                           ).reshape(4096, 768))
    b1_ = {"ctx": ctx_fm, "wq": z["L1_attn_b_Wq"], "wk": z["L1_attn_b_Wk"],
           "wv": z["L1_attn_b_Wv"], "wout": z["L1_attn_b_Wout"],
           "bout": vec2(z["L1_attn_b_bout"])}
    r = _run(P7, [dict(b1_, xin=xb[c]) for c in range(8)])
    xb = [r.results[c]["xout"] for c in range(8)]

    r = _run(P6, [dict(f1, xin=xb[c]) for c in range(8)])
    xb = [r.results[c]["xout"] for c in range(8)]

    out = np.empty((32, 2048, 1024), np.float32)
    for c in range(8):
        out[:, 256 * c:256 * (c + 1), :] = (
            xb[c].reshape(1024, 32, 256).transpose(1, 2, 0))
    return out
